# version 10
# baseline (speedup 1.0000x reference)
import numpy as np
import ml_dtypes
from contextlib import ExitStack

import concourse.bass as bass
import concourse.mybir as mybir
from concourse.bass_utils import run_bass_kernel_spmd

F32 = mybir.dt.float32
BF16 = mybir.dt.bfloat16
AF = mybir.ActivationFunctionType
ALU = mybir.AluOpType
AX = mybir.AxisListType

S = 4096
D = 1024
HID = 4096
NP = 8
DP = 1152
SHIFT = 20.0
EPS = 1e-5


def _qk_perm():
    p = []
    for h in range(16):
        b = 64 * h
        p += [b + 2 * i for i in range(32)]
        p += [b + 2 * i + 1 for i in range(32)]
    return np.array(p, np.int64)


def _rope_tables():
    theta = 10000.0 ** (-2.0 * (np.arange(32, dtype=np.float64) / 64.0))
    freqs = np.arange(S, dtype=np.float64)[:, None] * theta[None, :]
    cos = np.cos(freqs).astype(np.float32)
    sin = np.sin(freqs).astype(np.float32)
    cos128 = np.ascontiguousarray(cos.T[np.arange(128) % 32])
    sin128 = np.ascontiguousarray(sin.T[np.arange(128) % 32])
    sign = np.where((np.arange(128) % 64) < 32, -1.0, 1.0).astype(np.float32)
    return cos128, np.ascontiguousarray(sin128 * sign[:, None])


def _alpha_beta(wstage, stage):
    i = np.arange(HID)
    s = 1 << stage
    half = (i >> stage) & 1
    blk = i >> (stage + 1)
    j = i & (s - 1)
    k = blk * s + j
    return (wstage[k, half, half].astype(np.float32),
            wstage[k, half, 1 - half].astype(np.float32))


def _scat_nat(alpha, beta, stage):
    s = 1 << stage
    mats = np.zeros((32, 128, 128), np.float32)
    v = np.arange(128)
    for u in range(32):
        idx = u * 128 + v
        mats[u, v, v] = alpha[idx]
        mats[u, v, v ^ s] = beta[idx]
    return mats


def _scat_grp(alpha, beta, stage):
    s2 = 1 << (stage - 5)
    mats = np.zeros((32, 128, 128), np.float32)
    p = np.arange(128)
    for g in range(32):
        idx = (p << 5) | g
        mats[g, p, p] = alpha[idx]
        mats[g, p, p ^ s2] = beta[idx]
    return mats


def _pack(tiles_per_block):
    cols = []
    for blk in tiles_per_block:
        cols.extend(blk)
    return np.ascontiguousarray(np.concatenate(cols, axis=1))


def _host_prepare(inputs):
    x = np.ascontiguousarray(np.asarray(inputs["L0_attn_x"], np.float32))
    Wqkv = np.asarray(inputs["L0_attn_W_qkv"], np.float32)
    Wo = np.asarray(inputs["L0_attn_W_o"], np.float32)
    Wexp = np.asarray(inputs["L0_ffn_W_exp"], np.float32)
    b_exp = np.asarray(inputs["L0_ffn_b_exp"], np.float32)
    up_w = np.asarray(inputs["L0_ffn_up_weights"], np.float32)
    up_bias = np.asarray(inputs["L0_ffn_up_bias"], np.float32)
    dn_w = np.asarray(inputs["L0_ffn_down_weights"], np.float32)
    Wcon = np.asarray(inputs["L0_ffn_W_con"], np.float32)
    b_con = np.asarray(inputs["L0_ffn_b_con"], np.float32)
    dn_bias = np.asarray(inputs["L0_ffn_down_bias"], np.float32)

    pqk = _qk_perm()
    perm = np.concatenate([pqk, 1024 + pqk, 2048 + np.arange(1024)])
    cos128, sinS = _rope_tables()

    wexpTb = np.zeros((HID, DP), np.float32)
    wexpTb[:, :D] = Wexp.T
    wexpTb[:, D] = b_exp

    i = np.arange(HID)
    g2pos = (i & 31) * 128 + ((i >> 5) & 127)
    wconG = np.empty_like(Wcon)
    wconG[g2pos] = Wcon

    ab_up = [_alpha_beta(up_w[j], j) for j in range(12)]
    ab_dn = [_alpha_beta(dn_w[j], j) for j in range(12)]
    upA_nat = [_scat_nat(*ab_up[j], j) for j in range(5)]
    upB_grp = [_scat_grp(*ab_up[j], j) for j in range(5, 12)]
    dnA_nat = [_scat_nat(*ab_dn[j], j) for j in range(5)]
    dnB_grp = [_scat_grp(*ab_dn[j], j) for j in range(5, 12)]

    upA = _pack([[upA_nat[4][u].T, upA_nat[3][u], upA_nat[2][u],
                  upA_nat[1][u], upA_nat[0][u]] for u in range(32)])
    upB = _pack([[upB_grp[6][g].T] + [upB_grp[j][g] for j in (5, 4, 3, 2, 1, 0)]
                 for g in range(32)])
    dnB = _pack([[dnB_grp[0][g]] + [dnB_grp[j][g].T for j in (1, 2, 3, 4, 5, 6)]
                 for g in range(32)])
    dnA = _pack([[dnA_nat[0][u]] + [dnA_nat[j][u].T for j in (1, 2, 3, 4)]
                 for u in range(32)])

    masks = np.zeros((128, 4 * 1024), np.float32)
    pp = np.arange(128)[:, None]
    jj = np.arange(512)[None, :]
    for m in range(4):
        mk = (pp + 128 * m <= jj)
        masks[:, m * 1024:m * 1024 + 512] = mk
        masks[:, m * 1024 + 512:(m + 1) * 1024] = mk

    consts = np.zeros((128, 8), np.float32)
    consts[:, 0] = -SHIFT
    consts[:, 1] = EPS
    consts[:, 2] = 1.0 / D
    consts[:, 3] = -1.0 / D
    consts[:, 4] = -0.5
    consts[:, 5] = -1.0

    shared = {
        "wqkvT": np.ascontiguousarray(Wqkv[perm].T),
        "woT": np.ascontiguousarray(Wo.T),
        "wexpTb": wexpTb.astype(ml_dtypes.bfloat16),
        "wconG": np.ascontiguousarray(wconG).astype(ml_dtypes.bfloat16),
        "cosT": cos128, "sinT": sinS,
        "identF": np.eye(128, dtype=np.float32),
        "identB": np.eye(128).astype(ml_dtypes.bfloat16),
        "p32": np.eye(128)[np.arange(128) ^ 32].astype(ml_dtypes.bfloat16),
        "maskq": masks.astype(ml_dtypes.bfloat16),
        "upA": upA.astype(ml_dtypes.bfloat16), "upB": upB.astype(ml_dtypes.bfloat16),
        "dnB": dnB.astype(ml_dtypes.bfloat16), "dnA": dnA.astype(ml_dtypes.bfloat16),
        "upbias": np.ascontiguousarray(up_bias.reshape(128, 32)),
        "bcon": np.ascontiguousarray(np.tile(b_con, (128, 1))),
        "dnb": np.ascontiguousarray(np.tile(dn_bias, (128, 1))),
        "consts": consts,
    }
    return x, shared



class _Sem:
    def __init__(self, h):
        self.h = h
        self.n = 0

    def inc(self, k=1):
        self.n += k
        return self.n


ENGS = ("sync", "tensor", "vector", "scalar", "gpsimd")


class _Plan:
    def __init__(self):
        self.ops = {e: [] for e in ENGS}
        self.dirty = {e: False for e in ENGS}
        self.lastw = {}
        self.bar_sem = None
        self.bar_target = 0

    def emit(self, eng, fn):
        self.ops[eng].append(fn)
        self.dirty[eng] = True

    def wait(self, eng, sem, val):
        v = int(val)
        key = (eng, id(sem))
        if v <= 0 or self.lastw.get(key, 0) >= v:
            return
        self.lastw[key] = v
        h = sem.h
        self.ops[eng].append(lambda e: e.wait_ge(h, v))

    def barrier(self):
        for e in ENGS:
            for sem in self.all_sems:
                self.wait(e, sem, sem.n)
            self.dirty[e] = False


def build_nc(phase_limit=99, debug=(), a_groups=8, a_parts=31):
    nc = bass.Bass("TRN2", target_bir_lowering=False, detect_race_conditions=False)
    P = _Plan()

    x_d = nc.dram_tensor("x", [S, D], F32, kind="ExternalInput")
    wqkvT_d = nc.dram_tensor("wqkvT", [D, 3072], F32, kind="ExternalInput")
    woT_d = nc.dram_tensor("woT", [D, D], F32, kind="ExternalInput")
    wexpTb_d = nc.dram_tensor("wexpTb", [HID, DP], BF16, kind="ExternalInput")
    wconG_d = nc.dram_tensor("wconG", [HID, D], BF16, kind="ExternalInput")
    cos_d = nc.dram_tensor("cosT", [128, S], F32, kind="ExternalInput")
    sin_d = nc.dram_tensor("sinT", [128, S], F32, kind="ExternalInput")
    identF_d = nc.dram_tensor("identF", [128, 128], F32, kind="ExternalInput")
    identB_d = nc.dram_tensor("identB", [128, 128], BF16, kind="ExternalInput")
    p32_d = nc.dram_tensor("p32", [128, 128], BF16, kind="ExternalInput")
    maskq_d = nc.dram_tensor("maskq", [128, 4096], BF16, kind="ExternalInput")
    upA_d = nc.dram_tensor("upA", [128, 5 * 32 * 128], BF16, kind="ExternalInput")
    upB_d = nc.dram_tensor("upB", [128, 7 * 32 * 128], BF16, kind="ExternalInput")
    dnB_d = nc.dram_tensor("dnB", [128, 7 * 32 * 128], BF16, kind="ExternalInput")
    dnA_d = nc.dram_tensor("dnA", [128, 5 * 32 * 128], BF16, kind="ExternalInput")
    upbias_d = nc.dram_tensor("upbias", [128, 32], F32, kind="ExternalInput")
    bcon_d = nc.dram_tensor("bcon", [128, D], F32, kind="ExternalInput")
    dnb_d = nc.dram_tensor("dnb", [128, D], F32, kind="ExternalInput")
    consts_d = nc.dram_tensor("consts", [128, 8], F32, kind="ExternalInput")

    out_d = nc.dram_tensor("out", [S, D], F32, kind="ExternalOutput")
    q_dram = nc.dram_tensor("q_sc", [NP, 128, S], BF16, kind="Internal")
    v_dram = nc.dram_tensor("v_sc", [NP, 128, 4160], BF16, kind="Internal")
    x1_dram = nc.dram_tensor("x1_sc", [S, D], F32, kind="Internal")
    ln2T_dram = nc.dram_tensor("ln2T_sc", [D, S], BF16, kind="Internal")
    w1_dram = nc.dram_tensor("w1_sc", [HID, DP], BF16, kind="Internal")
    w1f_dram = nc.dram_tensor("w1f_sc", [DP, HID], BF16, kind="Internal")
    r_dram = nc.dram_tensor("r_sc", [HID, D], BF16, kind="Internal")
    r2_dram = nc.dram_tensor("r2_sc", [HID, D], BF16, kind="Internal")
    bias_dram = nc.dram_tensor("bias_sc", [HID], BF16, kind="Internal")

    dbg_d = {}
    for name, shape, dt in (("dbg_ln", [128, 4 * D], F32),
                            ("dbg_lnt", [128, 8 * 512], BF16),
                            ("dbg_k", [128, NP * S], BF16),
                            ("dbg_o", [128, NP * S], BF16),
                            ("dbg_x1", [S, D], F32),
                            ("dbg_l2t", [D, S], BF16),
                            ("dbg_w1f", [128, 9 * HID], BF16),
                            ("dbg_r", [128, 32 * D], BF16),
                            ("dbg_bh", [128, 32], F32)):
        if name in debug:
            dbg_d[name] = nc.dram_tensor(name, shape, dt, kind="ExternalOutput")

    st = ExitStack()

    def SEM(name):
        return _Sem(st.enter_context(nc.semaphore(name)))

    sPE = SEM("sPE")
    sACT = SEM("sACT")
    sDVE = SEM("sDVE")
    sPOOL = SEM("sPOOL")
    P.bar_sem = SEM("sBAR")
    chC = SEM("chC"); chCos = SEM("chCos"); chWq = SEM("chWq"); chWo = SEM("chWo")
    chX = SEM("chX"); chX1 = SEM("chX1"); chL2 = SEM("chL2"); chBnc = SEM("chBnc")
    chBY = SEM("chBY"); chDbg = SEM("chDbg")
    chWF = [SEM(f"chWF{i}") for i in range(4)]
    chRL = [SEM(f"chRL{i}") for i in range(4)]
    chQsp = [SEM(f"chQsp{i}") for i in range(2)]
    chVsp = [SEM(f"chVsp{i}") for i in range(2)]
    chV = [SEM(f"chV{i}") for i in range(2)]
    chQ = [SEM(f"chQ{i}") for i in range(2)]
    chScat = [SEM(f"chScat{i}") for i in range(2)]
    chWEc = [SEM(f"chWEc{i}") for i in range(2)]
    chL2D = [SEM(f"chL2D{i}") for i in range(2)]
    chX1D = [SEM(f"chX1D{i}") for i in range(2)]
    chOut = [SEM(f"chOut{i}") for i in range(2)]
    chStg = [SEM(f"chStg{i}") for i in range(2)]
    chT3 = SEM("chT3")
    chIn = [SEM(f"chIn{i}") for i in range(2)]
    P.all_sems = [sPE, sACT, sDVE, sPOOL, chC, chCos, chWq, chWo, chX, chX1,
                  chL2, chBnc, chBY, chDbg] + chWF + chRL + chQsp + chVsp + chV + chQ + \
                 chScat + chWEc + chL2D + chX1D + chOut + chStg + chIn + [chT3]

    def dma(eng, sem, out_ap, in_ap, **kw):
        v = sem.inc(16)
        h = sem.h
        P.emit(eng, lambda e: e.dma_start(out_ap, in_ap, **kw).then_inc(h, 16))
        return v

    def op(eng, sem, build):
        v = sem.inc()
        h = sem.h
        P.emit(eng, lambda e: build(e).then_inc(h, 1))
        return v

    def opn(eng, build):
        P.emit(eng, build)

    sConst = st.enter_context(nc.sbuf_tensor("sConst", [128, 8], F32, side="left"))
    sIdF = st.enter_context(nc.sbuf_tensor("sIdF", [128, 128], F32, side="left"))
    sIdB = st.enter_context(nc.sbuf_tensor("sIdB", [128, 128], BF16, side="left"))
    sP32 = st.enter_context(nc.sbuf_tensor("sP32", [128, 128], BF16, side="left"))
    sOne = st.enter_context(nc.sbuf_tensor("sOne", [1, 128], BF16, side="left"))
    sBiasH = st.enter_context(nc.sbuf_tensor("sBiasH", [128, 32], F32, side="left"))
    sUpb = st.enter_context(nc.sbuf_tensor("sUpb", [128, 32], F32, side="left"))

    dma("sync", chC, sConst[:], consts_d.ap())
    dma("sync", chC, sIdF[:], identF_d.ap())
    dma("sync", chC, sIdB[:], identB_d.ap())
    dma("sync", chC, sP32[:], p32_d.ap())
    const_ld = dma("sync", chC, sUpb[:], upbias_d.ap())
    for e in ENGS:
        P.wait(e, chC, const_ld)
    v_ones = op("gpsimd", sPOOL, lambda e: e.memset(sOne[:], 1.0))

    CSHIFT = sConst[:, 0:1]
    CEPS = sConst[:, 1:2]
    CINVD = sConst[:, 2:3]
    CNINVD = sConst[:, 3:4]
    CNEG1 = sConst[:, 5:6]

    ln_last = {}

    def emit_ln_wide(xin_ap, out_aps, ssq_ap, scr, W, prev=0):
        sm, smsq, svar, srstd = scr
        key = id(scr)
        P.wait("vector", sDVE, max(prev, ln_last.get(key, 0)))
        seg_in = bass.AP(xin_ap.tensor, xin_ap.offset,
                         [list(xin_ap.ap[0]), [D, W], [1, D]])
        seg_sq = bass.AP(ssq_ap.tensor, ssq_ap.offset,
                         [list(ssq_ap.ap[0]), [D, W], [1, D]])
        v_a = op("vector", sDVE, lambda e: e.reduce_sum(
            bass.AP(sm, 0, [[sm.shape[1], 128], [1, W], [1, 1]]), seg_in, axis=AX.X))
        v_b = op("vector", sDVE, lambda e: e.tensor_tensor(ssq_ap, xin_ap, xin_ap, ALU.mult))
        P.wait("vector", sDVE, v_b)
        v_c = op("vector", sDVE, lambda e: e.reduce_sum(
            bass.AP(smsq, 0, [[smsq.shape[1], 128], [1, W], [1, 1]]), seg_sq, axis=AX.X))
        v_d = op("vector", sDVE, lambda e: e.tensor_scalar(sm[:, 0:W], sm[:, 0:W],
                                                           CNINVD, None, ALU.mult))
        P.wait("vector", sDVE, v_d)
        v_e = op("vector", sDVE, lambda e: e.tensor_tensor(svar[:, 0:W], sm[:, 0:W],
                                                           sm[:, 0:W], ALU.mult))
        v_f = op("vector", sDVE, lambda e: e.tensor_scalar(smsq[:, 0:W], smsq[:, 0:W],
                                                           CINVD, CEPS, ALU.mult, ALU.add))
        P.wait("vector", sDVE, v_f)
        v_g = op("vector", sDVE, lambda e: e.tensor_tensor(svar[:, 0:W], smsq[:, 0:W],
                                                           svar[:, 0:W], ALU.subtract))
        P.wait("scalar", sDVE, v_g)
        v_ln = op("scalar", sACT, lambda e: e.activation(srstd[:, 0:W], svar[:, 0:W], AF.Ln))
        P.wait("scalar", sACT, v_ln)
        v_rs = op("scalar", sACT, lambda e: e.activation(srstd[:, 0:W], srstd[:, 0:W],
                                                         AF.Exp, scale=-0.5))
        P.wait("vector", sACT, v_rs)
        outs = []
        for a in range(W):
            xin_a = bass.AP(xin_ap.tensor, xin_ap.offset + a * D,
                            [list(xin_ap.ap[0]), [1, D]])
            outs.append(op("vector", sDVE,
                           lambda e, o=out_aps[a], xin_a=xin_a, a=a:
                           e.tensor_scalar(o, xin_a, sm[:, a:a + 1],
                                           srstd[:, a:a + 1], ALU.add, ALU.mult)))
        ln_last[key] = outs[-1]
        return outs

    stK = ExitStack()
    sK = stK.enter_context(nc.sbuf_tensor("sK", [128, NP * S], BF16, side="left"))

    if phase_limit >= 1:
        stA = ExitStack()
        A = lambda n, s, d: stA.enter_context(nc.sbuf_tensor(n, s, d, side="left"))
        sWq = A("sWq", [128, 8 * 3072], BF16)
        sCos = A("sCos", [128, S], F32)
        sSin = A("sSin", [128, S], F32)
        sX = A("sX", [128, 4 * D], F32)
        sLN = A("sLN", [128, 4 * D], F32)
        sLnT = A("sLnT", [128, 8 * 512], BF16)
        sKraw = [A(f"sKraw{i}", [128, 512], BF16) for i in range(2)]
        sTa = A("sTa", [128, 512], BF16)
        sQsp = [A(f"sQsp{i}", [128, 512], BF16) for i in range(2)]
        ssqA = A("ssqA", [128, 1024], F32)
        sVraw = [A(f"sVraw{i}", [128, 512], BF16) for i in range(2)]
        sVt = [A(f"sVt{i}", [128, 520], BF16) for i in range(2)]
        scrA = tuple(A(f"scrA{i}", [128, 8], F32) for i in range(4))
        stPA = ExitStack()
        pTr = [stPA.enter_context(nc.psum_tensor(f"pTr{i}", [128, 512], F32)) for i in range(2)]
        pQk = [stPA.enter_context(nc.psum_tensor(f"pQk{i}", [128, 512], F32)) for i in range(2)]
        pSw = stPA.enter_context(nc.psum_tensor("pSw", [128, 512], F32))
        pVt = [stPA.enter_context(nc.psum_tensor(f"pVt{i}", [128, 512], BF16)) for i in range(2)]

        wq_r = wqkvT_d.ap().rearrange("(c p) n -> p c n", p=128)
        sWq_r = sWq.ap().rearrange("p (c n) -> p c n", c=8)
        v_wq_dc = []
        for dc in range(8):
            v_wq_dc.append(dma("gpsimd", chWq, sWq_r[:, dc:dc + 1, :],
                               wq_r[:, dc:dc + 1, :]))
        for i in range(2):
            t = sVt[i]
            opn("gpsimd", lambda e, t=t: e.memset(bass.AP(t, 64, [[520, 128], [130, 4]]), 1.0))
            if i == 0:
                opn("gpsimd", lambda e, t=t: e.memset(bass.AP(t, 129, [[520, 128], [130, 4]]), 1.0))
        v_vones = op("gpsimd", sPOOL, lambda e, t=sVt[1]:
                     e.memset(bass.AP(t, 129, [[520, 128], [130, 4]]), 1.0))

        x_r = x_d.ap().rearrange("(g a p) d -> g p a d", a=4, p=128)
        st_ = dict(tr_free={}, qkv_free={}, sw_free=0, vt_free={}, kraw_free={},
                   qsp_free={}, vraw_free={}, vtile_free={}, x_free=0,
                   lnT_used=0, ln_tr_done=0, v_x={}, lnA_last=0)

        def xload(tg, queue):
            P.wait(queue, sDVE, st_["x_free"])
            st_["v_x"][tg] = dma(queue, chX,
                                 sX.ap().rearrange("p (a d) -> p a d", a=4), x_r[tg])

        def emit_ln_A(tg):
            sm, smsq, svar, srstd = scrA
            P.wait("vector", chX, st_["v_x"][tg])
            P.wait("vector", sDVE, st_["lnA_last"])
            v_a = op("vector", sDVE, lambda e: e.reduce_sum(
                bass.AP(sm, 0, [[sm.shape[1], 128], [1, 4], [1, 1]]),
                bass.AP(sX, 0, [[4 * D, 128], [D, 4], [1, D]]), axis=AX.X))
            prev = v_a
            for a in range(4):
                P.wait("vector", sDVE, prev)
                v_m = op("vector", sDVE, lambda e, a=a:
                         e.tensor_tensor(ssqA[:], sX[:, a * D:(a + 1) * D],
                                         sX[:, a * D:(a + 1) * D], ALU.mult))
                P.wait("vector", sDVE, v_m)
                prev = op("vector", sDVE, lambda e, a=a:
                          e.reduce_sum(smsq[:, a:a + 1], ssqA[:], axis=AX.X))
            P.wait("vector", sDVE, prev)
            v_d = op("vector", sDVE, lambda e: e.tensor_scalar(sm[:, 0:4], sm[:, 0:4],
                                                               CNINVD, None, ALU.mult))
            P.wait("vector", sDVE, v_d)
            v_e = op("vector", sDVE, lambda e: e.tensor_tensor(svar[:, 0:4], sm[:, 0:4],
                                                               sm[:, 0:4], ALU.mult))
            v_f = op("vector", sDVE, lambda e: e.tensor_scalar(smsq[:, 0:4], smsq[:, 0:4],
                                                               CINVD, CEPS, ALU.mult, ALU.add))
            P.wait("vector", sDVE, v_f)
            v_g = op("vector", sDVE, lambda e: e.tensor_tensor(svar[:, 0:4], smsq[:, 0:4],
                                                               svar[:, 0:4], ALU.subtract))
            P.wait("scalar", sDVE, v_g)
            v_ln = op("scalar", sACT, lambda e: e.activation(srstd[:, 0:4], svar[:, 0:4], AF.Ln))
            P.wait("scalar", sACT, v_ln)
            v_rs = op("scalar", sACT, lambda e: e.activation(srstd[:, 0:4], srstd[:, 0:4],
                                                             AF.Exp, scale=-0.5))
            P.wait("vector", sACT, v_rs)
            P.wait("vector", sPE, st_["ln_tr_done"])
            outs = []
            for a in range(4):
                outs.append(op("vector", sDVE,
                               lambda e, a=a:
                               e.tensor_scalar(sLN[:, a * D:(a + 1) * D],
                                               sX[:, a * D:(a + 1) * D],
                                               sm[:, a:a + 1], srstd[:, a:a + 1],
                                               ALU.add, ALU.mult)))
            st_["lnA_last"] = outs[-1]
            return outs
        pend_rope = []

        def flush_rope():
            while pend_rope:
                kind, pair, kb, t0, v_ev = pend_rope.pop(0)
                P.wait("tensor", sACT, v_ev)
                P.wait("tensor", sDVE, st_["sw_free"])
                v_sw = op("tensor", sPE, lambda e, kb=kb:
                          e.matmul(pSw[:], sP32[:], sKraw[kb][:], start=True, stop=True))
                P.wait("vector", sACT, v_ev)
                P.wait("vector", sPE, v_sw)
                opn("vector", lambda e, kb=kb, t0=t0:
                    e.tensor_tensor(sTa[:], sKraw[kb][:], sCos[:, t0:t0 + 512], ALU.mult))
                if kind == "q":
                    qb = pair % 2
                    P.wait("vector", chQsp[qb], st_["qsp_free"].get(qb, 0))
                    v_tb = op("vector", sDVE, lambda e, qb=qb, t0=t0:
                              e.tensor_tensor(sQsp[qb][:], pSw[:], sSin[:, t0:t0 + 512], ALU.mult))
                    P.wait("vector", sDVE, v_tb)
                    v_rp = op("vector", sDVE, lambda e, qb=qb:
                              e.tensor_tensor(sQsp[qb][:], sTa[:], sQsp[qb][:], ALU.add))
                    P.wait("sync", sDVE, v_rp)
                    st_["qsp_free"][qb] = dma("sync", chQsp[qb],
                                              q_dram.ap()[pair, :, t0:t0 + 512],
                                              sQsp[qb][:])
                else:
                    kcol = pair * S + t0
                    v_tb = op("vector", sDVE, lambda e, kcol=kcol, t0=t0:
                              e.tensor_tensor(sK[:, kcol:kcol + 512], pSw[:],
                                              sSin[:, t0:t0 + 512], ALU.mult))
                    P.wait("vector", sDVE, v_tb)
                    v_rp = op("vector", sDVE, lambda e, kcol=kcol:
                              e.tensor_tensor(sK[:, kcol:kcol + 512], sTa[:],
                                              sK[:, kcol:kcol + 512], ALU.add))
                st_["sw_free"] = v_rp
                st_["kraw_free"][kb] = v_rp
        xload(0, "sync")
        dma("sync", chCos, sCos[:], cos_d.ap())
        v_cos = dma("sync", chCos, sSin[:], sin_d.ap())
        if a_parts & 1:
            ln_done = emit_ln_A(0)
            st_["x_free"] = ln_done[-1]
        else:
            ln_done = []
        xload(1, "gpsimd")
        for tg in range(a_groups):
            t0 = tg * 512
            P.wait("tensor", chCos, v_cos)
            for dc in (range(8) if a_parts & 16 else []):
                bank = pTr[dc % 2]
                P.wait("tensor", sACT, st_["tr_free"].get(dc % 2, 0))
                for a in range(4):
                    P.wait("tensor", sDVE, ln_done[a] if a < len(ln_done) else 0)
                    bld = (lambda e, b=bank, a=a, dc=dc:
                           e.transpose(b[:, a * 128:(a + 1) * 128],
                                       sLN[:, a * D + dc * 128: a * D + (dc + 1) * 128],
                                       sIdF[:]))
                    if a < 3:
                        opn("tensor", bld)
                    else:
                        v_tr = op("tensor", sPE, bld)
                P.wait("scalar", sPE, v_tr)
                P.wait("scalar", sPE, st_["lnT_used"])
                st_["tr_free"][dc % 2] = op("scalar", sACT,
                                            lambda e, b=bank, dc=dc:
                                            e.copy(sLnT[:, dc * 512:(dc + 1) * 512], b[:]))
            st_["ln_tr_done"] = sPE.n
            tr_all = (st_["tr_free"].get(0, 0), st_["tr_free"].get(1, 0))
            if tg + 1 < a_groups and (a_parts & 1):
                ln_next = emit_ln_A(tg + 1)
                st_["x_free"] = ln_next[-1]
                if tg + 2 < a_groups:
                    xload(tg + 2, "gpsimd")
            else:
                ln_next = []
            for c in ((list(range(16, 24)) + list(range(16))) if a_parts & 2 else []):
                bank = pQk[c % 2]
                eng_w, val_w = st_["qkv_free"].get(c % 2, ("ACT", 0))
                P.wait("tensor", sACT if eng_w == "ACT" else sDVE, val_w)
                P.wait("tensor", sACT, max(tr_all))
                for dc in range(8):
                    P.wait("tensor", chWq, v_wq_dc[dc])
                    bld = (lambda e, b=bank, c=c, dc=dc:
                           e.matmul(b[:], sWq[:, dc * 3072 + c * 128: dc * 3072 + (c + 1) * 128],
                                    sLnT[:, dc * 512:(dc + 1) * 512],
                                    start=(dc == 0), stop=(dc == 7)))
                    if dc < 7:
                        opn("tensor", bld)
                    else:
                        v_mm = op("tensor", sPE, bld)
                flush_rope()
                if c < 16 and not (a_parts & 4):
                    st_["qkv_free"][c % 2] = ("ACT", 0)
                elif c < 16:
                    pair = c % 8
                    kb = c % 2
                    P.wait("scalar", sPE, v_mm)
                    P.wait("scalar", sDVE, st_["kraw_free"].get(kb, 0))
                    v_ev = op("scalar", sACT, lambda e, b=bank, kb=kb:
                              e.copy(sKraw[kb][:], b[:]))
                    st_["qkv_free"][c % 2] = ("ACT", v_ev)
                    pend_rope.append(("q" if c < 8 else "k", pair, kb, t0, v_ev))
                elif not (a_parts & 8):
                    st_["qkv_free"][c % 2] = ("ACT", 0)
                else:
                    pair = c - 16
                    vb = c % 2
                    P.wait("scalar", sPE, v_mm)
                    P.wait("scalar", sPE, st_["vraw_free"].get(vb, 0))
                    v_ev = op("scalar", sACT, lambda e, b=bank, vb=vb:
                              e.copy(sVraw[vb][:], b[:]))
                    st_["qkv_free"][c % 2] = ("ACT", v_ev)
                    vtb = pVt[c % 2]
                    P.wait("tensor", sACT, v_ev)
                    P.wait("tensor", sACT, st_["vt_free"].get(c % 2, 0))
                    for i4 in range(4):
                        bld = (lambda e, b=vtb, vb=vb, i4=i4:
                               e.transpose(b[:, i4 * 128:(i4 + 1) * 128],
                                           sVraw[vb][:, i4 * 128:(i4 + 1) * 128],
                                           sIdB[:]))
                        if i4 < 3:
                            opn("tensor", bld)
                        else:
                            v_vt = op("tensor", sPE, bld)
                    st_["vraw_free"][vb] = v_vt
                    P.wait("scalar", sPE, v_vt)
                    P.wait("scalar", chVsp[vb], st_["vtile_free"].get(vb, 0))
                    P.wait("scalar", sPOOL, v_vones)
                    opn("scalar", lambda e, b=vtb, vb=vb:
                        e.copy(bass.AP(sVt[vb], 0, [[520, 128], [130, 4], [1, 64]]),
                               bass.AP(b, 0, [[512, 128], [128, 4], [1, 64]])))
                    v_ev2 = op("scalar", sACT, lambda e, b=vtb, vb=vb:
                               e.copy(bass.AP(sVt[vb], 65, [[520, 128], [130, 4], [1, 64]]),
                                      bass.AP(b, 64, [[512, 128], [128, 4], [1, 64]])))
                    st_["vt_free"][c % 2] = v_ev2
                    P.wait("sync", sACT, v_ev2)
                    st_["vtile_free"][vb] = dma("sync", chVsp[vb],
                                                v_dram.ap()[pair, :, tg * 520:(tg + 1) * 520],
                                                sVt[vb][:])
            st_["lnT_used"] = sPE.n
            flush_rope()
            ln_done = ln_next

        if "dbg_k" in dbg_d:
            P.wait("sync", sDVE, sDVE.n)
            dma("sync", chDbg, dbg_d["dbg_k"].ap(), sK[:])
        if "dbg_ln" in dbg_d:
            P.wait("sync", sDVE, sDVE.n)
            dma("sync", chDbg, dbg_d["dbg_ln"].ap(), sLN[:])
        if "dbg_lnt" in dbg_d:
            P.wait("sync", sACT, sACT.n)
            dma("sync", chDbg, dbg_d["dbg_lnt"].ap(), sLnT[:])
        P.barrier()
        stPA.close()
        stA.close()

    stO = ExitStack()
    sO = stO.enter_context(nc.sbuf_tensor("sO", [128, NP * S], BF16, side="right"))

    if phase_limit >= 2:
        stB = ExitStack()
        B_ = lambda n, s, d: stB.enter_context(nc.sbuf_tensor(n, s, d, side="left"))
        sV = [B_(f"sV{i}", [128, 4160], BF16) for i in range(2)]
        sQt = [B_(f"sQt{i}", [128, 512], BF16) for i in range(2)]
        sPP = [B_(f"sPP{i}", [128, 1024], BF16) for i in range(2)]
        sOev0 = B_("sOev0", [64, 512], BF16)
        sOev1 = B_("sOev1", [64, 512], BF16)
        sRecB0 = B_("sRecB0", [1, 512], BF16)
        sRecB1 = B_("sRecB1", [1, 512], BF16)
        sMask = B_("sMask", [128, 4096], BF16)
        v_mask = dma("sync", chC, sMask[:], maskq_d.ap())
        P.wait("gpsimd", chC, v_mask)
        stPB = ExitStack()
        pSc = [stPB.enter_context(nc.psum_tensor(f"pSc{i}", [128, 1024], F32)) for i in range(2)]
        pO = [stPB.enter_context(nc.psum_tensor(f"pO{i}", [65, 512], F32)) for i in range(4)]

        def _recip(e, out_ap, in_ap):
            with nc.allow_low_precision(reason="softmax 1/l to bf16 operand"):
                return e.reciprocal(out_ap, in_ap)

        bst = dict(q_free={}, v_free={}, pp_free={}, sc_free={},
                   o_clean={0: 0, 1: 0}, last_nrm=0, last_bc=0)
        v_vdma = {0: dma("sync", chV[0], sV[0][:], v_dram.ap()[0])}
        v_qdma = {(0, 0): dma("sync", chQ[0], sQt[0][:], q_dram.ap()[0, :, 0:512])}
        tails = {}

        def emit_tail_head(i, opar, pr, q0, v_last):
            P.wait("scalar", sPE, v_last)
            P.wait("scalar", sDVE, bst["last_nrm"])
            opn("scalar", lambda e, opar=opar: e.copy(sOev0[:], pO[2 * opar][0:64, :]))
            v_oc = op("scalar", sACT,
                      lambda e, opar=opar: e.copy(sOev1[:], pO[2 * opar + 1][0:64, :]))
            P.wait("vector", sPE, v_last)
            P.wait("vector", sPE, bst["last_bc"])
            opn("vector", lambda e, opar=opar: _recip(e, sRecB0[:], pO[2 * opar][64:65, :]))
            v_rec = op("vector", sDVE,
                       lambda e, opar=opar: _recip(e, sRecB1[:], pO[2 * opar + 1][64:65, :]))
            tails[i] = (opar, pr, q0, v_oc, v_rec)

        def emit_tail_body(i):
            opar, pr, q0, v_oc, v_rec = tails.pop(i)
            P.wait("tensor", sDVE, v_rec)
            P.wait("tensor", sACT, v_oc)
            opn("tensor", lambda e, opar=opar:
                e.matmul(pO[2 * opar][0:64, :], sOne[:, 0:64], sRecB0[:],
                         start=True, stop=True))
            v_bc = op("tensor", sPE, lambda e, opar=opar:
                      e.matmul(pO[2 * opar + 1][0:64, :], sOne[:, 0:64], sRecB1[:],
                               start=True, stop=True))
            bst["last_bc"] = v_bc
            P.wait("vector", sPE, v_bc)
            ocol = pr * S + q0
            opn("vector", lambda e, ocol=ocol, opar=opar:
                e.tensor_tensor(sO[0:64, ocol:ocol + 512], sOev0[:],
                                pO[2 * opar][0:64, :], ALU.mult))
            v_nrm = op("vector", sDVE, lambda e, ocol=ocol, opar=opar:
                       e.tensor_tensor(sO[64:128, ocol:ocol + 512], sOev1[:],
                                       pO[2 * opar + 1][0:64, :], ALU.mult))
            bst["o_clean"][opar] = v_nrm
            bst["last_nrm"] = v_nrm

        it = 0
        for pr in range(NP):
            if pr + 1 < NP:
                P.wait("sync", sPE, bst["v_free"].get((pr + 1) % 2, 0))
                v_vdma[pr + 1] = dma("sync", chV[(pr + 1) % 2], sV[(pr + 1) % 2][:],
                                     v_dram.ap()[pr + 1])
            vb = pr % 2
            for qt in range(8):
                q0 = qt * 512
                qb = qt % 2
                opar = it % 2
                nk = 4 * qt + 4
                nxt = (pr, qt + 1) if qt < 7 else ((pr + 1, 0) if pr + 1 < NP else None)
                if nxt is not None:
                    P.wait("sync", sPE, bst["q_free"].get((qt + 1) % 2, 0))
                    v_qdma[nxt] = dma("sync", chQ[(qt + 1) % 2], sQt[(qt + 1) % 2][:],
                                      q_dram.ap()[nxt[0], :, nxt[1] * 512:nxt[1] * 512 + 512])
                P.wait("tensor", chQ[qb], v_qdma[(pr, qt)])
                P.wait("tensor", chV[vb], v_vdma[pr])
                exp_done = {}
                mask_done = {}
                vmm_done = {}

                def emit_vmm(ktv):
                    kbv = ktv % 2
                    if ktv in mask_done:
                        P.wait("tensor", sDVE, mask_done[ktv])
                    else:
                        P.wait("tensor", sACT, exp_done[ktv])
                    if ktv == 0:
                        P.wait("tensor", sDVE, bst["o_clean"][opar])
                    opn("tensor", lambda e, ktv=ktv, kbv=kbv, vb=vb, opar=opar, nk=nk:
                        e.matmul(pO[2 * opar][:], sV[vb][:, ktv * 130: ktv * 130 + 65],
                                 sPP[kbv][:, 0:512], start=(ktv == 0), stop=(ktv == nk - 1)))
                    v = op("tensor", sPE, lambda e, ktv=ktv, kbv=kbv, vb=vb, opar=opar, nk=nk:
                           e.matmul(pO[2 * opar + 1][:], sV[vb][:, ktv * 130 + 65: ktv * 130 + 130],
                                    sPP[kbv][:, 512:1024], start=(ktv == 0), stop=(ktv == nk - 1)))
                    vmm_done[ktv] = v
                    bst["pp_free"][kbv] = v

                for kt in range(nk):
                    kb = kt % 2
                    P.wait("tensor", sACT, bst["sc_free"].get(kb, 0))
                    opn("tensor", lambda e, pr=pr, kt=kt, kb=kb, qb=qb:
                        e.matmul(pSc[kb][:, 0:512],
                                 sK[0:64, pr * S + kt * 128: pr * S + (kt + 1) * 128],
                                 sQt[qb][0:64, :], start=True, stop=True,
                                 tile_position=(0, 0)))
                    v_s = op("tensor", sPE, lambda e, pr=pr, kt=kt, kb=kb, qb=qb:
                             e.matmul(pSc[kb][:, 512:1024],
                                      sK[64:128, pr * S + kt * 128: pr * S + (kt + 1) * 128],
                                      sQt[qb][64:128, :], start=True, stop=True,
                                      tile_position=(64, 0)))
                    if kt == min(2, nk - 1) and it > 0:
                        emit_tail_body(it - 1)
                    P.wait("scalar", sPE, v_s)
                    P.wait("scalar", sPE, bst["pp_free"].get(kb, 0))
                    v_e = op("scalar", sACT, lambda e, kb=kb:
                             e.activation(sPP[kb][:], pSc[kb][:], AF.Exp,
                                          bias=CSHIFT, scale=0.125))
                    exp_done[kt] = v_e
                    bst["sc_free"][kb] = v_e
                    if kt >= 4 * qt:
                        m = kt - 4 * qt
                        P.wait("vector", sACT, v_e)
                        mask_done[kt] = op("vector", sDVE,
                                           lambda e, kb=kb, m=m:
                                           e.tensor_tensor(sPP[kb][:], sPP[kb][:],
                                                           sMask[:, m * 1024:(m + 1) * 1024],
                                                           ALU.mult))
                    if kt >= 1:
                        emit_vmm(kt - 1)
                    if kt == nk - 1:
                        emit_vmm(kt)

                v_last = vmm_done[nk - 1]
                emit_tail_head(it, opar, pr, q0, v_last)
                bst["q_free"][qb] = v_last
                bst["v_free"][vb] = v_last
                it += 1
        emit_tail_body(it - 1)

        if "dbg_o" in dbg_d:
            P.wait("sync", sDVE, sDVE.n)
            dma("sync", chDbg, dbg_d["dbg_o"].ap(), sO[:])
        P.barrier()
        stPB.close()
        stB.close()
    stK.close()

    cmp = {}
    if phase_limit >= 4:
        stCm = ExitStack()
        M_ = lambda n, s, d: stCm.enter_context(nc.sbuf_tensor(n, s, d, side="left"))
        sMW = [M_(f"sMW{i}", [128, 32 * 128], BF16) for i in range(2)]
        sScW = M_("sScW", [128, 32 * 128], BF16)
        sIn = [M_(f"sIn{i}", [128, DP], BF16) for i in range(2)]
        sStg = [M_(f"sStg{i}", [128, DP], BF16) for i in range(2)]
        sBg2 = M_("sBg2", [128, 32], BF16)
        sT3 = M_("sT3", [128, 2 * 9 * 128], BF16)
        stPZ = ExitStack()
        pF = stPZ.enter_context(nc.psum_tensor("pFz", [128, 1024], F32))
        pB3 = [stPZ.enter_context(nc.psum_tensor(f"pB3z{i}", [128, 512], BF16)) for i in range(2)]
        mstc = dict(mw_gate={}, scw_busy={}, v_ld={}, v0=0, pf_free={},
                    we_free={}, in_free={}, t3p_free={}, stg_free={},
                    t3stg={}, bankctr=0)

        def _scat_load(src_d, nsteps, step, c):
            row = 32 * nsteps * 128
            P.wait("gpsimd", sPE, mstc["scw_busy"].get(c, 0))
            mstc["v_ld"][(step, c)] = dma(
                "gpsimd", chScat[c % 2],
                sScW.ap().rearrange("p (b k) -> p b k", b=32)[:, c * 8:(c + 1) * 8, :],
                bass.AP(src_d, step * 128 + c * 8 * nsteps * 128,
                        [[row, 128], [nsteps * 128, 8], [1, 128]]))

        def _chain_pre(src_d, nsteps, m0):
            row = 32 * nsteps * 128
            P.wait("gpsimd", sPE, mstc["mw_gate"].get(m0, 0))
            mstc["v0"] = dma("gpsimd", chScat[0],
                             sMW[m0].ap().rearrange("p (b c) -> p b c", b=32),
                             bass.AP(src_d, 0, [[row, 128], [nsteps * 128, 32], [1, 128]]))
            for c in range(4):
                _scat_load(src_d, nsteps, 1, c)

        def _chain_run(src_d, nsteps, m0):
            v0 = mstc["v0"]
            evprev = {}
            for step in range(1, nsteps):
                cur = sMW[(m0 + step - 1) % 2]
                out = sMW[(m0 + step) % 2]
                for chunk in range(4):
                    P.wait("tensor", chScat[0], v0)
                    P.wait("tensor", chScat[chunk % 2], mstc["v_ld"][(step, chunk)])
                    for ep in evprev.get(chunk, ()):
                        P.wait("tensor", ep[1], ep[2])
                    evs = []
                    for half in range(2):
                        eng0, sem0, val0 = mstc["pf_free"].get(half, ("vector", sDVE, 0))
                        P.wait("tensor", sem0, val0)
                        for j in range(4):
                            b = chunk * 8 + half * 4 + j
                            bld = (lambda e, half=half, j=j, b=b, cur=cur:
                                   e.matmul(pF[:, half * 512 + j * 128: half * 512 + (j + 1) * 128],
                                            sScW[:, b * 128:(b + 1) * 128],
                                            cur[:, b * 128:(b + 1) * 128],
                                            start=True, stop=True))
                            if j < 3:
                                opn("tensor", bld)
                            else:
                                v_bm = op("tensor", sPE, bld)
                        eng = "vector" if half == 0 else "scalar"
                        sem_ = sDVE if half == 0 else sACT
                        P.wait(eng, sPE, v_bm)
                        if eng == "vector":
                            v_ev = op(eng, sem_, lambda e, chunk=chunk, half=half, out=out:
                                      e.tensor_copy(out[:, chunk * 1024 + half * 512:
                                                        chunk * 1024 + (half + 1) * 512],
                                                    pF[:, half * 512:(half + 1) * 512]))
                        else:
                            v_ev = op(eng, sem_, lambda e, chunk=chunk, half=half, out=out:
                                      e.copy(out[:, chunk * 1024 + half * 512:
                                                 chunk * 1024 + (half + 1) * 512],
                                             pF[:, half * 512:(half + 1) * 512]))
                        mstc["pf_free"][half] = (eng, sem_, v_ev)
                        evs.append((eng, sem_, v_ev))
                    evprev[chunk] = evs
                    mstc["scw_busy"][chunk] = sPE.n
                    if step + 1 < nsteps:
                        _scat_load(src_d, nsteps, step + 1, chunk)
                    yield
            mstc["mw_gate"][(m0 + nsteps - 2) % 2] = sPE.n
            cmp["final"] = sMW[(m0 + nsteps - 1) % 2]

        def _apply_one(final, blk, sb, nsl, v_rhs, chrhs):
            for sem, val in mstc["stg_free"].get(sb, ()):
                P.wait("scalar", sem, val)
            banks = [(pF, 0), (pF, 512)] + cmp.get("xbanks", [])
            evl = []
            for k, (c0, w) in enumerate(nsl):
                bi = mstc["bankctr"] % len(banks)
                mstc["bankctr"] += 1
                bt, cb = banks[bi]
                P.wait("tensor", chrhs, v_rhs)
                eng_a, sem_a, val_a = mstc["pf_free"].get(bi, ("scalar", sACT, 0))
                P.wait("tensor", sem_a, val_a)
                v_a = op("tensor", sPE, lambda e, bt=bt, cb=cb, w=w, blk=blk, c0=c0,
                         final=final, sb=sb:
                         e.matmul(bt[:, cb:cb + w], final[:, blk * 128:(blk + 1) * 128],
                                  sIn[sb][:, c0:c0 + w], start=True, stop=True))
                P.wait("scalar", sPE, v_a)
                v_e = op("scalar", sACT, lambda e, bt=bt, cb=cb, sb=sb, c0=c0, w=w:
                         e.copy(sStg[sb][:, c0:c0 + w], bt[:, cb:cb + w]))
                mstc["pf_free"][bi] = ("scalar", sACT, v_e)
                evl.append(v_e)
            return evl

        def compose_gen():
            _chain_pre(upA_d, 5, 0)
            yield from _chain_run(upA_d, 5, 0)
            finalA = cmp["final"]

            def pre_upA(u):
                sb = u % 2
                P.wait("gpsimd", sPE, mstc["we_free"].get(sb, 0))
                return dma("gpsimd", chWEc[sb], sIn[sb][:],
                           wexpTb_d.ap().rearrange("(u p) c -> u p c", p=128)[u])

            v_rhs = {0: pre_upA(0)}
            for u in range(32):
                sb = u % 2
                if u + 1 < 32:
                    v_rhs[u + 1] = pre_upA(u + 1)
                evl = _apply_one(finalA, u, sb, ((0, 512), (512, 512), (1024, 128)),
                                 v_rhs[u], chWEc[sb])
                P.wait("gpsimd", sACT, evl[-1])
                v_st = dma("gpsimd", chStg[sb],
                           w1_dram.ap().rearrange("(u p) c -> u p c", p=128)[u],
                           sStg[sb][:])
                mstc["stg_free"][sb] = [(chStg[sb], v_st)]
                mstc["we_free"][sb] = sPE.n
                if u == 4:
                    _chain_pre(upB_d, 7, 1)
                yield
            mstc["mw_gate"][0] = sPE.n
            yield from _chain_run(upB_d, 7, 1)
            finalB = cmp["final"]
            for chs in chStg:
                P.wait("gpsimd", chs, chs.n)

            def pre_upB(g):
                sb = g % 2
                P.wait("gpsimd", sPE, mstc["in_free"].get(sb, 0))
                return dma("gpsimd", chIn[sb], sIn[sb][:],
                           bass.AP(w1_dram, g * DP, [[32 * DP, 128], [1, DP]]))

            v_rhs = {0: pre_upB(0)}
            for g in range(32):
                sb = g % 2
                if g + 1 < 32:
                    v_rhs[g + 1] = pre_upB(g + 1)
                evl = _apply_one(finalB, g, sb, ((0, 512), (512, 512), (1024, 128)),
                                 v_rhs[g], chIn[sb])
                mstc["in_free"][sb] = sPE.n
                yield
                P.wait("vector", sACT, evl[2])
                v_bg = op("vector", sDVE, lambda e, sb=sb, g=g:
                          e.tensor_copy(sBg2[:, g:g + 1], sStg[sb][:, 1024:1025]))
                for bi, (dcs, ve) in enumerate((((0, 1, 2, 3), evl[0]),
                                                ((4, 5, 6, 7), evl[1]),
                                                ((8,), evl[2]))):
                    bank = pB3[bi % 2]
                    engt, semt, valt = mstc["t3p_free"].get(bi % 2, ("vector", sDVE, 0))
                    P.wait("tensor", semt, valt)
                    P.wait("tensor", sACT, ve)
                    for kk, dc in enumerate(dcs):
                        bld = (lambda e, bank=bank, kk=kk, dc=dc, sb=sb:
                               e.transpose(bank[:, kk * 128:(kk + 1) * 128],
                                           sStg[sb][:, dc * 128:(dc + 1) * 128],
                                           sIdB[:]))
                        if kk < len(dcs) - 1:
                            opn("tensor", bld)
                        else:
                            v_t = op("tensor", sPE, bld)
                    P.wait("vector", sPE, v_t)
                    if bi == 0:
                        P.wait("vector", chT3, mstc["t3stg"].get(g % 2, 0))
                    nb = len(dcs)
                    dc0 = dcs[0]
                    base = (g % 2) * 9 * 128
                    v_t3 = op("vector", sDVE,
                              lambda e, bank=bank, nb=nb, dc0=dc0, base=base:
                              e.tensor_copy(sT3[:, base + dc0 * 128: base + (dc0 + nb) * 128],
                                            bank[:, 0:nb * 128]))
                    mstc["t3p_free"][bi % 2] = ("vector", sDVE, v_t3)
                P.wait("gpsimd", sDVE, v_t3)
                mstc["t3stg"][g % 2] = dma(
                    "gpsimd", chT3,
                    bass.AP(w1f_dram, g * 128, [[HID, 128], [128 * HID, 9], [1, 128]]),
                    bass.AP(sT3, (g % 2) * 9 * 128, [[2 * 9 * 128, 128], [128, 9], [1, 128]]))
                mstc["stg_free"][sb] = [(sPE, sPE.n), (sDVE, v_bg)]
                if g == 4:
                    _chain_pre(dnB_d, 7, 0)
                yield
            mstc["mw_gate"][1] = sPE.n
            cmp["w1f_done"] = [(chT3, chT3.n)]
            P.wait("vector", sDVE, sDVE.n)
            cmp["v_bh"] = op("vector", sDVE, lambda e:
                             e.tensor_tensor(sBiasH[:], sBg2[:], sUpb[:], ALU.add))
            yield
            yield from _chain_run(dnB_d, 7, 0)
            finalDB = cmp["final"]

            def pre_dnB(g):
                sb = g % 2
                P.wait("gpsimd", sPE, mstc["we_free"].get(sb, 0))
                return dma("gpsimd", chWEc[sb], sIn[sb][:, 0:D],
                           wconG_d.ap().rearrange("(g p) c -> g p c", p=128)[g])

            v_rhs = {0: pre_dnB(0)}
            for g in range(32):
                sb = g % 2
                if g + 1 < 32:
                    v_rhs[g + 1] = pre_dnB(g + 1)
                evl = _apply_one(finalDB, g, sb, ((0, 512), (512, 512)),
                                 v_rhs[g], chWEc[sb])
                P.wait("gpsimd", sACT, evl[-1])
                v_st = dma("gpsimd", chStg[sb],
                           r_dram.ap().rearrange("(g p) c -> g p c", p=128)[g],
                           sStg[sb][:, 0:D])
                mstc["stg_free"][sb] = [(chStg[sb], v_st)]
                mstc["we_free"][sb] = sPE.n
                if g == 4:
                    _chain_pre(dnA_d, 5, 1)
                yield
            mstc["mw_gate"][0] = sPE.n
            yield from _chain_run(dnA_d, 5, 1)
            finalDA = cmp["final"]
            for chs in chStg:
                P.wait("gpsimd", chs, chs.n)

            def pre_dnA(u):
                sb = u % 2
                P.wait("gpsimd", sPE, mstc["in_free"].get(sb, 0))
                for a4 in range(4):
                    v = dma("gpsimd", chIn[sb],
                            sIn[sb][a4 * 32:(a4 + 1) * 32, 0:D],
                            bass.AP(r_dram, (u * 4 + a4) * D, [[128 * D, 32], [1, D]]))
                return v

            v_rhs = {0: pre_dnA(0)}
            for u in range(32):
                sb = u % 2
                if u + 1 < 32:
                    v_rhs[u + 1] = pre_dnA(u + 1)
                evl = _apply_one(finalDA, u, sb, ((0, 512), (512, 512)),
                                 v_rhs[u], chIn[sb])
                P.wait("gpsimd", sACT, evl[-1])
                v_st = dma("gpsimd", chStg[sb],
                           r2_dram.ap().rearrange("(u p) c -> u p c", p=128)[u],
                           sStg[sb][:, 0:D])
                mstc["stg_free"][sb] = [(chStg[sb], v_st)]
                mstc["in_free"][sb] = sPE.n
                if cmp.get("sr_load"):
                    cmp["sr_load"](u, v_st, chStg[sb])
                yield
            mstc["mw_gate"][1] = sPE.n

        cmp["gen"] = compose_gen()
        cmp["seed_pf"] = lambda d: mstc["pf_free"].update(
            {k: v for k, v in d.items()})

        def drive(n):
            g = cmp.get("gen")
            if g is None:
                return
            for _ in range(n):
                try:
                    next(g)
                except StopIteration:
                    cmp["gen"] = None
                    return
        cmp["drive"] = drive
    else:
        cmp["drive"] = lambda n: None

    stO.close()
    if phase_limit >= 5:
        sW1F = st.enter_context(nc.sbuf_tensor("sW1F", [128, 8 * HID], BF16, side="right"))

        def emit_w1f_load(pe_val):
            P.wait("sync", sPE, pe_val)
            for sem_, val_ in cmp["w1f_done"]:
                P.wait("sync", sem_, val_)
            chans = []
            for q in range(4):
                v = dma("sync", chWF[q],
                        sW1F.ap().rearrange("p (dc c) -> p dc c", dc=8)[:, q * 2:(q + 1) * 2, :],
                        w1f_dram.ap().rearrange("(dc p) c -> p dc c", p=128)[:, q * 2:(q + 1) * 2, :])
                chans.append((chWF[q], v))
            cmp["w1f_chans"] = chans

    if phase_limit >= 3:
        stC = ExitStack()
        C_ = lambda n, s, d: stC.enter_context(nc.sbuf_tensor(n, s, d, side="left"))
        sWo = C_("sWo", [128, 8 * 1024], BF16)
        sYT = C_("sYT", [128, 8 * 512], F32)
        sY = C_("sY", [128, 4 * D], F32)
        sX1 = C_("sX1", [128, 4 * D], F32)
        sLn2 = C_("sLn2", [128, 4 * D], F32)
        sL2Tg = C_("sL2Tg", [128, 8 * 512], BF16)
        scrC = tuple(C_(f"scrC{i}", [128, 8], F32) for i in range(4))
        stPC = ExitStack()
        pY = [stPC.enter_context(nc.psum_tensor(f"pYc{i}", [128, 512], F32)) for i in range(2)]
        pT = [stPC.enter_context(nc.psum_tensor(f"pTc{i}", [128, 512], F32)) for i in range(2)]

        v_wo = dma("gpsimd", chWo, sWo.ap().rearrange("p (c n) -> p c n", c=8),
                   woT_d.ap().rearrange("(c p) n -> p c n", p=128))
        x_r = x_d.ap().rearrange("(g a p) d -> g p a d", a=4, p=128)
        cst = dict(ymm_free={}, yt_used=0, pt_free=0, sy_ev=0,
                   x1add={}, ln2=None, ln2_last=0, ln3_last=0, tr2_pe=0,
                   x1sp_free=0, l2t_free=0, v_x1={})
        drv = cmp["drive"]

        def emit_ymm(tg):
            t0 = tg * 512
            for oc in range(8):
                bank = pY[oc % 2]
                P.wait("tensor", sACT, cst["ymm_free"].get(oc % 2, 0))
                P.wait("tensor", chWo, v_wo)
                for pc in range(8):
                    bld = (lambda e, b=bank, oc=oc, pc=pc, t0=t0:
                           e.matmul(b[:], sWo[:, pc * 1024 + oc * 128: pc * 1024 + (oc + 1) * 128],
                                    sO[:, pc * S + t0: pc * S + t0 + 512],
                                    start=(pc == 0), stop=(pc == 7)))
                    if pc < 7:
                        opn("tensor", bld)
                    else:
                        v_mm = op("tensor", sPE, bld)
                P.wait("scalar", sPE, v_mm)
                P.wait("scalar", sPE, cst["yt_used"])
                cst["ymm_free"][oc % 2] = op("scalar", sACT,
                                             lambda e, b=bank, oc=oc:
                                             e.copy(sYT[:, oc * 512:(oc + 1) * 512], b[:]))
            cst["yt_ev"] = sACT.n

        def emit_ytr(tg):
            v_yt_ev = cst["yt_ev"]
            for a in range(4):
                P.wait("tensor", sACT, v_yt_ev)
                P.wait("tensor", sACT, cst["pt_free"])
                for oc in range(8):
                    bb = pT[0] if oc < 4 else pT[1]
                    bld = (lambda e, bb=bb, oc=oc, a=a:
                           e.transpose(bb[:, (oc % 4) * 128:(oc % 4 + 1) * 128],
                                       sYT[:, oc * 512 + a * 128: oc * 512 + (a + 1) * 128],
                                       sIdF[:]))
                    if oc < 7:
                        opn("tensor", bld)
                    else:
                        v_tr = op("tensor", sPE, bld)
                P.wait("scalar", sPE, v_tr)
                P.wait("scalar", sDVE, cst["ln2_last"])
                opn("scalar", lambda e, a=a: e.copy(sY[:, a * D: a * D + 512], pT[0][:]))
                cst["pt_free"] = op("scalar", sACT,
                                    lambda e, a=a:
                                    e.copy(sY[:, a * D + 512: (a + 1) * D], pT[1][:]))
            cst["yt_used"] = sPE.n
            cst["sy_ev"] = sACT.n

        def emit_lnsec(tg):
            P.wait("vector", sACT, cst["sy_ev"])
            P.wait("vector", sPE, cst["tr2_pe"])
            ln3 = emit_ln_wide(sY[:], [sY[:, a * D:(a + 1) * D] for a in range(4)],
                               sLn2[:], scrC, 4)
            cst["ln3_last"] = ln3[-1]
            P.wait("vector", chX, cst["v_x1"][tg])
            P.wait("vector", sDVE, ln3[-1])
            v_add = 0
            for a in range(4):
                v_add = op("vector", sDVE, lambda e, a=a:
                           e.tensor_tensor(sX1[:, a * D:(a + 1) * D],
                                           sX1[:, a * D:(a + 1) * D],
                                           sY[:, a * D:(a + 1) * D], ALU.add))
            cst["x1add"][tg] = v_add
            P.wait("sync", sDVE, v_add)
            for a in range(4):
                v_sp = dma("sync", chX1,
                           x1_dram.ap().rearrange("(n p) d -> n p d", p=128)[tg * 4 + a],
                           sX1[:, a * D:(a + 1) * D])
            cst["x1sp_free"] = v_sp
            cst["ln2"] = emit_ln_wide(sX1[:], [sLn2[:, a * D:(a + 1) * D] for a in range(4)],
                                      sY[:], scrC, 4, prev=v_add)
            cst["ln2_last"] = cst["ln2"][-1]

        def emit_tr2(tg):
            t0 = tg * 512
            for a in range(4):
                P.wait("tensor", sDVE, cst["ln2"][a])
                P.wait("tensor", sACT, cst["pt_free"])
                for dc in range(8):
                    bb = pT[0] if dc < 4 else pT[1]
                    bld = (lambda e, bb=bb, dc=dc, a=a:
                           e.transpose(bb[:, (dc % 4) * 128:(dc % 4 + 1) * 128],
                                       sLn2[:, a * D + dc * 128: a * D + (dc + 1) * 128],
                                       sIdF[:]))
                    if dc < 7:
                        opn("tensor", bld)
                    else:
                        v_tr2 = op("tensor", sPE, bld)
                P.wait("scalar", sPE, v_tr2)
                P.wait("scalar", chL2, cst["l2t_free"])
                opn("scalar", lambda e, a=a:
                    e.copy(bass.AP(sL2Tg, a * 128, [[4096, 128], [512, 4], [1, 128]]),
                           bass.AP(pT[0], 0, [[512, 128], [128, 4], [1, 128]])))
                cst["pt_free"] = op("scalar", sACT, lambda e, a=a:
                                    e.copy(bass.AP(sL2Tg, 2048 + a * 128,
                                                   [[4096, 128], [512, 4], [1, 128]]),
                                           bass.AP(pT[1], 0, [[512, 128], [128, 4], [1, 128]])))
            cst["tr2_pe"] = sPE.n
            v_l2ev = sACT.n
            P.wait("sync", sACT, v_l2ev)
            l2r = ln2T_dram.ap().rearrange("(dc p) t -> dc p t", p=128)
            for dc in range(8):
                v = dma("sync", chL2, l2r[dc][:, t0:t0 + 512],
                        sL2Tg[:, dc * 512:(dc + 1) * 512])
            cst["l2t_free"] = v

        def emit_xload(tg):
            P.wait("sync", chX1, cst["x1sp_free"])
            P.wait("sync", sDVE, cst["ln2_last"])
            cst["v_x1"][tg] = dma("sync", chX,
                                  sX1.ap().rearrange("p (a d) -> p a d", a=4), x_r[tg])

        for tg in range(8):
            emit_ymm(tg)
            if tg > 0:
                emit_lnsec(tg - 1)
            emit_xload(tg)
            emit_ytr(tg)
            if tg > 0:
                emit_tr2(tg - 1)
        emit_lnsec(7)
        drv(16)
        emit_tr2(7)
        if phase_limit >= 4:
            cmp["xbanks"] = [(pY[0], 0), (pY[1], 0)]
            gate = ("scalar", sACT, sACT.n)
            cmp["seed_pf"]({2: gate, 3: gate})
        stC.close()
        if phase_limit >= 5:
            sR_ = st.enter_context(nc.sbuf_tensor("sR", [128, 32 * D], BF16, side="right"))
            cmp["sR"] = sR_
            rl_done = {}

            def _sr_load(u, v_st, chs):
                if (u + 1) % 8 != 0:
                    return
                q = u // 8
                for chs_ in chStg:
                    P.wait("sync", chs_, chs_.n)
                rl_done[q] = dma("sync", chRL[q],
                                 sR_[32 * q:32 * (q + 1), :].rearrange(
                                     "p (g c) -> p g c", c=D),
                                 bass.AP(r2_dram, q * 32 * 32 * D,
                                         [[32 * D, 32], [D, 32], [1, D]]))
            cmp["sr_load"] = _sr_load
            cmp["rl_done"] = rl_done
        while cmp.get("gen") is not None:
            drv(1)
            if phase_limit >= 5 and cmp.get("w1f_done") and not cmp.get("w1f_chans"):
                emit_w1f_load(sPE.n)

        if "dbg_x1" in dbg_d:
            P.wait("sync", chX1, chX1.n)
            dma("sync", chDbg, dbg_d["dbg_x1"].ap(), x1_dram.ap())
        if "dbg_l2t" in dbg_d:
            P.wait("sync", chL2, chL2.n)
            dma("sync", chDbg, dbg_d["dbg_l2t"].ap(), ln2T_dram.ap())
        if "dbg_bh" in dbg_d and phase_limit >= 4:
            P.wait("sync", sDVE, sDVE.n)
            dma("sync", chDbg, dbg_d["dbg_bh"].ap(), sBiasH[:])
        P.barrier()
        stPC.close()
    elif phase_limit >= 4:
        while cmp.get("gen") is not None:
            cmp["drive"](50)
        P.barrier()
    if phase_limit >= 4:
        stPZ.close()
        stCm.close()

    if phase_limit >= 5:
        sR = cmp["sR"]
        if cmp.get("w1f_chans") is None:
            emit_w1f_load(sPE.n)
        rl_chans = [(chRL[q], cmp["rl_done"][q]) for q in range(4)]
        for ch_, v_ in cmp["w1f_chans"]:
            P.wait("tensor", ch_, v_)
        stD = ExitStack()
        D_ = lambda n, s, d: stD.enter_context(nc.sbuf_tensor(n, s, d, side="left"))
        sL2T = [D_(f"sL2T{i}", [128, 8 * 256], BF16) for i in range(2)]
        sX1g = [D_(f"sX1g{i}", [128, 2 * D], F32) for i in range(2)]
        sG = D_("sG", [128, 32 * 256], BF16)
        sBY = D_("sBY", [128, D], F32)
        sYF = D_("sYF", [128, D], F32)
        sLn4 = D_("sLn4", [128, D], F32)
        sOut = [D_(f"sOut{i}", [128, D], F32) for i in range(2)]
        sSqD = D_("sSqD", [128, D], F32)
        scrD = tuple(D_(f"scrD{i}", [128, 8], F32) for i in range(4))
        stPD = ExitStack()
        pH = [stPD.enter_context(nc.psum_tensor(f"pH{i}", [128, 256], F32)) for i in range(2)]
        pYd = [stPD.enter_context(nc.psum_tensor(f"pYd{i}", [128, 512], F32)) for i in range(2)]

        v_b1 = dma("sync", chBY, sBY[:], bcon_d.ap())
        v_b2 = dma("sync", chBY, sYF[:], dnb_d.ap())
        P.wait("vector", chBY, v_b2)
        v_by = op("vector", sDVE, lambda e: e.tensor_tensor(sBY[:], sBY[:], sYF[:], ALU.add))

        dst = dict(h_free={}, g_free=0, y_free={}, l2t_free={}, x1g_free={},
                   out_free={}, yf_free=0, ln4_free=0)
        v_l2 = {}
        v_x1g = {}

        def dma_group(g):
            gb = g % 2
            P.wait("sync", sPE, dst["l2t_free"].get(gb, 0))
            v_l2[g] = dma("sync", chL2D[gb],
                          sL2T[gb].ap().rearrange("p (dc c) -> p dc c", dc=8),
                          bass.AP(ln2T_dram, g * 256, [[S, 128], [128 * S, 8], [1, 256]]))
            P.wait("sync", sDVE, dst["x1g_free"].get(gb, 0))
            v_x1g[g] = dma("sync", chX1D[gb],
                           sX1g[gb].ap().rearrange("p (a d) -> p a d", a=2),
                           x1_dram.ap().rearrange("(n a p) d -> n p a d", a=2, p=128)[g])

        dma_group(0)
        for g in range(16):
            gb = g % 2
            if g + 1 < 16:
                dma_group(g + 1)
            P.wait("tensor", chL2D[gb], v_l2[g])
            for u in range(32):
                ub = u % 2
                P.wait("tensor", sACT, dst["h_free"].get(ub, 0))
                for dc in range(8):
                    bld = (lambda e, ub=ub, u=u, dc=dc, gb=gb:
                           e.matmul(pH[ub][:], sW1F[:, dc * HID + u * 128: dc * HID + (u + 1) * 128],
                                    sL2T[gb][:, dc * 256:(dc + 1) * 256],
                                    start=(dc == 0), stop=(dc == 7)))
                    if dc < 7:
                        opn("tensor", bld)
                    else:
                        v_h = op("tensor", sPE, bld)
                P.wait("scalar", sPE, v_h)
                P.wait("scalar", sPE, dst["g_free"])
                P.wait("scalar", sDVE, cmp.get("v_bh", 0))
                dst["h_free"][ub] = op("scalar", sACT,
                                       lambda e, ub=ub, u=u:
                                       e.activation(sG[:, u * 256:(u + 1) * 256], pH[ub][:],
                                                    AF.Gelu, bias=sBiasH[:, u:u + 1]))
            v_gelu = sACT.n
            for a in range(2):
                for ch_, v_ in rl_chans:
                    P.wait("tensor", ch_, v_)
                P.wait("tensor", sACT, v_gelu)
                P.wait("tensor", sDVE, dst["y_free"].get(0, 0))
                P.wait("tensor", sDVE, dst["y_free"].get(1, 0))
                for u in range(32):
                    for oh in range(2):
                        bld = (lambda e, a=a, u=u, oh=oh:
                               e.matmul(pYd[oh][:], sG[:, u * 256 + a * 128: u * 256 + (a + 1) * 128],
                                        sR[:, u * D + oh * 512: u * D + (oh + 1) * 512],
                                        start=(u == 0), stop=(u == 31)))
                        if u == 31 and oh == 1:
                            v_y = op("tensor", sPE, bld)
                        else:
                            opn("tensor", bld)
                P.wait("vector", sPE, v_y)
                P.wait("vector", sDVE, dst["yf_free"])
                P.wait("vector", sDVE, v_by)
                opn("vector", lambda e: e.tensor_tensor(sYF[:, 0:512], pYd[0][:],
                                                        sBY[:, 0:512], ALU.add))
                v_yf = op("vector", sDVE,
                          lambda e: e.tensor_tensor(sYF[:, 512:1024], pYd[1][:],
                                                    sBY[:, 512:1024], ALU.add))
                dst["y_free"][0] = v_yf
                dst["y_free"][1] = v_yf
                v_ln4 = emit_ln_wide(sYF[:], [sLn4[:]], sSqD[:], scrD, 1,
                                     prev=max(v_yf, dst["ln4_free"]))[-1]
                dst["yf_free"] = v_ln4
                ob = a % 2
                P.wait("vector", chX1D[gb], v_x1g[g])
                P.wait("vector", chOut[ob], dst["out_free"].get(ob, 0))
                P.wait("vector", sDVE, v_ln4)
                v_o = op("vector", sDVE, lambda e, gb=gb, a=a, ob=ob:
                         e.tensor_tensor(sOut[ob][:], sX1g[gb][:, a * D:(a + 1) * D],
                                         sLn4[:], ALU.add))
                dst["ln4_free"] = v_o
                if a == 1:
                    dst["x1g_free"][gb] = v_o
                P.wait("sync", sDVE, v_o)
                dst["out_free"][ob] = dma("sync", chOut[ob],
                                          out_d.ap().rearrange("(n p) d -> n p d", p=128)[g * 2 + a],
                                          sOut[ob][:])
            dst["g_free"] = sPE.n
            dst["l2t_free"][gb] = v_h
        for chf in (chOut[0], chOut[1], chDbg):
            P.wait("gpsimd", chf, chf.n)
            P.wait("sync", chf, chf.n)
        stPD.close()
        stD.close()
    else:
        for chf in (chOut[0], chOut[1], chDbg):
            P.wait("gpsimd", chf, chf.n)
            P.wait("sync", chf, chf.n)

    with st, nc.Block() as block:
        @block.sync
        def _(e):
            for f in P.ops["sync"]:
                f(e)

        @block.tensor
        def _(e):
            for f in P.ops["tensor"]:
                f(e)

        @block.vector
        def _(e):
            for f in P.ops["vector"]:
                f(e)

        @block.scalar
        def _(e):
            for f in P.ops["scalar"]:
                f(e)

        @block.gpsimd
        def _(e):
            for f in P.ops["gpsimd"]:
                f(e)

    return nc


def kernel(**inputs):
    x, shared = _host_prepare(inputs)
    nc = build_nc()
    in_maps = [dict(shared, x=np.ascontiguousarray(x[b])) for b in range(8)]
    res = run_bass_kernel_spmd(nc, in_maps, core_ids=list(range(8)))
    return np.stack([np.asarray(r["out"], np.float32) for r in res.results])


# version 11
# speedup vs baseline: 1.2001x; 1.2001x over previous
import numpy as np
import ml_dtypes
from contextlib import ExitStack

import concourse.bass as bass
import concourse.mybir as mybir
from concourse.bass_utils import run_bass_kernel_spmd

F32 = mybir.dt.float32
BF16 = mybir.dt.bfloat16
AF = mybir.ActivationFunctionType
ALU = mybir.AluOpType
AX = mybir.AxisListType

S = 4096
D = 1024
HID = 4096
NP = 8
DP = 1152
SHIFT = 20.0
EPS = 1e-5


def _qk_perm():
    p = []
    for h in range(16):
        b = 64 * h
        p += [b + 2 * i for i in range(32)]
        p += [b + 2 * i + 1 for i in range(32)]
    return np.array(p, np.int64)


def _rope_tables():
    theta = 10000.0 ** (-2.0 * (np.arange(32, dtype=np.float64) / 64.0))
    freqs = np.arange(S, dtype=np.float64)[:, None] * theta[None, :]
    cos = np.cos(freqs).astype(np.float32)
    sin = np.sin(freqs).astype(np.float32)
    cos128 = np.ascontiguousarray(cos.T[np.arange(128) % 32])
    sin128 = np.ascontiguousarray(sin.T[np.arange(128) % 32])
    sign = np.where((np.arange(128) % 64) < 32, -1.0, 1.0).astype(np.float32)
    return cos128, np.ascontiguousarray(sin128 * sign[:, None])


def _alpha_beta(wstage, stage):
    i = np.arange(HID)
    s = 1 << stage
    half = (i >> stage) & 1
    blk = i >> (stage + 1)
    j = i & (s - 1)
    k = blk * s + j
    return (wstage[k, half, half].astype(np.float32),
            wstage[k, half, 1 - half].astype(np.float32))


def _scat_nat(alpha, beta, stage):
    s = 1 << stage
    mats = np.zeros((32, 128, 128), np.float32)
    v = np.arange(128)
    for u in range(32):
        idx = u * 128 + v
        mats[u, v, v] = alpha[idx]
        mats[u, v, v ^ s] = beta[idx]
    return mats


def _scat_grp(alpha, beta, stage):
    s2 = 1 << (stage - 5)
    mats = np.zeros((32, 128, 128), np.float32)
    p = np.arange(128)
    for g in range(32):
        idx = (p << 5) | g
        mats[g, p, p] = alpha[idx]
        mats[g, p, p ^ s2] = beta[idx]
    return mats


def _pack(tiles_per_block):
    cols = []
    for blk in tiles_per_block:
        cols.extend(blk)
    return np.ascontiguousarray(np.concatenate(cols, axis=1))


def _host_prepare(inputs):
    x = np.ascontiguousarray(np.asarray(inputs["L0_attn_x"], np.float32))
    Wqkv = np.asarray(inputs["L0_attn_W_qkv"], np.float32)
    Wo = np.asarray(inputs["L0_attn_W_o"], np.float32)
    Wexp = np.asarray(inputs["L0_ffn_W_exp"], np.float32)
    b_exp = np.asarray(inputs["L0_ffn_b_exp"], np.float32)
    up_w = np.asarray(inputs["L0_ffn_up_weights"], np.float32)
    up_bias = np.asarray(inputs["L0_ffn_up_bias"], np.float32)
    dn_w = np.asarray(inputs["L0_ffn_down_weights"], np.float32)
    Wcon = np.asarray(inputs["L0_ffn_W_con"], np.float32)
    b_con = np.asarray(inputs["L0_ffn_b_con"], np.float32)
    dn_bias = np.asarray(inputs["L0_ffn_down_bias"], np.float32)

    pqk = _qk_perm()
    perm = np.concatenate([pqk, 1024 + pqk, 2048 + np.arange(1024)])
    cos128, sinS = _rope_tables()

    wexpTb = np.zeros((HID, DP), np.float32)
    wexpTb[:, :D] = Wexp.T
    wexpTb[:, D] = b_exp

    i = np.arange(HID)
    g2pos = (i & 31) * 128 + ((i >> 5) & 127)
    wconG = np.empty_like(Wcon)
    wconG[g2pos] = Wcon

    ab_up = [_alpha_beta(up_w[j], j) for j in range(12)]
    ab_dn = [_alpha_beta(dn_w[j], j) for j in range(12)]
    upA_nat = [_scat_nat(*ab_up[j], j) for j in range(5)]
    upB_grp = [_scat_grp(*ab_up[j], j) for j in range(5, 12)]
    dnA_nat = [_scat_nat(*ab_dn[j], j) for j in range(5)]
    dnB_grp = [_scat_grp(*ab_dn[j], j) for j in range(5, 12)]

    upA = _pack([[upA_nat[4][u].T, upA_nat[3][u], upA_nat[2][u],
                  upA_nat[1][u], upA_nat[0][u]] for u in range(32)])
    upB = _pack([[upB_grp[6][g].T] + [upB_grp[j][g] for j in (5, 4, 3, 2, 1, 0)]
                 for g in range(32)])
    dnB = _pack([[dnB_grp[0][g]] + [dnB_grp[j][g].T for j in (1, 2, 3, 4, 5, 6)]
                 for g in range(32)])
    dnA = _pack([[dnA_nat[0][u]] + [dnA_nat[j][u].T for j in (1, 2, 3, 4)]
                 for u in range(32)])

    masks = np.zeros((128, 4 * 1024), np.float32)
    pp = np.arange(128)[:, None]
    jj = np.arange(512)[None, :]
    for m in range(4):
        mk = (pp + 128 * m <= jj)
        masks[:, m * 1024:m * 1024 + 512] = mk
        masks[:, m * 1024 + 512:(m + 1) * 1024] = mk

    consts = np.zeros((128, 8), np.float32)
    consts[:, 0] = -SHIFT
    consts[:, 1] = EPS
    consts[:, 2] = 1.0 / D
    consts[:, 3] = -1.0 / D
    consts[:, 4] = -0.5
    consts[:, 5] = -1.0

    shared = {
        "wqkvT": np.ascontiguousarray(Wqkv[perm].T),
        "woT": np.ascontiguousarray(Wo.T),
        "wexpTb": wexpTb.astype(ml_dtypes.bfloat16),
        "wconG": np.ascontiguousarray(wconG).astype(ml_dtypes.bfloat16),
        "cosT": cos128, "sinT": sinS,
        "identF": np.eye(128, dtype=np.float32),
        "identB": np.eye(128).astype(ml_dtypes.bfloat16),
        "p32": np.eye(128)[np.arange(128) ^ 32].astype(ml_dtypes.bfloat16),
        "maskq": masks.astype(ml_dtypes.bfloat16),
        "upA": upA.astype(ml_dtypes.bfloat16), "upB": upB.astype(ml_dtypes.bfloat16),
        "dnB": dnB.astype(ml_dtypes.bfloat16), "dnA": dnA.astype(ml_dtypes.bfloat16),
        "upbias": np.ascontiguousarray(up_bias.reshape(128, 32)),
        "bcon": np.ascontiguousarray(np.tile(b_con, (128, 1))),
        "dnb": np.ascontiguousarray(np.tile(dn_bias, (128, 1))),
        "consts": consts,
    }
    return x, shared



class _Sem:
    def __init__(self, h):
        self.h = h
        self.n = 0

    def inc(self, k=1):
        self.n += k
        return self.n


ENGS = ("sync", "tensor", "vector", "scalar", "gpsimd")


class _Plan:
    def __init__(self):
        self.ops = {e: [] for e in ENGS}
        self.dirty = {e: False for e in ENGS}
        self.lastw = {}
        self.bar_sem = None
        self.bar_target = 0

    def emit(self, eng, fn):
        self.ops[eng].append(fn)
        self.dirty[eng] = True

    def wait(self, eng, sem, val):
        v = int(val)
        key = (eng, id(sem))
        if v <= 0 or self.lastw.get(key, 0) >= v:
            return
        self.lastw[key] = v
        h = sem.h
        self.ops[eng].append(lambda e: e.wait_ge(h, v))

    def barrier(self):
        for e in ENGS:
            for sem in self.all_sems:
                self.wait(e, sem, sem.n)
            self.dirty[e] = False


def build_nc(phase_limit=99, debug=(), a_groups=8, a_parts=31):
    nc = bass.Bass("TRN2", target_bir_lowering=False, detect_race_conditions=False)
    P = _Plan()

    x_d = nc.dram_tensor("x", [S, D], F32, kind="ExternalInput")
    wqkvT_d = nc.dram_tensor("wqkvT", [D, 3072], F32, kind="ExternalInput")
    woT_d = nc.dram_tensor("woT", [D, D], F32, kind="ExternalInput")
    wexpTb_d = nc.dram_tensor("wexpTb", [HID, DP], BF16, kind="ExternalInput")
    wconG_d = nc.dram_tensor("wconG", [HID, D], BF16, kind="ExternalInput")
    cos_d = nc.dram_tensor("cosT", [128, S], F32, kind="ExternalInput")
    sin_d = nc.dram_tensor("sinT", [128, S], F32, kind="ExternalInput")
    identF_d = nc.dram_tensor("identF", [128, 128], F32, kind="ExternalInput")
    identB_d = nc.dram_tensor("identB", [128, 128], BF16, kind="ExternalInput")
    p32_d = nc.dram_tensor("p32", [128, 128], BF16, kind="ExternalInput")
    maskq_d = nc.dram_tensor("maskq", [128, 4096], BF16, kind="ExternalInput")
    upA_d = nc.dram_tensor("upA", [128, 5 * 32 * 128], BF16, kind="ExternalInput")
    upB_d = nc.dram_tensor("upB", [128, 7 * 32 * 128], BF16, kind="ExternalInput")
    dnB_d = nc.dram_tensor("dnB", [128, 7 * 32 * 128], BF16, kind="ExternalInput")
    dnA_d = nc.dram_tensor("dnA", [128, 5 * 32 * 128], BF16, kind="ExternalInput")
    upbias_d = nc.dram_tensor("upbias", [128, 32], F32, kind="ExternalInput")
    bcon_d = nc.dram_tensor("bcon", [128, D], F32, kind="ExternalInput")
    dnb_d = nc.dram_tensor("dnb", [128, D], F32, kind="ExternalInput")
    consts_d = nc.dram_tensor("consts", [128, 8], F32, kind="ExternalInput")

    out_d = nc.dram_tensor("out", [S, D], F32, kind="ExternalOutput")
    q_dram = nc.dram_tensor("q_sc", [NP, 128, S], BF16, kind="Internal")
    v_dram = nc.dram_tensor("v_sc", [NP, 128, 4160], BF16, kind="Internal")
    x1_dram = nc.dram_tensor("x1_sc", [S, D], F32, kind="Internal")
    ln2T_dram = nc.dram_tensor("ln2T_sc", [D, S], BF16, kind="Internal")
    w1_dram = nc.dram_tensor("w1_sc", [HID, DP], BF16, kind="Internal")
    w1f_dram = nc.dram_tensor("w1f_sc", [DP, HID], BF16, kind="Internal")
    r_dram = nc.dram_tensor("r_sc", [HID, D], BF16, kind="Internal")
    r2_dram = nc.dram_tensor("r2_sc", [HID, D], BF16, kind="Internal")
    bias_dram = nc.dram_tensor("bias_sc", [HID], BF16, kind="Internal")

    dbg_d = {}
    for name, shape, dt in (("dbg_ln", [128, 4 * D], F32),
                            ("dbg_lnt", [128, 8 * 512], BF16),
                            ("dbg_k", [128, NP * S], BF16),
                            ("dbg_o", [128, NP * S], BF16),
                            ("dbg_x1", [S, D], F32),
                            ("dbg_l2t", [D, S], BF16),
                            ("dbg_w1f", [128, 9 * HID], BF16),
                            ("dbg_r", [128, 32 * D], BF16),
                            ("dbg_bh", [128, 32], F32)):
        if name in debug:
            dbg_d[name] = nc.dram_tensor(name, shape, dt, kind="ExternalOutput")

    st = ExitStack()

    def SEM(name):
        return _Sem(st.enter_context(nc.semaphore(name)))

    sPE = SEM("sPE")
    sACT = SEM("sACT")
    sDVE = SEM("sDVE")
    sPOOL = SEM("sPOOL")
    P.bar_sem = SEM("sBAR")
    chC = SEM("chC"); chCos = SEM("chCos"); chWq = SEM("chWq"); chWo = SEM("chWo")
    chX = SEM("chX"); chX1 = SEM("chX1"); chL2 = SEM("chL2"); chBnc = SEM("chBnc")
    chBY = SEM("chBY"); chDbg = SEM("chDbg")
    chWF = [SEM(f"chWF{i}") for i in range(4)]
    chRL = [SEM(f"chRL{i}") for i in range(4)]
    chQsp = [SEM(f"chQsp{i}") for i in range(2)]
    chVsp = [SEM(f"chVsp{i}") for i in range(2)]
    chV = [SEM(f"chV{i}") for i in range(2)]
    chQ = [SEM(f"chQ{i}") for i in range(2)]
    chScat = [SEM(f"chScat{i}") for i in range(2)]
    chWEc = [SEM(f"chWEc{i}") for i in range(2)]
    chL2D = [SEM(f"chL2D{i}") for i in range(2)]
    chX1D = [SEM(f"chX1D{i}") for i in range(2)]
    chOut = [SEM(f"chOut{i}") for i in range(2)]
    chStg = [SEM(f"chStg{i}") for i in range(2)]
    chT3 = SEM("chT3")
    chIn = [SEM(f"chIn{i}") for i in range(2)]
    P.all_sems = [sPE, sACT, sDVE, sPOOL, chC, chCos, chWq, chWo, chX, chX1,
                  chL2, chBnc, chBY, chDbg] + chWF + chRL + chQsp + chVsp + chV + chQ + \
                 chScat + chWEc + chL2D + chX1D + chOut + chStg + chIn + [chT3]

    def dma(eng, sem, out_ap, in_ap, **kw):
        v = sem.inc(16)
        h = sem.h
        P.emit(eng, lambda e: e.dma_start(out_ap, in_ap, **kw).then_inc(h, 16))
        return v

    def op(eng, sem, build):
        v = sem.inc()
        h = sem.h
        P.emit(eng, lambda e: build(e).then_inc(h, 1))
        return v

    def opn(eng, build):
        P.emit(eng, build)

    sConst = st.enter_context(nc.sbuf_tensor("sConst", [128, 8], F32, side="left"))
    sIdF = st.enter_context(nc.sbuf_tensor("sIdF", [128, 128], F32, side="left"))
    sIdB = st.enter_context(nc.sbuf_tensor("sIdB", [128, 128], BF16, side="left"))
    sP32 = st.enter_context(nc.sbuf_tensor("sP32", [128, 128], BF16, side="left"))
    sOne = st.enter_context(nc.sbuf_tensor("sOne", [1, 128], BF16, side="left"))
    sBiasH = st.enter_context(nc.sbuf_tensor("sBiasH", [128, 32], F32, side="left"))
    sUpb = st.enter_context(nc.sbuf_tensor("sUpb", [128, 32], F32, side="left"))

    dma("sync", chC, sConst[:], consts_d.ap())
    dma("sync", chC, sIdF[:], identF_d.ap())
    dma("sync", chC, sIdB[:], identB_d.ap())
    dma("sync", chC, sP32[:], p32_d.ap())
    const_ld = dma("sync", chC, sUpb[:], upbias_d.ap())
    for e in ENGS:
        P.wait(e, chC, const_ld)
    v_ones = op("gpsimd", sPOOL, lambda e: e.memset(sOne[:], 1.0))

    CSHIFT = sConst[:, 0:1]
    CEPS = sConst[:, 1:2]
    CINVD = sConst[:, 2:3]
    CNINVD = sConst[:, 3:4]
    CNEG1 = sConst[:, 5:6]

    ln_last = {}

    def emit_ln_wide(xin_ap, out_aps, ssq_ap, scr, W, prev=0):
        sm, smsq, svar, srstd = scr
        key = id(scr)
        P.wait("vector", sDVE, max(prev, ln_last.get(key, 0)))
        seg_in = bass.AP(xin_ap.tensor, xin_ap.offset,
                         [list(xin_ap.ap[0]), [D, W], [1, D]])
        seg_sq = bass.AP(ssq_ap.tensor, ssq_ap.offset,
                         [list(ssq_ap.ap[0]), [D, W], [1, D]])
        v_a = op("vector", sDVE, lambda e: e.reduce_sum(
            bass.AP(sm, 0, [[sm.shape[1], 128], [1, W], [1, 1]]), seg_in, axis=AX.X))
        v_b = op("vector", sDVE, lambda e: e.tensor_tensor(ssq_ap, xin_ap, xin_ap, ALU.mult))
        P.wait("vector", sDVE, v_b)
        v_c = op("vector", sDVE, lambda e: e.reduce_sum(
            bass.AP(smsq, 0, [[smsq.shape[1], 128], [1, W], [1, 1]]), seg_sq, axis=AX.X))
        v_d = op("vector", sDVE, lambda e: e.tensor_scalar(sm[:, 0:W], sm[:, 0:W],
                                                           CNINVD, None, ALU.mult))
        P.wait("vector", sDVE, v_d)
        v_e = op("vector", sDVE, lambda e: e.tensor_tensor(svar[:, 0:W], sm[:, 0:W],
                                                           sm[:, 0:W], ALU.mult))
        v_f = op("vector", sDVE, lambda e: e.tensor_scalar(smsq[:, 0:W], smsq[:, 0:W],
                                                           CINVD, CEPS, ALU.mult, ALU.add))
        P.wait("vector", sDVE, v_f)
        v_g = op("vector", sDVE, lambda e: e.tensor_tensor(svar[:, 0:W], smsq[:, 0:W],
                                                           svar[:, 0:W], ALU.subtract))
        P.wait("scalar", sDVE, v_g)
        v_ln = op("scalar", sACT, lambda e: e.activation(srstd[:, 0:W], svar[:, 0:W], AF.Ln))
        P.wait("scalar", sACT, v_ln)
        v_rs = op("scalar", sACT, lambda e: e.activation(srstd[:, 0:W], srstd[:, 0:W],
                                                         AF.Exp, scale=-0.5))
        P.wait("vector", sACT, v_rs)
        outs = []
        for a in range(W):
            xin_a = bass.AP(xin_ap.tensor, xin_ap.offset + a * D,
                            [list(xin_ap.ap[0]), [1, D]])
            outs.append(op("vector", sDVE,
                           lambda e, o=out_aps[a], xin_a=xin_a, a=a:
                           e.tensor_scalar(o, xin_a, sm[:, a:a + 1],
                                           srstd[:, a:a + 1], ALU.add, ALU.mult)))
        ln_last[key] = outs[-1]
        return outs

    stK = ExitStack()
    sK = stK.enter_context(nc.sbuf_tensor("sK", [128, NP * S], BF16, side="left"))

    if phase_limit >= 1:
        stA = ExitStack()
        A = lambda n, s, d: stA.enter_context(nc.sbuf_tensor(n, s, d, side="left"))
        sWq = A("sWq", [128, 8 * 3072], BF16)
        sCos = A("sCos", [128, S], F32)
        sSin = A("sSin", [128, S], F32)
        sX = A("sX", [128, 4 * D], F32)
        sLN = A("sLN", [128, 4 * D], F32)
        sLnT = A("sLnT", [128, 8 * 512], BF16)
        sKraw = [A(f"sKraw{i}", [128, 512], BF16) for i in range(2)]
        sTa = A("sTa", [128, 512], BF16)
        sQsp = [A(f"sQsp{i}", [128, 512], BF16) for i in range(2)]
        ssqA = A("ssqA", [128, 1024], F32)
        sVraw = [A(f"sVraw{i}", [128, 512], BF16) for i in range(2)]
        sVt = [A(f"sVt{i}", [128, 520], BF16) for i in range(2)]
        scrA = tuple(A(f"scrA{i}", [128, 8], F32) for i in range(4))
        stPA = ExitStack()
        pTr = [stPA.enter_context(nc.psum_tensor(f"pTr{i}", [128, 512], F32)) for i in range(2)]
        pQk = [stPA.enter_context(nc.psum_tensor(f"pQk{i}", [128, 512], F32)) for i in range(2)]
        pSw = stPA.enter_context(nc.psum_tensor("pSw", [128, 512], F32))
        pVt = [stPA.enter_context(nc.psum_tensor(f"pVt{i}", [128, 512], BF16)) for i in range(2)]

        wq_r = wqkvT_d.ap().rearrange("(c p) n -> p c n", p=128)
        sWq_r = sWq.ap().rearrange("p (c n) -> p c n", c=8)
        v_wq_dc = []
        for dc in range(8):
            v_wq_dc.append(dma("gpsimd", chWq, sWq_r[:, dc:dc + 1, :],
                               wq_r[:, dc:dc + 1, :]))
        for i in range(2):
            t = sVt[i]
            opn("gpsimd", lambda e, t=t: e.memset(bass.AP(t, 64, [[520, 128], [130, 4]]), 1.0))
            if i == 0:
                opn("gpsimd", lambda e, t=t: e.memset(bass.AP(t, 129, [[520, 128], [130, 4]]), 1.0))
        v_vones = op("gpsimd", sPOOL, lambda e, t=sVt[1]:
                     e.memset(bass.AP(t, 129, [[520, 128], [130, 4]]), 1.0))

        x_r = x_d.ap().rearrange("(g a p) d -> g p a d", a=4, p=128)
        st_ = dict(tr_free={}, qkv_free={}, sw_free=0, vt_free={}, kraw_free={},
                   qsp_free={}, vraw_free={}, vtile_free={}, x_free=0,
                   lnT_used=0, ln_tr_done=0, v_x={}, lnA_last=0)

        def xload(tg, queue):
            P.wait(queue, sDVE, st_["x_free"])
            st_["v_x"][tg] = dma(queue, chX,
                                 sX.ap().rearrange("p (a d) -> p a d", a=4), x_r[tg])

        def emit_ln_A(tg):
            sm, smsq, svar, srstd = scrA
            P.wait("vector", chX, st_["v_x"][tg])
            P.wait("vector", sDVE, st_["lnA_last"])
            v_a = op("vector", sDVE, lambda e: e.reduce_sum(
                bass.AP(sm, 0, [[sm.shape[1], 128], [1, 4], [1, 1]]),
                bass.AP(sX, 0, [[4 * D, 128], [D, 4], [1, D]]), axis=AX.X))
            prev = v_a
            for a in range(4):
                P.wait("vector", sDVE, prev)
                v_m = op("vector", sDVE, lambda e, a=a:
                         e.tensor_tensor(ssqA[:], sX[:, a * D:(a + 1) * D],
                                         sX[:, a * D:(a + 1) * D], ALU.mult))
                P.wait("vector", sDVE, v_m)
                prev = op("vector", sDVE, lambda e, a=a:
                          e.reduce_sum(smsq[:, a:a + 1], ssqA[:], axis=AX.X))
            P.wait("vector", sDVE, prev)
            v_d = op("vector", sDVE, lambda e: e.tensor_scalar(sm[:, 0:4], sm[:, 0:4],
                                                               CNINVD, None, ALU.mult))
            P.wait("vector", sDVE, v_d)
            v_e = op("vector", sDVE, lambda e: e.tensor_tensor(svar[:, 0:4], sm[:, 0:4],
                                                               sm[:, 0:4], ALU.mult))
            v_f = op("vector", sDVE, lambda e: e.tensor_scalar(smsq[:, 0:4], smsq[:, 0:4],
                                                               CINVD, CEPS, ALU.mult, ALU.add))
            P.wait("vector", sDVE, v_f)
            v_g = op("vector", sDVE, lambda e: e.tensor_tensor(svar[:, 0:4], smsq[:, 0:4],
                                                               svar[:, 0:4], ALU.subtract))
            P.wait("scalar", sDVE, v_g)
            v_ln = op("scalar", sACT, lambda e: e.activation(srstd[:, 0:4], svar[:, 0:4], AF.Ln))
            P.wait("scalar", sACT, v_ln)
            v_rs = op("scalar", sACT, lambda e: e.activation(srstd[:, 0:4], srstd[:, 0:4],
                                                             AF.Exp, scale=-0.5))
            P.wait("vector", sACT, v_rs)
            P.wait("vector", sPE, st_["ln_tr_done"])
            outs = []
            for a in range(4):
                outs.append(op("vector", sDVE,
                               lambda e, a=a:
                               e.tensor_scalar(sLN[:, a * D:(a + 1) * D],
                                               sX[:, a * D:(a + 1) * D],
                                               sm[:, a:a + 1], srstd[:, a:a + 1],
                                               ALU.add, ALU.mult)))
            st_["lnA_last"] = outs[-1]
            return outs
        pend_rope = []

        def flush_rope():
            while pend_rope:
                kind, pair, kb, t0, v_ev = pend_rope.pop(0)
                P.wait("tensor", sACT, v_ev)
                P.wait("tensor", sDVE, st_["sw_free"])
                v_sw = op("tensor", sPE, lambda e, kb=kb:
                          e.matmul(pSw[:], sP32[:], sKraw[kb][:], start=True, stop=True))
                P.wait("vector", sACT, v_ev)
                P.wait("vector", sPE, v_sw)
                opn("vector", lambda e, kb=kb, t0=t0:
                    e.tensor_tensor(sTa[:], sKraw[kb][:], sCos[:, t0:t0 + 512], ALU.mult))
                if kind == "q":
                    qb = pair % 2
                    P.wait("vector", chQsp[qb], st_["qsp_free"].get(qb, 0))
                    v_tb = op("vector", sDVE, lambda e, qb=qb, t0=t0:
                              e.tensor_tensor(sQsp[qb][:], pSw[:], sSin[:, t0:t0 + 512], ALU.mult))
                    P.wait("vector", sDVE, v_tb)
                    v_rp = op("vector", sDVE, lambda e, qb=qb:
                              e.tensor_tensor(sQsp[qb][:], sTa[:], sQsp[qb][:], ALU.add))
                    P.wait("sync", sDVE, v_rp)
                    st_["qsp_free"][qb] = dma("sync", chQsp[qb],
                                              q_dram.ap()[pair, :, t0:t0 + 512],
                                              sQsp[qb][:])
                else:
                    kcol = pair * S + t0
                    v_tb = op("vector", sDVE, lambda e, kcol=kcol, t0=t0:
                              e.tensor_tensor(sK[:, kcol:kcol + 512], pSw[:],
                                              sSin[:, t0:t0 + 512], ALU.mult))
                    P.wait("vector", sDVE, v_tb)
                    v_rp = op("vector", sDVE, lambda e, kcol=kcol:
                              e.tensor_tensor(sK[:, kcol:kcol + 512], sTa[:],
                                              sK[:, kcol:kcol + 512], ALU.add))
                st_["sw_free"] = v_rp
                st_["kraw_free"][kb] = v_rp
        xload(0, "sync")
        dma("sync", chCos, sCos[:], cos_d.ap())
        v_cos = dma("sync", chCos, sSin[:], sin_d.ap())
        if a_parts & 1:
            ln_done = emit_ln_A(0)
            st_["x_free"] = ln_done[-1]
        else:
            ln_done = []
        xload(1, "gpsimd")
        for tg in range(a_groups):
            t0 = tg * 512
            P.wait("tensor", chCos, v_cos)
            for dc in (range(8) if a_parts & 16 else []):
                bank = pTr[dc % 2]
                P.wait("tensor", sACT, st_["tr_free"].get(dc % 2, 0))
                for a in range(4):
                    P.wait("tensor", sDVE, ln_done[a] if a < len(ln_done) else 0)
                    bld = (lambda e, b=bank, a=a, dc=dc:
                           e.transpose(b[:, a * 128:(a + 1) * 128],
                                       sLN[:, a * D + dc * 128: a * D + (dc + 1) * 128],
                                       sIdF[:]))
                    if a < 3:
                        opn("tensor", bld)
                    else:
                        v_tr = op("tensor", sPE, bld)
                P.wait("scalar", sPE, v_tr)
                P.wait("scalar", sPE, st_["lnT_used"])
                st_["tr_free"][dc % 2] = op("scalar", sACT,
                                            lambda e, b=bank, dc=dc:
                                            e.copy(sLnT[:, dc * 512:(dc + 1) * 512], b[:]))
            st_["ln_tr_done"] = sPE.n
            tr_all = (st_["tr_free"].get(0, 0), st_["tr_free"].get(1, 0))
            if tg + 1 < a_groups and (a_parts & 1):
                ln_next = emit_ln_A(tg + 1)
                st_["x_free"] = ln_next[-1]
                if tg + 2 < a_groups:
                    xload(tg + 2, "gpsimd")
            else:
                ln_next = []
            for c in ((list(range(16, 24)) + list(range(16))) if a_parts & 2 else []):
                bank = pQk[c % 2]
                eng_w, val_w = st_["qkv_free"].get(c % 2, ("ACT", 0))
                P.wait("tensor", sACT if eng_w == "ACT" else sDVE, val_w)
                P.wait("tensor", sACT, max(tr_all))
                for dc in range(8):
                    P.wait("tensor", chWq, v_wq_dc[dc])
                    bld = (lambda e, b=bank, c=c, dc=dc:
                           e.matmul(b[:], sWq[:, dc * 3072 + c * 128: dc * 3072 + (c + 1) * 128],
                                    sLnT[:, dc * 512:(dc + 1) * 512],
                                    start=(dc == 0), stop=(dc == 7)))
                    if dc < 7:
                        opn("tensor", bld)
                    else:
                        v_mm = op("tensor", sPE, bld)
                flush_rope()
                if c < 16 and not (a_parts & 4):
                    st_["qkv_free"][c % 2] = ("ACT", 0)
                elif c < 16:
                    pair = c % 8
                    kb = c % 2
                    P.wait("scalar", sPE, v_mm)
                    P.wait("scalar", sDVE, st_["kraw_free"].get(kb, 0))
                    v_ev = op("scalar", sACT, lambda e, b=bank, kb=kb:
                              e.copy(sKraw[kb][:], b[:]))
                    st_["qkv_free"][c % 2] = ("ACT", v_ev)
                    pend_rope.append(("q" if c < 8 else "k", pair, kb, t0, v_ev))
                elif not (a_parts & 8):
                    st_["qkv_free"][c % 2] = ("ACT", 0)
                else:
                    pair = c - 16
                    vb = c % 2
                    P.wait("scalar", sPE, v_mm)
                    P.wait("scalar", sPE, st_["vraw_free"].get(vb, 0))
                    v_ev = op("scalar", sACT, lambda e, b=bank, vb=vb:
                              e.copy(sVraw[vb][:], b[:]))
                    st_["qkv_free"][c % 2] = ("ACT", v_ev)
                    vtb = pVt[c % 2]
                    P.wait("tensor", sACT, v_ev)
                    P.wait("tensor", sACT, st_["vt_free"].get(c % 2, 0))
                    for i4 in range(4):
                        bld = (lambda e, b=vtb, vb=vb, i4=i4:
                               e.transpose(b[:, i4 * 128:(i4 + 1) * 128],
                                           sVraw[vb][:, i4 * 128:(i4 + 1) * 128],
                                           sIdB[:]))
                        if i4 < 3:
                            opn("tensor", bld)
                        else:
                            v_vt = op("tensor", sPE, bld)
                    st_["vraw_free"][vb] = v_vt
                    P.wait("scalar", sPE, v_vt)
                    P.wait("scalar", chVsp[vb], st_["vtile_free"].get(vb, 0))
                    P.wait("scalar", sPOOL, v_vones)
                    opn("scalar", lambda e, b=vtb, vb=vb:
                        e.copy(bass.AP(sVt[vb], 0, [[520, 128], [130, 4], [1, 64]]),
                               bass.AP(b, 0, [[512, 128], [128, 4], [1, 64]])))
                    v_ev2 = op("scalar", sACT, lambda e, b=vtb, vb=vb:
                               e.copy(bass.AP(sVt[vb], 65, [[520, 128], [130, 4], [1, 64]]),
                                      bass.AP(b, 64, [[512, 128], [128, 4], [1, 64]])))
                    st_["vt_free"][c % 2] = v_ev2
                    P.wait("sync", sACT, v_ev2)
                    st_["vtile_free"][vb] = dma("sync", chVsp[vb],
                                                v_dram.ap()[pair, :, tg * 520:(tg + 1) * 520],
                                                sVt[vb][:])
            st_["lnT_used"] = sPE.n
            flush_rope()
            ln_done = ln_next

        if "dbg_k" in dbg_d:
            P.wait("sync", sDVE, sDVE.n)
            dma("sync", chDbg, dbg_d["dbg_k"].ap(), sK[:])
        if "dbg_ln" in dbg_d:
            P.wait("sync", sDVE, sDVE.n)
            dma("sync", chDbg, dbg_d["dbg_ln"].ap(), sLN[:])
        if "dbg_lnt" in dbg_d:
            P.wait("sync", sACT, sACT.n)
            dma("sync", chDbg, dbg_d["dbg_lnt"].ap(), sLnT[:])
        P.barrier()
        stPA.close()
        stA.close()

    stO = ExitStack()
    sO = stO.enter_context(nc.sbuf_tensor("sO", [128, NP * S], BF16, side="right"))

    if phase_limit >= 2:
        stB = ExitStack()
        B_ = lambda n, s, d: stB.enter_context(nc.sbuf_tensor(n, s, d, side="left"))
        sV = [B_(f"sV{i}", [128, 4160], BF16) for i in range(2)]
        sQt = [B_(f"sQt{i}", [128, 512], BF16) for i in range(2)]
        sPP = [B_(f"sPP{i}", [128, 1024], BF16) for i in range(4)]
        sOev0 = B_("sOev0", [64, 512], BF16)
        sOev1 = B_("sOev1", [64, 512], BF16)
        sRecB0 = B_("sRecB0", [1, 512], BF16)
        sRecB1 = B_("sRecB1", [1, 512], BF16)
        sMask = B_("sMask", [128, 4096], BF16)
        v_mask = dma("sync", chC, sMask[:], maskq_d.ap())
        P.wait("gpsimd", chC, v_mask)
        stPB = ExitStack()
        pSc = [stPB.enter_context(nc.psum_tensor(f"pSc{i}", [128, 1024], F32)) for i in range(3)]
        pO = [stPB.enter_context(nc.psum_tensor(f"pO{i}", [65, 512], F32)) for i in range(2)]

        def _recip(e, out_ap, in_ap):
            with nc.allow_low_precision(reason="softmax 1/l to bf16 operand"):
                return e.reciprocal(out_ap, in_ap)

        bst = dict(q_free={}, v_free={}, pp_free={}, sc_free={},
                   o_clean=0, last_nrm=0, last_bc=0)
        v_vdma = {0: dma("sync", chV[0], sV[0][:], v_dram.ap()[0])}
        v_qdma = {(0, 0): dma("sync", chQ[0], sQt[0][:], q_dram.ap()[0, :, 0:512])}
        tails = {}

        def emit_tail_head(i, pr, q0, v_last):
            P.wait("scalar", sPE, v_last)
            P.wait("scalar", sDVE, bst["last_nrm"])
            opn("scalar", lambda e: e.copy(sOev0[:], pO[0][0:64, :]))
            v_oc = op("scalar", sACT,
                      lambda e: e.copy(sOev1[:], pO[1][0:64, :]))
            P.wait("vector", sPE, v_last)
            P.wait("vector", sPE, bst["last_bc"])
            opn("vector", lambda e: _recip(e, sRecB0[:], pO[0][64:65, :]))
            v_rec = op("vector", sDVE,
                       lambda e: _recip(e, sRecB1[:], pO[1][64:65, :]))
            tails[i] = (pr, q0, v_oc, v_rec)

        def emit_tail_body(i):
            pr, q0, v_oc, v_rec = tails.pop(i)
            P.wait("tensor", sDVE, v_rec)
            P.wait("tensor", sACT, v_oc)
            opn("tensor", lambda e:
                e.matmul(pO[0][0:64, :], sOne[:, 0:64], sRecB0[:],
                         start=True, stop=True))
            v_bc = op("tensor", sPE, lambda e:
                      e.matmul(pO[1][0:64, :], sOne[:, 0:64], sRecB1[:],
                               start=True, stop=True))
            bst["last_bc"] = v_bc
            P.wait("vector", sPE, v_bc)
            ocol = pr * S + q0
            opn("vector", lambda e, ocol=ocol:
                e.tensor_tensor(sO[0:64, ocol:ocol + 512], sOev0[:],
                                pO[0][0:64, :], ALU.mult))
            v_nrm = op("vector", sDVE, lambda e, ocol=ocol:
                       e.tensor_tensor(sO[64:128, ocol:ocol + 512], sOev1[:],
                                       pO[1][0:64, :], ALU.mult))
            bst["o_clean"] = v_nrm
            bst["last_nrm"] = v_nrm

        it = 0
        for pr in range(NP):
            if pr + 1 < NP:
                P.wait("sync", sPE, bst["v_free"].get((pr + 1) % 2, 0))
                v_vdma[pr + 1] = dma("sync", chV[(pr + 1) % 2], sV[(pr + 1) % 2][:],
                                     v_dram.ap()[pr + 1])
            vb = pr % 2
            for qt in range(8):
                q0 = qt * 512
                qb = qt % 2
                nk = 4 * qt + 4
                nxt = (pr, qt + 1) if qt < 7 else ((pr + 1, 0) if pr + 1 < NP else None)
                if nxt is not None:
                    P.wait("sync", sPE, bst["q_free"].get((qt + 1) % 2, 0))
                    v_qdma[nxt] = dma("sync", chQ[(qt + 1) % 2], sQt[(qt + 1) % 2][:],
                                      q_dram.ap()[nxt[0], :, nxt[1] * 512:nxt[1] * 512 + 512])
                P.wait("tensor", chQ[qb], v_qdma[(pr, qt)])
                P.wait("tensor", chV[vb], v_vdma[pr])
                exp_done = {}
                mask_done = {}
                vmm_done = {}

                def emit_vmm(ktv):
                    kbv = ktv % 4
                    if ktv in mask_done:
                        P.wait("tensor", sDVE, mask_done[ktv])
                    else:
                        P.wait("tensor", sACT, exp_done[ktv])
                    if ktv == 0:
                        P.wait("tensor", sDVE, bst["o_clean"])
                    opn("tensor", lambda e, ktv=ktv, kbv=kbv, vb=vb, nk=nk:
                        e.matmul(pO[0][:], sV[vb][:, ktv * 130: ktv * 130 + 65],
                                 sPP[kbv][:, 0:512], start=(ktv == 0), stop=(ktv == nk - 1)))
                    v = op("tensor", sPE, lambda e, ktv=ktv, kbv=kbv, vb=vb, nk=nk:
                           e.matmul(pO[1][:], sV[vb][:, ktv * 130 + 65: ktv * 130 + 130],
                                    sPP[kbv][:, 512:1024], start=(ktv == 0), stop=(ktv == nk - 1)))
                    vmm_done[ktv] = v
                    bst["pp_free"][kbv] = v

                for kt in range(nk):
                    kb = kt % 3
                    kp = kt % 4
                    P.wait("tensor", sACT, bst["sc_free"].get(kb, 0))
                    opn("tensor", lambda e, pr=pr, kt=kt, kb=kb, qb=qb:
                        e.matmul(pSc[kb][:, 0:512],
                                 sK[0:64, pr * S + kt * 128: pr * S + (kt + 1) * 128],
                                 sQt[qb][0:64, :], start=True, stop=True,
                                 tile_position=(0, 0)))
                    v_s = op("tensor", sPE, lambda e, pr=pr, kt=kt, kb=kb, qb=qb:
                             e.matmul(pSc[kb][:, 512:1024],
                                      sK[64:128, pr * S + kt * 128: pr * S + (kt + 1) * 128],
                                      sQt[qb][64:128, :], start=True, stop=True,
                                      tile_position=(64, 0)))
                    if kt == 1 and it > 0:
                        emit_tail_body(it - 1)
                    P.wait("scalar", sPE, v_s)
                    P.wait("scalar", sPE, bst["pp_free"].get(kp, 0))
                    v_e = op("scalar", sACT, lambda e, kb=kb, kp=kp:
                             e.activation(sPP[kp][:], pSc[kb][:], AF.Exp,
                                          bias=CSHIFT, scale=0.125))
                    exp_done[kt] = v_e
                    bst["sc_free"][kb] = v_e
                    if kt >= 4 * qt:
                        m = kt - 4 * qt
                        P.wait("vector", sACT, v_e)
                        mask_done[kt] = op("vector", sDVE,
                                           lambda e, kp=kp, m=m:
                                           e.tensor_tensor(sPP[kp][:], sPP[kp][:],
                                                           sMask[:, m * 1024:(m + 1) * 1024],
                                                           ALU.mult))
                    if kt >= 2:
                        emit_vmm(kt - 2)
                    if kt == nk - 1:
                        emit_vmm(kt - 1)
                        emit_vmm(kt)

                v_last = vmm_done[nk - 1]
                emit_tail_head(it, pr, q0, v_last)
                bst["q_free"][qb] = v_last
                bst["v_free"][vb] = v_last
                it += 1
        emit_tail_body(it - 1)

        if "dbg_o" in dbg_d:
            P.wait("sync", sDVE, sDVE.n)
            dma("sync", chDbg, dbg_d["dbg_o"].ap(), sO[:])
        P.barrier()
        stPB.close()
        stB.close()
    stK.close()

    cmp = {}
    if phase_limit >= 4:
        stCm = ExitStack()
        M_ = lambda n, s, d: stCm.enter_context(nc.sbuf_tensor(n, s, d, side="left"))
        sMW = [M_(f"sMW{i}", [128, 32 * 128], BF16) for i in range(2)]
        sScW = M_("sScW", [128, 32 * 128], BF16)
        sIn = [M_(f"sIn{i}", [128, DP], BF16) for i in range(2)]
        sStg = [M_(f"sStg{i}", [128, DP], BF16) for i in range(2)]
        sBg2 = M_("sBg2", [128, 32], BF16)
        sT3 = M_("sT3", [128, 2 * 9 * 128], BF16)
        stPZ = ExitStack()
        pF = stPZ.enter_context(nc.psum_tensor("pFz", [128, 1024], F32))
        pB3 = [stPZ.enter_context(nc.psum_tensor(f"pB3z{i}", [128, 512], BF16)) for i in range(2)]
        mstc = dict(mw_gate={}, scw_busy={}, v_ld={}, v0=0, pf_free={},
                    we_free={}, in_free={}, t3p_free={}, stg_free={},
                    t3stg={}, bankctr=0)

        def _scat_load(src_d, nsteps, step, c):
            row = 32 * nsteps * 128
            P.wait("gpsimd", sPE, mstc["scw_busy"].get(c, 0))
            mstc["v_ld"][(step, c)] = dma(
                "gpsimd", chScat[c % 2],
                sScW.ap().rearrange("p (b k) -> p b k", b=32)[:, c * 8:(c + 1) * 8, :],
                bass.AP(src_d, step * 128 + c * 8 * nsteps * 128,
                        [[row, 128], [nsteps * 128, 8], [1, 128]]))

        def _chain_pre(src_d, nsteps, m0):
            row = 32 * nsteps * 128
            P.wait("gpsimd", sPE, mstc["mw_gate"].get(m0, 0))
            mstc["v0"] = dma("gpsimd", chScat[0],
                             sMW[m0].ap().rearrange("p (b c) -> p b c", b=32),
                             bass.AP(src_d, 0, [[row, 128], [nsteps * 128, 32], [1, 128]]))
            for c in range(4):
                _scat_load(src_d, nsteps, 1, c)

        def _chain_run(src_d, nsteps, m0):
            v0 = mstc["v0"]
            evprev = {}
            for step in range(1, nsteps):
                cur = sMW[(m0 + step - 1) % 2]
                out = sMW[(m0 + step) % 2]
                for chunk in range(4):
                    P.wait("tensor", chScat[0], v0)
                    P.wait("tensor", chScat[chunk % 2], mstc["v_ld"][(step, chunk)])
                    for ep in evprev.get(chunk, ()):
                        P.wait("tensor", ep[1], ep[2])
                    evs = []
                    for half in range(2):
                        eng0, sem0, val0 = mstc["pf_free"].get(half, ("vector", sDVE, 0))
                        P.wait("tensor", sem0, val0)
                        for j in range(4):
                            b = chunk * 8 + half * 4 + j
                            bld = (lambda e, half=half, j=j, b=b, cur=cur:
                                   e.matmul(pF[:, half * 512 + j * 128: half * 512 + (j + 1) * 128],
                                            sScW[:, b * 128:(b + 1) * 128],
                                            cur[:, b * 128:(b + 1) * 128],
                                            start=True, stop=True))
                            if j < 3:
                                opn("tensor", bld)
                            else:
                                v_bm = op("tensor", sPE, bld)
                        eng = "vector" if half == 0 else "scalar"
                        sem_ = sDVE if half == 0 else sACT
                        P.wait(eng, sPE, v_bm)
                        if eng == "vector":
                            v_ev = op(eng, sem_, lambda e, chunk=chunk, half=half, out=out:
                                      e.tensor_copy(out[:, chunk * 1024 + half * 512:
                                                        chunk * 1024 + (half + 1) * 512],
                                                    pF[:, half * 512:(half + 1) * 512]))
                        else:
                            v_ev = op(eng, sem_, lambda e, chunk=chunk, half=half, out=out:
                                      e.copy(out[:, chunk * 1024 + half * 512:
                                                 chunk * 1024 + (half + 1) * 512],
                                             pF[:, half * 512:(half + 1) * 512]))
                        mstc["pf_free"][half] = (eng, sem_, v_ev)
                        evs.append((eng, sem_, v_ev))
                    evprev[chunk] = evs
                    mstc["scw_busy"][chunk] = sPE.n
                    if step + 1 < nsteps:
                        _scat_load(src_d, nsteps, step + 1, chunk)
                    yield
            mstc["mw_gate"][(m0 + nsteps - 2) % 2] = sPE.n
            cmp["final"] = sMW[(m0 + nsteps - 1) % 2]

        def _apply_one(final, blk, sb, nsl, v_rhs, chrhs):
            for sem, val in mstc["stg_free"].get(sb, ()):
                P.wait("scalar", sem, val)
            banks = [(pF, 0), (pF, 512)] + cmp.get("xbanks", [])
            evl = []
            for k, (c0, w) in enumerate(nsl):
                bi = mstc["bankctr"] % len(banks)
                mstc["bankctr"] += 1
                bt, cb = banks[bi]
                P.wait("tensor", chrhs, v_rhs)
                eng_a, sem_a, val_a = mstc["pf_free"].get(bi, ("scalar", sACT, 0))
                P.wait("tensor", sem_a, val_a)
                v_a = op("tensor", sPE, lambda e, bt=bt, cb=cb, w=w, blk=blk, c0=c0,
                         final=final, sb=sb:
                         e.matmul(bt[:, cb:cb + w], final[:, blk * 128:(blk + 1) * 128],
                                  sIn[sb][:, c0:c0 + w], start=True, stop=True))
                P.wait("scalar", sPE, v_a)
                v_e = op("scalar", sACT, lambda e, bt=bt, cb=cb, sb=sb, c0=c0, w=w:
                         e.copy(sStg[sb][:, c0:c0 + w], bt[:, cb:cb + w]))
                mstc["pf_free"][bi] = ("scalar", sACT, v_e)
                evl.append(v_e)
            return evl

        def compose_gen():
            _chain_pre(upA_d, 5, 0)
            yield from _chain_run(upA_d, 5, 0)
            finalA = cmp["final"]

            def pre_upA(u):
                sb = u % 2
                P.wait("gpsimd", sPE, mstc["we_free"].get(sb, 0))
                return dma("gpsimd", chWEc[sb], sIn[sb][:],
                           wexpTb_d.ap().rearrange("(u p) c -> u p c", p=128)[u])

            v_rhs = {0: pre_upA(0)}
            for u in range(32):
                sb = u % 2
                if u + 1 < 32:
                    v_rhs[u + 1] = pre_upA(u + 1)
                evl = _apply_one(finalA, u, sb, ((0, 512), (512, 512), (1024, 128)),
                                 v_rhs[u], chWEc[sb])
                P.wait("gpsimd", sACT, evl[-1])
                v_st = dma("gpsimd", chStg[sb],
                           w1_dram.ap().rearrange("(u p) c -> u p c", p=128)[u],
                           sStg[sb][:])
                mstc["stg_free"][sb] = [(chStg[sb], v_st)]
                mstc["we_free"][sb] = sPE.n
                if u == 4:
                    _chain_pre(upB_d, 7, 1)
                yield
            mstc["mw_gate"][0] = sPE.n
            yield from _chain_run(upB_d, 7, 1)
            finalB = cmp["final"]
            for chs in chStg:
                P.wait("gpsimd", chs, chs.n)

            def pre_upB(g):
                sb = g % 2
                P.wait("gpsimd", sPE, mstc["in_free"].get(sb, 0))
                return dma("gpsimd", chIn[sb], sIn[sb][:],
                           bass.AP(w1_dram, g * DP, [[32 * DP, 128], [1, DP]]))

            v_rhs = {0: pre_upB(0)}
            for g in range(32):
                sb = g % 2
                if g + 1 < 32:
                    v_rhs[g + 1] = pre_upB(g + 1)
                evl = _apply_one(finalB, g, sb, ((0, 512), (512, 512), (1024, 128)),
                                 v_rhs[g], chIn[sb])
                mstc["in_free"][sb] = sPE.n
                yield
                P.wait("vector", sACT, evl[2])
                v_bg = op("vector", sDVE, lambda e, sb=sb, g=g:
                          e.tensor_copy(sBg2[:, g:g + 1], sStg[sb][:, 1024:1025]))
                for bi, (dcs, ve) in enumerate((((0, 1, 2, 3), evl[0]),
                                                ((4, 5, 6, 7), evl[1]),
                                                ((8,), evl[2]))):
                    bank = pB3[bi % 2]
                    engt, semt, valt = mstc["t3p_free"].get(bi % 2, ("vector", sDVE, 0))
                    P.wait("tensor", semt, valt)
                    P.wait("tensor", sACT, ve)
                    for kk, dc in enumerate(dcs):
                        bld = (lambda e, bank=bank, kk=kk, dc=dc, sb=sb:
                               e.transpose(bank[:, kk * 128:(kk + 1) * 128],
                                           sStg[sb][:, dc * 128:(dc + 1) * 128],
                                           sIdB[:]))
                        if kk < len(dcs) - 1:
                            opn("tensor", bld)
                        else:
                            v_t = op("tensor", sPE, bld)
                    P.wait("vector", sPE, v_t)
                    if bi == 0:
                        P.wait("vector", chT3, mstc["t3stg"].get(g % 2, 0))
                    nb = len(dcs)
                    dc0 = dcs[0]
                    base = (g % 2) * 9 * 128
                    v_t3 = op("vector", sDVE,
                              lambda e, bank=bank, nb=nb, dc0=dc0, base=base:
                              e.tensor_copy(sT3[:, base + dc0 * 128: base + (dc0 + nb) * 128],
                                            bank[:, 0:nb * 128]))
                    mstc["t3p_free"][bi % 2] = ("vector", sDVE, v_t3)
                P.wait("gpsimd", sDVE, v_t3)
                mstc["t3stg"][g % 2] = dma(
                    "gpsimd", chT3,
                    bass.AP(w1f_dram, g * 128, [[HID, 128], [128 * HID, 9], [1, 128]]),
                    bass.AP(sT3, (g % 2) * 9 * 128, [[2 * 9 * 128, 128], [128, 9], [1, 128]]))
                mstc["stg_free"][sb] = [(sPE, sPE.n), (sDVE, v_bg)]
                if g == 4:
                    _chain_pre(dnB_d, 7, 0)
                yield
            mstc["mw_gate"][1] = sPE.n
            cmp["w1f_done"] = [(chT3, chT3.n)]
            P.wait("vector", sDVE, sDVE.n)
            cmp["v_bh"] = op("vector", sDVE, lambda e:
                             e.tensor_tensor(sBiasH[:], sBg2[:], sUpb[:], ALU.add))
            yield
            yield from _chain_run(dnB_d, 7, 0)
            finalDB = cmp["final"]

            def pre_dnB(g):
                sb = g % 2
                P.wait("gpsimd", sPE, mstc["we_free"].get(sb, 0))
                return dma("gpsimd", chWEc[sb], sIn[sb][:, 0:D],
                           wconG_d.ap().rearrange("(g p) c -> g p c", p=128)[g])

            v_rhs = {0: pre_dnB(0)}
            for g in range(32):
                sb = g % 2
                if g + 1 < 32:
                    v_rhs[g + 1] = pre_dnB(g + 1)
                evl = _apply_one(finalDB, g, sb, ((0, 512), (512, 512)),
                                 v_rhs[g], chWEc[sb])
                P.wait("gpsimd", sACT, evl[-1])
                v_st = dma("gpsimd", chStg[sb],
                           r_dram.ap().rearrange("(g p) c -> g p c", p=128)[g],
                           sStg[sb][:, 0:D])
                mstc["stg_free"][sb] = [(chStg[sb], v_st)]
                mstc["we_free"][sb] = sPE.n
                if g == 4:
                    _chain_pre(dnA_d, 5, 1)
                yield
            mstc["mw_gate"][0] = sPE.n
            yield from _chain_run(dnA_d, 5, 1)
            finalDA = cmp["final"]
            for chs in chStg:
                P.wait("gpsimd", chs, chs.n)

            def pre_dnA(u):
                sb = u % 2
                P.wait("gpsimd", sPE, mstc["in_free"].get(sb, 0))
                for a4 in range(4):
                    v = dma("gpsimd", chIn[sb],
                            sIn[sb][a4 * 32:(a4 + 1) * 32, 0:D],
                            bass.AP(r_dram, (u * 4 + a4) * D, [[128 * D, 32], [1, D]]))
                return v

            v_rhs = {0: pre_dnA(0)}
            for u in range(32):
                sb = u % 2
                if u + 1 < 32:
                    v_rhs[u + 1] = pre_dnA(u + 1)
                evl = _apply_one(finalDA, u, sb, ((0, 512), (512, 512)),
                                 v_rhs[u], chIn[sb])
                P.wait("gpsimd", sACT, evl[-1])
                v_st = dma("gpsimd", chStg[sb],
                           r2_dram.ap().rearrange("(u p) c -> u p c", p=128)[u],
                           sStg[sb][:, 0:D])
                mstc["stg_free"][sb] = [(chStg[sb], v_st)]
                mstc["in_free"][sb] = sPE.n
                if cmp.get("sr_load"):
                    cmp["sr_load"](u, v_st, chStg[sb])
                yield
            mstc["mw_gate"][1] = sPE.n

        cmp["gen"] = compose_gen()
        cmp["seed_pf"] = lambda d: mstc["pf_free"].update(
            {k: v for k, v in d.items()})

        def drive(n):
            g = cmp.get("gen")
            if g is None:
                return
            for _ in range(n):
                try:
                    next(g)
                except StopIteration:
                    cmp["gen"] = None
                    return
        cmp["drive"] = drive
    else:
        cmp["drive"] = lambda n: None

    stO.close()
    if phase_limit >= 5:
        sW1F = st.enter_context(nc.sbuf_tensor("sW1F", [128, 8 * HID], BF16, side="right"))

        def emit_w1f_load(pe_val):
            P.wait("sync", sPE, pe_val)
            for sem_, val_ in cmp["w1f_done"]:
                P.wait("sync", sem_, val_)
            chans = []
            for q in range(4):
                v = dma("sync", chWF[q],
                        sW1F.ap().rearrange("p (dc c) -> p dc c", dc=8)[:, q * 2:(q + 1) * 2, :],
                        w1f_dram.ap().rearrange("(dc p) c -> p dc c", p=128)[:, q * 2:(q + 1) * 2, :])
                chans.append((chWF[q], v))
            cmp["w1f_chans"] = chans

    if phase_limit >= 3:
        stC = ExitStack()
        C_ = lambda n, s, d: stC.enter_context(nc.sbuf_tensor(n, s, d, side="left"))
        sWo = C_("sWo", [128, 8 * 1024], BF16)
        sYT = C_("sYT", [128, 8 * 512], F32)
        sY = C_("sY", [128, 4 * D], F32)
        sX1 = C_("sX1", [128, 4 * D], F32)
        sLn2 = C_("sLn2", [128, 4 * D], F32)
        sL2Tg = C_("sL2Tg", [128, 8 * 512], BF16)
        scrC = tuple(C_(f"scrC{i}", [128, 8], F32) for i in range(4))
        stPC = ExitStack()
        pY = [stPC.enter_context(nc.psum_tensor(f"pYc{i}", [128, 512], F32)) for i in range(2)]
        pT = [stPC.enter_context(nc.psum_tensor(f"pTc{i}", [128, 512], F32)) for i in range(2)]

        v_wo = dma("gpsimd", chWo, sWo.ap().rearrange("p (c n) -> p c n", c=8),
                   woT_d.ap().rearrange("(c p) n -> p c n", p=128))
        x_r = x_d.ap().rearrange("(g a p) d -> g p a d", a=4, p=128)
        cst = dict(ymm_free={}, yt_used=0, pt_free=0, sy_ev=0,
                   x1add={}, ln2=None, ln2_last=0, ln3_last=0, tr2_pe=0,
                   x1sp_free=0, l2t_free=0, v_x1={})
        drv = cmp["drive"]

        def emit_ymm(tg):
            t0 = tg * 512
            for oc in range(8):
                bank = pY[oc % 2]
                P.wait("tensor", sACT, cst["ymm_free"].get(oc % 2, 0))
                P.wait("tensor", chWo, v_wo)
                for pc in range(8):
                    bld = (lambda e, b=bank, oc=oc, pc=pc, t0=t0:
                           e.matmul(b[:], sWo[:, pc * 1024 + oc * 128: pc * 1024 + (oc + 1) * 128],
                                    sO[:, pc * S + t0: pc * S + t0 + 512],
                                    start=(pc == 0), stop=(pc == 7)))
                    if pc < 7:
                        opn("tensor", bld)
                    else:
                        v_mm = op("tensor", sPE, bld)
                P.wait("scalar", sPE, v_mm)
                P.wait("scalar", sPE, cst["yt_used"])
                cst["ymm_free"][oc % 2] = op("scalar", sACT,
                                             lambda e, b=bank, oc=oc:
                                             e.copy(sYT[:, oc * 512:(oc + 1) * 512], b[:]))
            cst["yt_ev"] = sACT.n

        def emit_ytr(tg):
            v_yt_ev = cst["yt_ev"]
            for a in range(4):
                P.wait("tensor", sACT, v_yt_ev)
                P.wait("tensor", sACT, cst["pt_free"])
                for oc in range(8):
                    bb = pT[0] if oc < 4 else pT[1]
                    bld = (lambda e, bb=bb, oc=oc, a=a:
                           e.transpose(bb[:, (oc % 4) * 128:(oc % 4 + 1) * 128],
                                       sYT[:, oc * 512 + a * 128: oc * 512 + (a + 1) * 128],
                                       sIdF[:]))
                    if oc < 7:
                        opn("tensor", bld)
                    else:
                        v_tr = op("tensor", sPE, bld)
                P.wait("scalar", sPE, v_tr)
                P.wait("scalar", sDVE, cst["ln2_last"])
                opn("scalar", lambda e, a=a: e.copy(sY[:, a * D: a * D + 512], pT[0][:]))
                cst["pt_free"] = op("scalar", sACT,
                                    lambda e, a=a:
                                    e.copy(sY[:, a * D + 512: (a + 1) * D], pT[1][:]))
            cst["yt_used"] = sPE.n
            cst["sy_ev"] = sACT.n

        def emit_lnsec(tg):
            P.wait("vector", sACT, cst["sy_ev"])
            P.wait("vector", sPE, cst["tr2_pe"])
            ln3 = emit_ln_wide(sY[:], [sY[:, a * D:(a + 1) * D] for a in range(4)],
                               sLn2[:], scrC, 4)
            cst["ln3_last"] = ln3[-1]
            P.wait("vector", chX, cst["v_x1"][tg])
            P.wait("vector", sDVE, ln3[-1])
            v_add = 0
            for a in range(4):
                v_add = op("vector", sDVE, lambda e, a=a:
                           e.tensor_tensor(sX1[:, a * D:(a + 1) * D],
                                           sX1[:, a * D:(a + 1) * D],
                                           sY[:, a * D:(a + 1) * D], ALU.add))
            cst["x1add"][tg] = v_add
            P.wait("sync", sDVE, v_add)
            for a in range(4):
                v_sp = dma("sync", chX1,
                           x1_dram.ap().rearrange("(n p) d -> n p d", p=128)[tg * 4 + a],
                           sX1[:, a * D:(a + 1) * D])
            cst["x1sp_free"] = v_sp
            cst["ln2"] = emit_ln_wide(sX1[:], [sLn2[:, a * D:(a + 1) * D] for a in range(4)],
                                      sY[:], scrC, 4, prev=v_add)
            cst["ln2_last"] = cst["ln2"][-1]

        def emit_tr2(tg):
            t0 = tg * 512
            for a in range(4):
                P.wait("tensor", sDVE, cst["ln2"][a])
                P.wait("tensor", sACT, cst["pt_free"])
                for dc in range(8):
                    bb = pT[0] if dc < 4 else pT[1]
                    bld = (lambda e, bb=bb, dc=dc, a=a:
                           e.transpose(bb[:, (dc % 4) * 128:(dc % 4 + 1) * 128],
                                       sLn2[:, a * D + dc * 128: a * D + (dc + 1) * 128],
                                       sIdF[:]))
                    if dc < 7:
                        opn("tensor", bld)
                    else:
                        v_tr2 = op("tensor", sPE, bld)
                P.wait("scalar", sPE, v_tr2)
                P.wait("scalar", chL2, cst["l2t_free"])
                opn("scalar", lambda e, a=a:
                    e.copy(bass.AP(sL2Tg, a * 128, [[4096, 128], [512, 4], [1, 128]]),
                           bass.AP(pT[0], 0, [[512, 128], [128, 4], [1, 128]])))
                cst["pt_free"] = op("scalar", sACT, lambda e, a=a:
                                    e.copy(bass.AP(sL2Tg, 2048 + a * 128,
                                                   [[4096, 128], [512, 4], [1, 128]]),
                                           bass.AP(pT[1], 0, [[512, 128], [128, 4], [1, 128]])))
            cst["tr2_pe"] = sPE.n
            v_l2ev = sACT.n
            P.wait("sync", sACT, v_l2ev)
            l2r = ln2T_dram.ap().rearrange("(dc p) t -> dc p t", p=128)
            for dc in range(8):
                v = dma("sync", chL2, l2r[dc][:, t0:t0 + 512],
                        sL2Tg[:, dc * 512:(dc + 1) * 512])
            cst["l2t_free"] = v

        def emit_xload(tg):
            P.wait("sync", chX1, cst["x1sp_free"])
            P.wait("sync", sDVE, cst["ln2_last"])
            cst["v_x1"][tg] = dma("sync", chX,
                                  sX1.ap().rearrange("p (a d) -> p a d", a=4), x_r[tg])

        for tg in range(8):
            emit_ymm(tg)
            if tg > 0:
                emit_lnsec(tg - 1)
            emit_xload(tg)
            emit_ytr(tg)
            if tg > 0:
                emit_tr2(tg - 1)
        emit_lnsec(7)
        drv(16)
        emit_tr2(7)
        if phase_limit >= 4:
            cmp["xbanks"] = [(pY[0], 0), (pY[1], 0)]
            gate = ("scalar", sACT, sACT.n)
            cmp["seed_pf"]({2: gate, 3: gate})
        stC.close()
        if phase_limit >= 5:
            sR_ = st.enter_context(nc.sbuf_tensor("sR", [128, 32 * D], BF16, side="right"))
            cmp["sR"] = sR_
            rl_done = {}

            def _sr_load(u, v_st, chs):
                if (u + 1) % 8 != 0:
                    return
                q = u // 8
                for chs_ in chStg:
                    P.wait("sync", chs_, chs_.n)
                rl_done[q] = dma("sync", chRL[q],
                                 sR_[32 * q:32 * (q + 1), :].rearrange(
                                     "p (g c) -> p g c", c=D),
                                 bass.AP(r2_dram, q * 32 * 32 * D,
                                         [[32 * D, 32], [D, 32], [1, D]]))
            cmp["sr_load"] = _sr_load
            cmp["rl_done"] = rl_done
        while cmp.get("gen") is not None:
            drv(1)
            if phase_limit >= 5 and cmp.get("w1f_done") and not cmp.get("w1f_chans"):
                emit_w1f_load(sPE.n)

        if "dbg_x1" in dbg_d:
            P.wait("sync", chX1, chX1.n)
            dma("sync", chDbg, dbg_d["dbg_x1"].ap(), x1_dram.ap())
        if "dbg_l2t" in dbg_d:
            P.wait("sync", chL2, chL2.n)
            dma("sync", chDbg, dbg_d["dbg_l2t"].ap(), ln2T_dram.ap())
        if "dbg_bh" in dbg_d and phase_limit >= 4:
            P.wait("sync", sDVE, sDVE.n)
            dma("sync", chDbg, dbg_d["dbg_bh"].ap(), sBiasH[:])
        P.barrier()
        stPC.close()
    elif phase_limit >= 4:
        while cmp.get("gen") is not None:
            cmp["drive"](50)
        P.barrier()
    if phase_limit >= 4:
        stPZ.close()
        stCm.close()

    if phase_limit >= 5:
        sR = cmp["sR"]
        if cmp.get("w1f_chans") is None:
            emit_w1f_load(sPE.n)
        rl_chans = [(chRL[q], cmp["rl_done"][q]) for q in range(4)]
        for ch_, v_ in cmp["w1f_chans"]:
            P.wait("tensor", ch_, v_)
        stD = ExitStack()
        D_ = lambda n, s, d: stD.enter_context(nc.sbuf_tensor(n, s, d, side="left"))
        sL2T = [D_(f"sL2T{i}", [128, 8 * 256], BF16) for i in range(2)]
        sX1g = [D_(f"sX1g{i}", [128, 2 * D], F32) for i in range(2)]
        sG = D_("sG", [128, 32 * 256], BF16)
        sBY = D_("sBY", [128, D], F32)
        sYF = D_("sYF", [128, D], F32)
        sLn4 = D_("sLn4", [128, D], F32)
        sOut = [D_(f"sOut{i}", [128, D], F32) for i in range(2)]
        sSqD = D_("sSqD", [128, D], F32)
        scrD = tuple(D_(f"scrD{i}", [128, 8], F32) for i in range(4))
        stPD = ExitStack()
        pH = [stPD.enter_context(nc.psum_tensor(f"pH{i}", [128, 256], F32)) for i in range(2)]
        pYd = [stPD.enter_context(nc.psum_tensor(f"pYd{i}", [128, 512], F32)) for i in range(2)]

        v_b1 = dma("sync", chBY, sBY[:], bcon_d.ap())
        v_b2 = dma("sync", chBY, sYF[:], dnb_d.ap())
        P.wait("vector", chBY, v_b2)
        v_by = op("vector", sDVE, lambda e: e.tensor_tensor(sBY[:], sBY[:], sYF[:], ALU.add))

        dst = dict(h_free={}, g_free=0, y_free={}, l2t_free={}, x1g_free={},
                   out_free={}, yf_free=0, ln4_free=0)
        v_l2 = {}
        v_x1g = {}

        def dma_group(g):
            gb = g % 2
            P.wait("sync", sPE, dst["l2t_free"].get(gb, 0))
            v_l2[g] = dma("sync", chL2D[gb],
                          sL2T[gb].ap().rearrange("p (dc c) -> p dc c", dc=8),
                          bass.AP(ln2T_dram, g * 256, [[S, 128], [128 * S, 8], [1, 256]]))
            P.wait("sync", sDVE, dst["x1g_free"].get(gb, 0))
            v_x1g[g] = dma("sync", chX1D[gb],
                           sX1g[gb].ap().rearrange("p (a d) -> p a d", a=2),
                           x1_dram.ap().rearrange("(n a p) d -> n p a d", a=2, p=128)[g])

        dma_group(0)
        for g in range(16):
            gb = g % 2
            if g + 1 < 16:
                dma_group(g + 1)
            P.wait("tensor", chL2D[gb], v_l2[g])
            for u in range(32):
                ub = u % 2
                P.wait("tensor", sACT, dst["h_free"].get(ub, 0))
                for dc in range(8):
                    bld = (lambda e, ub=ub, u=u, dc=dc, gb=gb:
                           e.matmul(pH[ub][:], sW1F[:, dc * HID + u * 128: dc * HID + (u + 1) * 128],
                                    sL2T[gb][:, dc * 256:(dc + 1) * 256],
                                    start=(dc == 0), stop=(dc == 7)))
                    if dc < 7:
                        opn("tensor", bld)
                    else:
                        v_h = op("tensor", sPE, bld)
                P.wait("scalar", sPE, v_h)
                P.wait("scalar", sPE, dst["g_free"])
                P.wait("scalar", sDVE, cmp.get("v_bh", 0))
                dst["h_free"][ub] = op("scalar", sACT,
                                       lambda e, ub=ub, u=u:
                                       e.activation(sG[:, u * 256:(u + 1) * 256], pH[ub][:],
                                                    AF.Gelu, bias=sBiasH[:, u:u + 1]))
            v_gelu = sACT.n
            for a in range(2):
                for ch_, v_ in rl_chans:
                    P.wait("tensor", ch_, v_)
                P.wait("tensor", sACT, v_gelu)
                P.wait("tensor", sDVE, dst["y_free"].get(0, 0))
                P.wait("tensor", sDVE, dst["y_free"].get(1, 0))
                for u in range(32):
                    for oh in range(2):
                        bld = (lambda e, a=a, u=u, oh=oh:
                               e.matmul(pYd[oh][:], sG[:, u * 256 + a * 128: u * 256 + (a + 1) * 128],
                                        sR[:, u * D + oh * 512: u * D + (oh + 1) * 512],
                                        start=(u == 0), stop=(u == 31)))
                        if u == 31 and oh == 1:
                            v_y = op("tensor", sPE, bld)
                        else:
                            opn("tensor", bld)
                P.wait("vector", sPE, v_y)
                P.wait("vector", sDVE, dst["yf_free"])
                P.wait("vector", sDVE, v_by)
                opn("vector", lambda e: e.tensor_tensor(sYF[:, 0:512], pYd[0][:],
                                                        sBY[:, 0:512], ALU.add))
                v_yf = op("vector", sDVE,
                          lambda e: e.tensor_tensor(sYF[:, 512:1024], pYd[1][:],
                                                    sBY[:, 512:1024], ALU.add))
                dst["y_free"][0] = v_yf
                dst["y_free"][1] = v_yf
                v_ln4 = emit_ln_wide(sYF[:], [sLn4[:]], sSqD[:], scrD, 1,
                                     prev=max(v_yf, dst["ln4_free"]))[-1]
                dst["yf_free"] = v_ln4
                ob = a % 2
                P.wait("vector", chX1D[gb], v_x1g[g])
                P.wait("vector", chOut[ob], dst["out_free"].get(ob, 0))
                P.wait("vector", sDVE, v_ln4)
                v_o = op("vector", sDVE, lambda e, gb=gb, a=a, ob=ob:
                         e.tensor_tensor(sOut[ob][:], sX1g[gb][:, a * D:(a + 1) * D],
                                         sLn4[:], ALU.add))
                dst["ln4_free"] = v_o
                if a == 1:
                    dst["x1g_free"][gb] = v_o
                P.wait("sync", sDVE, v_o)
                dst["out_free"][ob] = dma("sync", chOut[ob],
                                          out_d.ap().rearrange("(n p) d -> n p d", p=128)[g * 2 + a],
                                          sOut[ob][:])
            dst["g_free"] = sPE.n
            dst["l2t_free"][gb] = v_h
        for chf in (chOut[0], chOut[1], chDbg):
            P.wait("gpsimd", chf, chf.n)
            P.wait("sync", chf, chf.n)
        stPD.close()
        stD.close()
    else:
        for chf in (chOut[0], chOut[1], chDbg):
            P.wait("gpsimd", chf, chf.n)
            P.wait("sync", chf, chf.n)

    with st, nc.Block() as block:
        @block.sync
        def _(e):
            for f in P.ops["sync"]:
                f(e)

        @block.tensor
        def _(e):
            for f in P.ops["tensor"]:
                f(e)

        @block.vector
        def _(e):
            for f in P.ops["vector"]:
                f(e)

        @block.scalar
        def _(e):
            for f in P.ops["scalar"]:
                f(e)

        @block.gpsimd
        def _(e):
            for f in P.ops["gpsimd"]:
                f(e)

    return nc


def kernel(**inputs):
    x, shared = _host_prepare(inputs)
    nc = build_nc()
    in_maps = [dict(shared, x=np.ascontiguousarray(x[b])) for b in range(8)]
    res = run_bass_kernel_spmd(nc, in_maps, core_ids=list(range(8)))
    return np.stack([np.asarray(r["out"], np.float32) for r in res.results])


# version 12
# speedup vs baseline: 1.2174x; 1.0144x over previous
import numpy as np
import ml_dtypes
from contextlib import ExitStack

import concourse.bass as bass
import concourse.mybir as mybir
from concourse.bass_utils import run_bass_kernel_spmd

F32 = mybir.dt.float32
BF16 = mybir.dt.bfloat16
AF = mybir.ActivationFunctionType
ALU = mybir.AluOpType
AX = mybir.AxisListType

S = 4096
D = 1024
HID = 4096
NP = 8
DP = 1152
SHIFT = 20.0
EPS = 1e-5


def _qk_perm():
    p = []
    for h in range(16):
        b = 64 * h
        p += [b + 2 * i for i in range(32)]
        p += [b + 2 * i + 1 for i in range(32)]
    return np.array(p, np.int64)


def _rope_tables():
    theta = 10000.0 ** (-2.0 * (np.arange(32, dtype=np.float64) / 64.0))
    freqs = np.arange(S, dtype=np.float64)[:, None] * theta[None, :]
    cos = np.cos(freqs).astype(np.float32)
    sin = np.sin(freqs).astype(np.float32)
    cos128 = np.ascontiguousarray(cos.T[np.arange(128) % 32])
    sin128 = np.ascontiguousarray(sin.T[np.arange(128) % 32])
    sign = np.where((np.arange(128) % 64) < 32, -1.0, 1.0).astype(np.float32)
    return cos128, np.ascontiguousarray(sin128 * sign[:, None])


def _alpha_beta(wstage, stage):
    i = np.arange(HID)
    s = 1 << stage
    half = (i >> stage) & 1
    blk = i >> (stage + 1)
    j = i & (s - 1)
    k = blk * s + j
    return (wstage[k, half, half].astype(np.float32),
            wstage[k, half, 1 - half].astype(np.float32))


def _scat_nat(alpha, beta, stage):
    s = 1 << stage
    mats = np.zeros((32, 128, 128), np.float32)
    v = np.arange(128)
    for u in range(32):
        idx = u * 128 + v
        mats[u, v, v] = alpha[idx]
        mats[u, v, v ^ s] = beta[idx]
    return mats


def _scat_grp(alpha, beta, stage):
    s2 = 1 << (stage - 5)
    mats = np.zeros((32, 128, 128), np.float32)
    p = np.arange(128)
    for g in range(32):
        idx = (p << 5) | g
        mats[g, p, p] = alpha[idx]
        mats[g, p, p ^ s2] = beta[idx]
    return mats


def _pack(tiles_per_block):
    cols = []
    for blk in tiles_per_block:
        cols.extend(blk)
    return np.ascontiguousarray(np.concatenate(cols, axis=1))


def _host_prepare(inputs):
    x = np.ascontiguousarray(np.asarray(inputs["L0_attn_x"], np.float32))
    Wqkv = np.asarray(inputs["L0_attn_W_qkv"], np.float32)
    Wo = np.asarray(inputs["L0_attn_W_o"], np.float32)
    Wexp = np.asarray(inputs["L0_ffn_W_exp"], np.float32)
    b_exp = np.asarray(inputs["L0_ffn_b_exp"], np.float32)
    up_w = np.asarray(inputs["L0_ffn_up_weights"], np.float32)
    up_bias = np.asarray(inputs["L0_ffn_up_bias"], np.float32)
    dn_w = np.asarray(inputs["L0_ffn_down_weights"], np.float32)
    Wcon = np.asarray(inputs["L0_ffn_W_con"], np.float32)
    b_con = np.asarray(inputs["L0_ffn_b_con"], np.float32)
    dn_bias = np.asarray(inputs["L0_ffn_down_bias"], np.float32)

    pqk = _qk_perm()
    perm = np.concatenate([pqk, 1024 + pqk, 2048 + np.arange(1024)])
    cos128, sinS = _rope_tables()

    wexpTb = np.zeros((HID, DP), np.float32)
    wexpTb[:, :D] = Wexp.T
    wexpTb[:, D] = b_exp

    i = np.arange(HID)
    g2pos = (i & 31) * 128 + ((i >> 5) & 127)
    wconG = np.empty_like(Wcon)
    wconG[g2pos] = Wcon

    ab_up = [_alpha_beta(up_w[j], j) for j in range(12)]
    ab_dn = [_alpha_beta(dn_w[j], j) for j in range(12)]
    upA_nat = [_scat_nat(*ab_up[j], j) for j in range(5)]
    upB_grp = [_scat_grp(*ab_up[j], j) for j in range(5, 12)]
    dnA_nat = [_scat_nat(*ab_dn[j], j) for j in range(5)]
    dnB_grp = [_scat_grp(*ab_dn[j], j) for j in range(5, 12)]

    upA = _pack([[upA_nat[4][u].T, upA_nat[3][u], upA_nat[2][u],
                  upA_nat[1][u], upA_nat[0][u]] for u in range(32)])
    upB = _pack([[upB_grp[6][g].T] + [upB_grp[j][g] for j in (5, 4, 3, 2, 1, 0)]
                 for g in range(32)])
    dnB = _pack([[dnB_grp[0][g]] + [dnB_grp[j][g].T for j in (1, 2, 3, 4, 5, 6)]
                 for g in range(32)])
    dnA = _pack([[dnA_nat[0][u]] + [dnA_nat[j][u].T for j in (1, 2, 3, 4)]
                 for u in range(32)])

    masks = np.zeros((128, 4 * 1024), np.float32)
    pp = np.arange(128)[:, None]
    jj = np.arange(512)[None, :]
    for m in range(4):
        mk = (pp + 128 * m <= jj)
        masks[:, m * 1024:m * 1024 + 512] = mk
        masks[:, m * 1024 + 512:(m + 1) * 1024] = mk

    consts = np.zeros((128, 8), np.float32)
    consts[:, 0] = -SHIFT
    consts[:, 1] = EPS
    consts[:, 2] = 1.0 / D
    consts[:, 3] = -1.0 / D
    consts[:, 4] = -0.5
    consts[:, 5] = -1.0

    shared = {
        "wqkvT": np.ascontiguousarray(Wqkv[perm].T),
        "woT": np.ascontiguousarray(Wo.T),
        "wexpTb": wexpTb.astype(ml_dtypes.bfloat16),
        "wconG": np.ascontiguousarray(wconG).astype(ml_dtypes.bfloat16),
        "cosT": cos128, "sinT": sinS,
        "identF": np.eye(128, dtype=np.float32),
        "identB": np.eye(128).astype(ml_dtypes.bfloat16),
        "p32": np.eye(128)[np.arange(128) ^ 32].astype(ml_dtypes.bfloat16),
        "maskq": masks.astype(ml_dtypes.bfloat16),
        "upA": upA.astype(ml_dtypes.bfloat16), "upB": upB.astype(ml_dtypes.bfloat16),
        "dnB": dnB.astype(ml_dtypes.bfloat16), "dnA": dnA.astype(ml_dtypes.bfloat16),
        "upbias": np.ascontiguousarray(up_bias.reshape(128, 32)),
        "bcon": np.ascontiguousarray(np.tile(b_con, (128, 1))),
        "dnb": np.ascontiguousarray(np.tile(dn_bias, (128, 1))),
        "consts": consts,
    }
    return x, shared



class _Sem:
    def __init__(self, h):
        self.h = h
        self.n = 0

    def inc(self, k=1):
        self.n += k
        return self.n


ENGS = ("sync", "tensor", "vector", "scalar", "gpsimd")


class _Plan:
    def __init__(self):
        self.ops = {e: [] for e in ENGS}
        self.dirty = {e: False for e in ENGS}
        self.lastw = {}
        self.bar_sem = None
        self.bar_target = 0

    def emit(self, eng, fn):
        self.ops[eng].append(fn)
        self.dirty[eng] = True

    def wait(self, eng, sem, val):
        v = int(val)
        key = (eng, id(sem))
        if v <= 0 or self.lastw.get(key, 0) >= v:
            return
        self.lastw[key] = v
        h = sem.h
        self.ops[eng].append(lambda e: e.wait_ge(h, v))

    def barrier(self):
        for e in ENGS:
            for sem in self.all_sems:
                self.wait(e, sem, sem.n)
            self.dirty[e] = False


def build_nc(phase_limit=99, debug=(), a_groups=8, a_parts=31):
    nc = bass.Bass("TRN2", target_bir_lowering=False, detect_race_conditions=False)
    P = _Plan()

    x_d = nc.dram_tensor("x", [S, D], F32, kind="ExternalInput")
    wqkvT_d = nc.dram_tensor("wqkvT", [D, 3072], F32, kind="ExternalInput")
    woT_d = nc.dram_tensor("woT", [D, D], F32, kind="ExternalInput")
    wexpTb_d = nc.dram_tensor("wexpTb", [HID, DP], BF16, kind="ExternalInput")
    wconG_d = nc.dram_tensor("wconG", [HID, D], BF16, kind="ExternalInput")
    cos_d = nc.dram_tensor("cosT", [128, S], F32, kind="ExternalInput")
    sin_d = nc.dram_tensor("sinT", [128, S], F32, kind="ExternalInput")
    identF_d = nc.dram_tensor("identF", [128, 128], F32, kind="ExternalInput")
    identB_d = nc.dram_tensor("identB", [128, 128], BF16, kind="ExternalInput")
    p32_d = nc.dram_tensor("p32", [128, 128], BF16, kind="ExternalInput")
    maskq_d = nc.dram_tensor("maskq", [128, 4096], BF16, kind="ExternalInput")
    upA_d = nc.dram_tensor("upA", [128, 5 * 32 * 128], BF16, kind="ExternalInput")
    upB_d = nc.dram_tensor("upB", [128, 7 * 32 * 128], BF16, kind="ExternalInput")
    dnB_d = nc.dram_tensor("dnB", [128, 7 * 32 * 128], BF16, kind="ExternalInput")
    dnA_d = nc.dram_tensor("dnA", [128, 5 * 32 * 128], BF16, kind="ExternalInput")
    upbias_d = nc.dram_tensor("upbias", [128, 32], F32, kind="ExternalInput")
    bcon_d = nc.dram_tensor("bcon", [128, D], F32, kind="ExternalInput")
    dnb_d = nc.dram_tensor("dnb", [128, D], F32, kind="ExternalInput")
    consts_d = nc.dram_tensor("consts", [128, 8], F32, kind="ExternalInput")

    out_d = nc.dram_tensor("out", [S, D], F32, kind="ExternalOutput")
    q_dram = nc.dram_tensor("q_sc", [NP, 128, S], BF16, kind="Internal")
    v_dram = nc.dram_tensor("v_sc", [NP, 128, 4160], BF16, kind="Internal")
    x1_dram = nc.dram_tensor("x1_sc", [S, D], F32, kind="Internal")
    ln2T_dram = nc.dram_tensor("ln2T_sc", [D, S], BF16, kind="Internal")
    w1_dram = nc.dram_tensor("w1_sc", [HID, DP], BF16, kind="Internal")
    w1f_dram = nc.dram_tensor("w1f_sc", [DP, HID], BF16, kind="Internal")
    r_dram = nc.dram_tensor("r_sc", [HID, D], BF16, kind="Internal")
    r2_dram = nc.dram_tensor("r2_sc", [HID, D], BF16, kind="Internal")
    bias_dram = nc.dram_tensor("bias_sc", [HID], BF16, kind="Internal")

    dbg_d = {}
    for name, shape, dt in (("dbg_ln", [128, 4 * D], F32),
                            ("dbg_lnt", [128, 8 * 512], BF16),
                            ("dbg_k", [128, NP * S], BF16),
                            ("dbg_o", [128, NP * S], BF16),
                            ("dbg_x1", [S, D], F32),
                            ("dbg_l2t", [D, S], BF16),
                            ("dbg_w1f", [128, 9 * HID], BF16),
                            ("dbg_r", [128, 32 * D], BF16),
                            ("dbg_bh", [128, 32], F32)):
        if name in debug:
            dbg_d[name] = nc.dram_tensor(name, shape, dt, kind="ExternalOutput")

    st = ExitStack()

    def SEM(name):
        return _Sem(st.enter_context(nc.semaphore(name)))

    sPE = SEM("sPE")
    sACT = SEM("sACT")
    sDVE = SEM("sDVE")
    sPOOL = SEM("sPOOL")
    P.bar_sem = SEM("sBAR")
    chC = SEM("chC"); chCos = SEM("chCos"); chWq = SEM("chWq"); chWo = SEM("chWo")
    chX = SEM("chX"); chX1 = SEM("chX1"); chL2 = SEM("chL2"); chBnc = SEM("chBnc")
    chBY = SEM("chBY"); chDbg = SEM("chDbg")
    chWF = [SEM(f"chWF{i}") for i in range(4)]
    chRL = [SEM(f"chRL{i}") for i in range(4)]
    chQsp = [SEM(f"chQsp{i}") for i in range(2)]
    chVsp = [SEM(f"chVsp{i}") for i in range(2)]
    chV = [SEM(f"chV{i}") for i in range(2)]
    chQ = [SEM(f"chQ{i}") for i in range(2)]
    chScat = [SEM(f"chScat{i}") for i in range(2)]
    chWEc = [SEM(f"chWEc{i}") for i in range(2)]
    chL2D = [SEM(f"chL2D{i}") for i in range(2)]
    chX1D = [SEM(f"chX1D{i}") for i in range(2)]
    chOut = [SEM(f"chOut{i}") for i in range(2)]
    chStg = [SEM(f"chStg{i}") for i in range(2)]
    chT3 = SEM("chT3")
    chIn = [SEM(f"chIn{i}") for i in range(2)]
    P.all_sems = [sPE, sACT, sDVE, sPOOL, chC, chCos, chWq, chWo, chX, chX1,
                  chL2, chBnc, chBY, chDbg] + chWF + chRL + chQsp + chVsp + chV + chQ + \
                 chScat + chWEc + chL2D + chX1D + chOut + chStg + chIn + [chT3]

    def dma(eng, sem, out_ap, in_ap, **kw):
        v = sem.inc(16)
        h = sem.h
        P.emit(eng, lambda e: e.dma_start(out_ap, in_ap, **kw).then_inc(h, 16))
        return v

    def op(eng, sem, build):
        v = sem.inc()
        h = sem.h
        P.emit(eng, lambda e: build(e).then_inc(h, 1))
        return v

    def opn(eng, build):
        P.emit(eng, build)

    sConst = st.enter_context(nc.sbuf_tensor("sConst", [128, 8], F32, side="left"))
    sIdF = st.enter_context(nc.sbuf_tensor("sIdF", [128, 128], F32, side="left"))
    sIdB = st.enter_context(nc.sbuf_tensor("sIdB", [128, 128], BF16, side="left"))
    sP32 = st.enter_context(nc.sbuf_tensor("sP32", [128, 128], BF16, side="left"))
    sOne = st.enter_context(nc.sbuf_tensor("sOne", [1, 128], BF16, side="left"))
    sBiasH = st.enter_context(nc.sbuf_tensor("sBiasH", [128, 32], F32, side="left"))
    sUpb = st.enter_context(nc.sbuf_tensor("sUpb", [128, 32], F32, side="left"))

    dma("sync", chC, sConst[:], consts_d.ap())
    dma("sync", chC, sIdF[:], identF_d.ap())
    dma("sync", chC, sIdB[:], identB_d.ap())
    dma("sync", chC, sP32[:], p32_d.ap())
    const_ld = dma("sync", chC, sUpb[:], upbias_d.ap())
    for e in ENGS:
        P.wait(e, chC, const_ld)
    v_ones = op("gpsimd", sPOOL, lambda e: e.memset(sOne[:], 1.0))

    CSHIFT = sConst[:, 0:1]
    CEPS = sConst[:, 1:2]
    CINVD = sConst[:, 2:3]
    CNINVD = sConst[:, 3:4]
    CNEG1 = sConst[:, 5:6]

    ln_last = {}

    def emit_ln_wide(xin_ap, out_aps, ssq_ap, scr, W, prev=0):
        sm, smsq, svar, srstd = scr
        key = id(scr)
        P.wait("vector", sDVE, max(prev, ln_last.get(key, 0)))
        seg_in = bass.AP(xin_ap.tensor, xin_ap.offset,
                         [list(xin_ap.ap[0]), [D, W], [1, D]])
        seg_sq = bass.AP(ssq_ap.tensor, ssq_ap.offset,
                         [list(ssq_ap.ap[0]), [D, W], [1, D]])
        v_a = op("vector", sDVE, lambda e: e.reduce_sum(
            bass.AP(sm, 0, [[sm.shape[1], 128], [1, W], [1, 1]]), seg_in, axis=AX.X))
        v_b = op("vector", sDVE, lambda e: e.tensor_tensor(ssq_ap, xin_ap, xin_ap, ALU.mult))
        P.wait("vector", sDVE, v_b)
        v_c = op("vector", sDVE, lambda e: e.reduce_sum(
            bass.AP(smsq, 0, [[smsq.shape[1], 128], [1, W], [1, 1]]), seg_sq, axis=AX.X))
        v_d = op("vector", sDVE, lambda e: e.tensor_scalar(sm[:, 0:W], sm[:, 0:W],
                                                           CNINVD, None, ALU.mult))
        P.wait("vector", sDVE, v_d)
        v_e = op("vector", sDVE, lambda e: e.tensor_tensor(svar[:, 0:W], sm[:, 0:W],
                                                           sm[:, 0:W], ALU.mult))
        v_f = op("vector", sDVE, lambda e: e.tensor_scalar(smsq[:, 0:W], smsq[:, 0:W],
                                                           CINVD, CEPS, ALU.mult, ALU.add))
        P.wait("vector", sDVE, v_f)
        v_g = op("vector", sDVE, lambda e: e.tensor_tensor(svar[:, 0:W], smsq[:, 0:W],
                                                           svar[:, 0:W], ALU.subtract))
        P.wait("scalar", sDVE, v_g)
        v_ln = op("scalar", sACT, lambda e: e.activation(srstd[:, 0:W], svar[:, 0:W], AF.Ln))
        P.wait("scalar", sACT, v_ln)
        v_rs = op("scalar", sACT, lambda e: e.activation(srstd[:, 0:W], srstd[:, 0:W],
                                                         AF.Exp, scale=-0.5))
        P.wait("vector", sACT, v_rs)
        outs = []
        for a in range(W):
            xin_a = bass.AP(xin_ap.tensor, xin_ap.offset + a * D,
                            [list(xin_ap.ap[0]), [1, D]])
            outs.append(op("vector", sDVE,
                           lambda e, o=out_aps[a], xin_a=xin_a, a=a:
                           e.tensor_scalar(o, xin_a, sm[:, a:a + 1],
                                           srstd[:, a:a + 1], ALU.add, ALU.mult)))
        ln_last[key] = outs[-1]
        return outs

    stK = ExitStack()
    sK = stK.enter_context(nc.sbuf_tensor("sK", [128, NP * S], BF16, side="left"))

    if phase_limit >= 1:
        stA = ExitStack()
        A = lambda n, s, d: stA.enter_context(nc.sbuf_tensor(n, s, d, side="left"))
        sWq = A("sWq", [128, 8 * 3072], BF16)
        sCos = A("sCos", [128, S], F32)
        sSin = A("sSin", [128, S], F32)
        sX = A("sX", [128, 4 * D], F32)
        sLN = A("sLN", [128, 4 * D], F32)
        sLnT = A("sLnT", [128, 8 * 512], BF16)
        sKraw = [A(f"sKraw{i}", [128, 512], BF16) for i in range(2)]
        sTa = A("sTa", [128, 512], BF16)
        sQsp = [A(f"sQsp{i}", [128, 512], BF16) for i in range(2)]
        ssqA = A("ssqA", [128, 1024], F32)
        sVraw = [A(f"sVraw{i}", [128, 512], BF16) for i in range(2)]
        sVt = [A(f"sVt{i}", [128, 520], BF16) for i in range(2)]
        scrA = tuple(A(f"scrA{i}", [128, 8], F32) for i in range(4))
        stPA = ExitStack()
        pTr = [stPA.enter_context(nc.psum_tensor(f"pTr{i}", [128, 512], F32)) for i in range(2)]
        pQk = [stPA.enter_context(nc.psum_tensor(f"pQk{i}", [128, 512], F32)) for i in range(2)]
        pSw = stPA.enter_context(nc.psum_tensor("pSw", [128, 512], F32))
        pVt = [stPA.enter_context(nc.psum_tensor(f"pVt{i}", [128, 512], BF16)) for i in range(2)]

        wq_r = wqkvT_d.ap().rearrange("(c p) n -> p c n", p=128)
        sWq_r = sWq.ap().rearrange("p (c n) -> p c n", c=8)
        v_wq_dc = []
        for dc in range(8):
            v_wq_dc.append(dma("gpsimd", chWq, sWq_r[:, dc:dc + 1, :],
                               wq_r[:, dc:dc + 1, :]))
        for i in range(2):
            t = sVt[i]
            opn("gpsimd", lambda e, t=t: e.memset(bass.AP(t, 64, [[520, 128], [130, 4]]), 1.0))
            if i == 0:
                opn("gpsimd", lambda e, t=t: e.memset(bass.AP(t, 129, [[520, 128], [130, 4]]), 1.0))
        v_vones = op("gpsimd", sPOOL, lambda e, t=sVt[1]:
                     e.memset(bass.AP(t, 129, [[520, 128], [130, 4]]), 1.0))

        x_r = x_d.ap().rearrange("(g a p) d -> g p a d", a=4, p=128)
        st_ = dict(tr_free={}, qkv_free={}, sw_free=0, vt_free={}, kraw_free={},
                   qsp_free={}, vraw_free={}, vtile_free={}, x_free=0,
                   lnT_used=0, ln_tr_done=0, v_x={}, lnA_last=0)

        def xload(tg, queue):
            P.wait(queue, sDVE, st_["x_free"])
            st_["v_x"][tg] = dma(queue, chX,
                                 sX.ap().rearrange("p (a d) -> p a d", a=4), x_r[tg])

        def emit_ln_A(tg):
            sm, smsq, svar, srstd = scrA
            P.wait("vector", chX, st_["v_x"][tg])
            P.wait("vector", sDVE, st_["lnA_last"])
            v_a = op("vector", sDVE, lambda e: e.reduce_sum(
                bass.AP(sm, 0, [[sm.shape[1], 128], [1, 4], [1, 1]]),
                bass.AP(sX, 0, [[4 * D, 128], [D, 4], [1, D]]), axis=AX.X))
            prev = v_a
            for a in range(4):
                P.wait("vector", sDVE, prev)
                v_m = op("vector", sDVE, lambda e, a=a:
                         e.tensor_tensor(ssqA[:], sX[:, a * D:(a + 1) * D],
                                         sX[:, a * D:(a + 1) * D], ALU.mult))
                P.wait("vector", sDVE, v_m)
                prev = op("vector", sDVE, lambda e, a=a:
                          e.reduce_sum(smsq[:, a:a + 1], ssqA[:], axis=AX.X))
            P.wait("vector", sDVE, prev)
            v_d = op("vector", sDVE, lambda e: e.tensor_scalar(sm[:, 0:4], sm[:, 0:4],
                                                               CNINVD, None, ALU.mult))
            P.wait("vector", sDVE, v_d)
            v_e = op("vector", sDVE, lambda e: e.tensor_tensor(svar[:, 0:4], sm[:, 0:4],
                                                               sm[:, 0:4], ALU.mult))
            v_f = op("vector", sDVE, lambda e: e.tensor_scalar(smsq[:, 0:4], smsq[:, 0:4],
                                                               CINVD, CEPS, ALU.mult, ALU.add))
            P.wait("vector", sDVE, v_f)
            v_g = op("vector", sDVE, lambda e: e.tensor_tensor(svar[:, 0:4], smsq[:, 0:4],
                                                               svar[:, 0:4], ALU.subtract))
            P.wait("scalar", sDVE, v_g)
            v_ln = op("scalar", sACT, lambda e: e.activation(srstd[:, 0:4], svar[:, 0:4], AF.Ln))
            P.wait("scalar", sACT, v_ln)
            v_rs = op("scalar", sACT, lambda e: e.activation(srstd[:, 0:4], srstd[:, 0:4],
                                                             AF.Exp, scale=-0.5))
            P.wait("vector", sACT, v_rs)
            P.wait("vector", sPE, st_["ln_tr_done"])
            outs = []
            for a in range(4):
                outs.append(op("vector", sDVE,
                               lambda e, a=a:
                               e.tensor_scalar(sLN[:, a * D:(a + 1) * D],
                                               sX[:, a * D:(a + 1) * D],
                                               sm[:, a:a + 1], srstd[:, a:a + 1],
                                               ALU.add, ALU.mult)))
            st_["lnA_last"] = outs[-1]
            return outs
        pend_rope = []

        def flush_rope():
            while pend_rope:
                kind, pair, kb, t0, v_ev = pend_rope.pop(0)
                P.wait("tensor", sACT, v_ev)
                P.wait("tensor", sDVE, st_["sw_free"])
                v_sw = op("tensor", sPE, lambda e, kb=kb:
                          e.matmul(pSw[:], sP32[:], sKraw[kb][:], start=True, stop=True))
                P.wait("vector", sACT, v_ev)
                P.wait("vector", sPE, v_sw)
                opn("vector", lambda e, kb=kb, t0=t0:
                    e.tensor_tensor(sTa[:], sKraw[kb][:], sCos[:, t0:t0 + 512], ALU.mult))
                if kind == "q":
                    qb = pair % 2
                    P.wait("vector", chQsp[qb], st_["qsp_free"].get(qb, 0))
                    v_tb = op("vector", sDVE, lambda e, qb=qb, t0=t0:
                              e.tensor_tensor(sQsp[qb][:], pSw[:], sSin[:, t0:t0 + 512], ALU.mult))
                    P.wait("vector", sDVE, v_tb)
                    v_rp = op("vector", sDVE, lambda e, qb=qb:
                              e.tensor_tensor(sQsp[qb][:], sTa[:], sQsp[qb][:], ALU.add))
                    P.wait("sync", sDVE, v_rp)
                    st_["qsp_free"][qb] = dma("sync", chQsp[qb],
                                              q_dram.ap()[pair, :, t0:t0 + 512],
                                              sQsp[qb][:])
                else:
                    kcol = pair * S + t0
                    v_tb = op("vector", sDVE, lambda e, kcol=kcol, t0=t0:
                              e.tensor_tensor(sK[:, kcol:kcol + 512], pSw[:],
                                              sSin[:, t0:t0 + 512], ALU.mult))
                    P.wait("vector", sDVE, v_tb)
                    v_rp = op("vector", sDVE, lambda e, kcol=kcol:
                              e.tensor_tensor(sK[:, kcol:kcol + 512], sTa[:],
                                              sK[:, kcol:kcol + 512], ALU.add))
                st_["sw_free"] = v_rp
                st_["kraw_free"][kb] = v_rp
        xload(0, "sync")
        dma("sync", chCos, sCos[:], cos_d.ap())
        v_cos = dma("sync", chCos, sSin[:], sin_d.ap())
        if a_parts & 1:
            ln_done = emit_ln_A(0)
            st_["x_free"] = ln_done[-1]
        else:
            ln_done = []
        xload(1, "gpsimd")
        for tg in range(a_groups):
            t0 = tg * 512
            P.wait("tensor", chCos, v_cos)
            for dc in (range(8) if a_parts & 16 else []):
                bank = pTr[dc % 2]
                P.wait("tensor", sACT, st_["tr_free"].get(dc % 2, 0))
                for a in range(4):
                    P.wait("tensor", sDVE, ln_done[a] if a < len(ln_done) else 0)
                    bld = (lambda e, b=bank, a=a, dc=dc:
                           e.transpose(b[:, a * 128:(a + 1) * 128],
                                       sLN[:, a * D + dc * 128: a * D + (dc + 1) * 128],
                                       sIdF[:]))
                    if a < 3:
                        opn("tensor", bld)
                    else:
                        v_tr = op("tensor", sPE, bld)
                P.wait("scalar", sPE, v_tr)
                P.wait("scalar", sPE, st_["lnT_used"])
                st_["tr_free"][dc % 2] = op("scalar", sACT,
                                            lambda e, b=bank, dc=dc:
                                            e.copy(sLnT[:, dc * 512:(dc + 1) * 512], b[:]))
            st_["ln_tr_done"] = sPE.n
            tr_all = (st_["tr_free"].get(0, 0), st_["tr_free"].get(1, 0))
            if tg + 1 < a_groups and (a_parts & 1):
                ln_next = emit_ln_A(tg + 1)
                st_["x_free"] = ln_next[-1]
                if tg + 2 < a_groups:
                    xload(tg + 2, "gpsimd")
            else:
                ln_next = []
            for c in ((list(range(16, 24)) + list(range(16))) if a_parts & 2 else []):
                bank = pQk[c % 2]
                eng_w, val_w = st_["qkv_free"].get(c % 2, ("ACT", 0))
                P.wait("tensor", sACT if eng_w == "ACT" else sDVE, val_w)
                P.wait("tensor", sACT, max(tr_all))
                for dc in range(8):
                    P.wait("tensor", chWq, v_wq_dc[dc])
                    bld = (lambda e, b=bank, c=c, dc=dc:
                           e.matmul(b[:], sWq[:, dc * 3072 + c * 128: dc * 3072 + (c + 1) * 128],
                                    sLnT[:, dc * 512:(dc + 1) * 512],
                                    start=(dc == 0), stop=(dc == 7)))
                    if dc < 7:
                        opn("tensor", bld)
                    else:
                        v_mm = op("tensor", sPE, bld)
                flush_rope()
                if c < 16 and not (a_parts & 4):
                    st_["qkv_free"][c % 2] = ("ACT", 0)
                elif c < 16:
                    pair = c % 8
                    kb = c % 2
                    P.wait("scalar", sPE, v_mm)
                    P.wait("scalar", sDVE, st_["kraw_free"].get(kb, 0))
                    v_ev = op("scalar", sACT, lambda e, b=bank, kb=kb:
                              e.copy(sKraw[kb][:], b[:]))
                    st_["qkv_free"][c % 2] = ("ACT", v_ev)
                    pend_rope.append(("q" if c < 8 else "k", pair, kb, t0, v_ev))
                elif not (a_parts & 8):
                    st_["qkv_free"][c % 2] = ("ACT", 0)
                else:
                    pair = c - 16
                    vb = c % 2
                    P.wait("scalar", sPE, v_mm)
                    P.wait("scalar", sPE, st_["vraw_free"].get(vb, 0))
                    v_ev = op("scalar", sACT, lambda e, b=bank, vb=vb:
                              e.copy(sVraw[vb][:], b[:]))
                    st_["qkv_free"][c % 2] = ("ACT", v_ev)
                    vtb = pVt[c % 2]
                    P.wait("tensor", sACT, v_ev)
                    P.wait("tensor", sACT, st_["vt_free"].get(c % 2, 0))
                    for i4 in range(4):
                        bld = (lambda e, b=vtb, vb=vb, i4=i4:
                               e.transpose(b[:, i4 * 128:(i4 + 1) * 128],
                                           sVraw[vb][:, i4 * 128:(i4 + 1) * 128],
                                           sIdB[:]))
                        if i4 < 3:
                            opn("tensor", bld)
                        else:
                            v_vt = op("tensor", sPE, bld)
                    st_["vraw_free"][vb] = v_vt
                    P.wait("scalar", sPE, v_vt)
                    P.wait("scalar", chVsp[vb], st_["vtile_free"].get(vb, 0))
                    P.wait("scalar", sPOOL, v_vones)
                    opn("scalar", lambda e, b=vtb, vb=vb:
                        e.copy(bass.AP(sVt[vb], 0, [[520, 128], [130, 4], [1, 64]]),
                               bass.AP(b, 0, [[512, 128], [128, 4], [1, 64]])))
                    v_ev2 = op("scalar", sACT, lambda e, b=vtb, vb=vb:
                               e.copy(bass.AP(sVt[vb], 65, [[520, 128], [130, 4], [1, 64]]),
                                      bass.AP(b, 64, [[512, 128], [128, 4], [1, 64]])))
                    st_["vt_free"][c % 2] = v_ev2
                    P.wait("sync", sACT, v_ev2)
                    st_["vtile_free"][vb] = dma("sync", chVsp[vb],
                                                v_dram.ap()[pair, :, tg * 520:(tg + 1) * 520],
                                                sVt[vb][:])
            st_["lnT_used"] = sPE.n
            flush_rope()
            ln_done = ln_next

        if "dbg_k" in dbg_d:
            P.wait("sync", sDVE, sDVE.n)
            dma("sync", chDbg, dbg_d["dbg_k"].ap(), sK[:])
        if "dbg_ln" in dbg_d:
            P.wait("sync", sDVE, sDVE.n)
            dma("sync", chDbg, dbg_d["dbg_ln"].ap(), sLN[:])
        if "dbg_lnt" in dbg_d:
            P.wait("sync", sACT, sACT.n)
            dma("sync", chDbg, dbg_d["dbg_lnt"].ap(), sLnT[:])
        P.barrier()
        stPA.close()
        stA.close()

    stO = ExitStack()
    sO = stO.enter_context(nc.sbuf_tensor("sO", [128, NP * S], BF16, side="right"))

    if phase_limit >= 2:
        stB = ExitStack()
        B_ = lambda n, s, d: stB.enter_context(nc.sbuf_tensor(n, s, d, side="left"))
        sV = [B_(f"sV{i}", [128, 4160], BF16) for i in range(2)]
        sQt = [B_(f"sQt{i}", [128, 512], BF16) for i in range(2)]
        sPP = [B_(f"sPP{i}", [128, 1024], BF16) for i in range(4)]
        sOev0 = B_("sOev0", [64, 512], BF16)
        sOev1 = B_("sOev1", [64, 512], BF16)
        sRecB0 = B_("sRecB0", [1, 512], BF16)
        sRecB1 = B_("sRecB1", [1, 512], BF16)
        sMask = B_("sMask", [128, 4096], BF16)
        v_mask = dma("sync", chC, sMask[:], maskq_d.ap())
        P.wait("gpsimd", chC, v_mask)
        stPB = ExitStack()
        pSc = [stPB.enter_context(nc.psum_tensor(f"pSc{i}", [128, 1024], F32)) for i in range(3)]
        pO = [stPB.enter_context(nc.psum_tensor(f"pO{i}", [65, 512], F32)) for i in range(2)]

        def _recip(e, out_ap, in_ap):
            with nc.allow_low_precision(reason="softmax 1/l to bf16 operand"):
                return e.reciprocal(out_ap, in_ap)

        bst = dict(q_free={}, v_free={}, pp_free={}, sc_free={},
                   o_clean=0, last_nrm=0, last_bc=0)
        v_vdma = {0: dma("sync", chV[0], sV[0][:], v_dram.ap()[0])}
        v_qdma = {(0, 0): dma("sync", chQ[0], sQt[0][:], q_dram.ap()[0, :, 0:512])}
        tails = {}

        def emit_tail_head(i, pr, q0, v_last):
            P.wait("scalar", sPE, v_last)
            P.wait("scalar", sDVE, bst["last_nrm"])
            opn("scalar", lambda e: e.copy(sOev0[:], pO[0][0:64, :]))
            v_oc = op("scalar", sACT,
                      lambda e: e.copy(sOev1[:], pO[1][0:64, :]))
            P.wait("vector", sPE, v_last)
            P.wait("vector", sPE, bst["last_bc"])
            opn("vector", lambda e: _recip(e, sRecB0[:], pO[0][64:65, :]))
            v_rec = op("vector", sDVE,
                       lambda e: _recip(e, sRecB1[:], pO[1][64:65, :]))
            tails[i] = (pr, q0, v_oc, v_rec)

        def emit_tail_body(i):
            pr, q0, v_oc, v_rec = tails.pop(i)
            P.wait("tensor", sDVE, v_rec)
            P.wait("tensor", sACT, v_oc)
            opn("tensor", lambda e:
                e.matmul(pO[0][0:64, :], sOne[:, 0:64], sRecB0[:],
                         start=True, stop=True))
            v_bc = op("tensor", sPE, lambda e:
                      e.matmul(pO[1][0:64, :], sOne[:, 0:64], sRecB1[:],
                               start=True, stop=True))
            bst["last_bc"] = v_bc
            P.wait("vector", sPE, v_bc)
            ocol = pr * S + q0
            opn("vector", lambda e, ocol=ocol:
                e.tensor_tensor(sO[0:64, ocol:ocol + 512], sOev0[:],
                                pO[0][0:64, :], ALU.mult))
            v_nrm = op("vector", sDVE, lambda e, ocol=ocol:
                       e.tensor_tensor(sO[64:128, ocol:ocol + 512], sOev1[:],
                                       pO[1][0:64, :], ALU.mult))
            bst["o_clean"] = v_nrm
            bst["last_nrm"] = v_nrm

        it = 0
        for pr in range(NP):
            if pr + 1 < NP:
                P.wait("sync", sPE, bst["v_free"].get((pr + 1) % 2, 0))
                v_vdma[pr + 1] = dma("sync", chV[(pr + 1) % 2], sV[(pr + 1) % 2][:],
                                     v_dram.ap()[pr + 1])
            vb = pr % 2
            for qt in range(8):
                q0 = qt * 512
                qb = qt % 2
                nk = 4 * qt + 4
                nxt = (pr, qt + 1) if qt < 7 else ((pr + 1, 0) if pr + 1 < NP else None)
                if nxt is not None:
                    P.wait("sync", sPE, bst["q_free"].get((qt + 1) % 2, 0))
                    v_qdma[nxt] = dma("sync", chQ[(qt + 1) % 2], sQt[(qt + 1) % 2][:],
                                      q_dram.ap()[nxt[0], :, nxt[1] * 512:nxt[1] * 512 + 512])
                P.wait("tensor", chQ[qb], v_qdma[(pr, qt)])
                P.wait("tensor", chV[vb], v_vdma[pr])
                exp_done = {}
                mask_done = {}
                vmm_done = {}

                def emit_vmm(ktv):
                    kbv = ktv % 4
                    if ktv in mask_done:
                        P.wait("tensor", sDVE, mask_done[ktv])
                    else:
                        P.wait("tensor", sACT, exp_done[ktv])
                    if ktv == 0:
                        P.wait("tensor", sDVE, bst["o_clean"])
                    opn("tensor", lambda e, ktv=ktv, kbv=kbv, vb=vb, nk=nk:
                        e.matmul(pO[0][:], sV[vb][:, ktv * 130: ktv * 130 + 65],
                                 sPP[kbv][:, 0:512], start=(ktv == 0), stop=(ktv == nk - 1)))
                    v = op("tensor", sPE, lambda e, ktv=ktv, kbv=kbv, vb=vb, nk=nk:
                           e.matmul(pO[1][:], sV[vb][:, ktv * 130 + 65: ktv * 130 + 130],
                                    sPP[kbv][:, 512:1024], start=(ktv == 0), stop=(ktv == nk - 1)))
                    vmm_done[ktv] = v
                    bst["pp_free"][kbv] = v

                for kt in range(nk):
                    kb = kt % 3
                    kp = kt % 4
                    P.wait("tensor", sACT, bst["sc_free"].get(kb, 0))
                    opn("tensor", lambda e, pr=pr, kt=kt, kb=kb, qb=qb:
                        e.matmul(pSc[kb][:, 0:512],
                                 sK[0:64, pr * S + kt * 128: pr * S + (kt + 1) * 128],
                                 sQt[qb][0:64, :], start=True, stop=True,
                                 tile_position=(0, 0)))
                    v_s = op("tensor", sPE, lambda e, pr=pr, kt=kt, kb=kb, qb=qb:
                             e.matmul(pSc[kb][:, 512:1024],
                                      sK[64:128, pr * S + kt * 128: pr * S + (kt + 1) * 128],
                                      sQt[qb][64:128, :], start=True, stop=True,
                                      tile_position=(64, 0)))
                    if kt == 1 and it > 0:
                        emit_tail_body(it - 1)
                    P.wait("scalar", sPE, v_s)
                    P.wait("scalar", sPE, bst["pp_free"].get(kp, 0))
                    v_e = op("scalar", sACT, lambda e, kb=kb, kp=kp:
                             e.activation(sPP[kp][:], pSc[kb][:], AF.Exp,
                                          bias=CSHIFT, scale=0.125))
                    exp_done[kt] = v_e
                    bst["sc_free"][kb] = v_e
                    if kt >= 4 * qt:
                        m = kt - 4 * qt
                        P.wait("vector", sACT, v_e)
                        mask_done[kt] = op("vector", sDVE,
                                           lambda e, kp=kp, m=m:
                                           e.tensor_tensor(sPP[kp][:], sPP[kp][:],
                                                           sMask[:, m * 1024:(m + 1) * 1024],
                                                           ALU.mult))
                    if kt >= 2:
                        emit_vmm(kt - 2)
                    if kt == nk - 1:
                        emit_vmm(kt - 1)
                        emit_vmm(kt)

                v_last = vmm_done[nk - 1]
                emit_tail_head(it, pr, q0, v_last)
                bst["q_free"][qb] = v_last
                bst["v_free"][vb] = v_last
                it += 1
        emit_tail_body(it - 1)

        if "dbg_o" in dbg_d:
            P.wait("sync", sDVE, sDVE.n)
            dma("sync", chDbg, dbg_d["dbg_o"].ap(), sO[:])
        P.barrier()
        stPB.close()
        stB.close()
    stK.close()

    cmp = {}
    if phase_limit >= 4:
        stCm = ExitStack()
        M_ = lambda n, s, d: stCm.enter_context(nc.sbuf_tensor(n, s, d, side="left"))
        sMW = [M_(f"sMW{i}", [128, 32 * 128], BF16) for i in range(2)]
        sScW = M_("sScW", [128, 32 * 128], BF16)
        sIn = [M_(f"sIn{i}", [128, DP], BF16) for i in range(2)]
        sStg = [M_(f"sStg{i}", [128, DP], BF16) for i in range(2)]
        sBg2 = M_("sBg2", [128, 32], BF16)
        sT3 = M_("sT3", [128, 2 * 9 * 128], BF16)
        stPZ = ExitStack()
        pF = stPZ.enter_context(nc.psum_tensor("pFz", [128, 1024], F32))
        pB3 = [stPZ.enter_context(nc.psum_tensor(f"pB3z{i}", [128, 512], BF16)) for i in range(2)]
        mstc = dict(mw_gate={}, scw_busy={}, v_ld={}, v0=0, pf_free={},
                    we_free={}, in_free={}, t3p_free={}, stg_free={},
                    t3stg={}, bankctr=0)

        def _scat_load(src_d, nsteps, step, c):
            row = 32 * nsteps * 128
            P.wait("gpsimd", sPE, mstc["scw_busy"].get(c, 0))
            mstc["v_ld"][(step, c)] = dma(
                "gpsimd", chScat[c % 2],
                sScW.ap().rearrange("p (b k) -> p b k", b=32)[:, c * 8:(c + 1) * 8, :],
                bass.AP(src_d, step * 128 + c * 8 * nsteps * 128,
                        [[row, 128], [nsteps * 128, 8], [1, 128]]))

        def _chain_pre(src_d, nsteps, m0):
            row = 32 * nsteps * 128
            P.wait("gpsimd", sPE, mstc["mw_gate"].get(m0, 0))
            mstc["v0"] = dma("gpsimd", chScat[0],
                             sMW[m0].ap().rearrange("p (b c) -> p b c", b=32),
                             bass.AP(src_d, 0, [[row, 128], [nsteps * 128, 32], [1, 128]]))
            for c in range(4):
                _scat_load(src_d, nsteps, 1, c)

        def _chain_run(src_d, nsteps, m0):
            v0 = mstc["v0"]
            evprev = {}
            for step in range(1, nsteps):
                cur = sMW[(m0 + step - 1) % 2]
                out = sMW[(m0 + step) % 2]
                for chunk in range(4):
                    P.wait("tensor", chScat[0], v0)
                    P.wait("tensor", chScat[chunk % 2], mstc["v_ld"][(step, chunk)])
                    for ep in evprev.get(chunk, ()):
                        P.wait("tensor", ep[1], ep[2])
                    evs = []
                    for half in range(2):
                        act_only = cmp.get("act_only", False)
                        eng0, sem0, val0 = mstc["pf_free"].get(half, ("vector", sDVE, 0))
                        P.wait("tensor", sem0, val0)
                        for j in range(4):
                            b = chunk * 8 + half * 4 + j
                            bld = (lambda e, half=half, j=j, b=b, cur=cur:
                                   e.matmul(pF[:, half * 512 + j * 128: half * 512 + (j + 1) * 128],
                                            sScW[:, b * 128:(b + 1) * 128],
                                            cur[:, b * 128:(b + 1) * 128],
                                            start=True, stop=True))
                            if j < 3:
                                opn("tensor", bld)
                            else:
                                v_bm = op("tensor", sPE, bld)
                        eng = "vector" if (half == 0 and not act_only) else "scalar"
                        sem_ = sDVE if (half == 0 and not act_only) else sACT
                        P.wait(eng, sPE, v_bm)
                        if eng == "vector":
                            v_ev = op(eng, sem_, lambda e, chunk=chunk, half=half, out=out:
                                      e.tensor_copy(out[:, chunk * 1024 + half * 512:
                                                        chunk * 1024 + (half + 1) * 512],
                                                    pF[:, half * 512:(half + 1) * 512]))
                        else:
                            v_ev = op(eng, sem_, lambda e, chunk=chunk, half=half, out=out:
                                      e.copy(out[:, chunk * 1024 + half * 512:
                                                 chunk * 1024 + (half + 1) * 512],
                                             pF[:, half * 512:(half + 1) * 512]))
                        mstc["pf_free"][half] = (eng, sem_, v_ev)
                        evs.append((eng, sem_, v_ev))
                    evprev[chunk] = evs
                    mstc["scw_busy"][chunk] = sPE.n
                    if step + 1 < nsteps:
                        _scat_load(src_d, nsteps, step + 1, chunk)
                    yield
            mstc["mw_gate"][(m0 + nsteps - 2) % 2] = sPE.n
            cmp["final"] = sMW[(m0 + nsteps - 1) % 2]

        def _apply_one(final, blk, sb, nsl, v_rhs, chrhs):
            for sem, val in mstc["stg_free"].get(sb, ()):
                P.wait("scalar", sem, val)
            banks = [(pF, 0), (pF, 512)] + cmp.get("xbanks", [])
            evl = []
            for k, (c0, w) in enumerate(nsl):
                bi = mstc["bankctr"] % len(banks)
                mstc["bankctr"] += 1
                bt, cb = banks[bi]
                P.wait("tensor", chrhs, v_rhs)
                eng_a, sem_a, val_a = mstc["pf_free"].get(bi, ("scalar", sACT, 0))
                P.wait("tensor", sem_a, val_a)
                v_a = op("tensor", sPE, lambda e, bt=bt, cb=cb, w=w, blk=blk, c0=c0,
                         final=final, sb=sb:
                         e.matmul(bt[:, cb:cb + w], final[:, blk * 128:(blk + 1) * 128],
                                  sIn[sb][:, c0:c0 + w], start=True, stop=True))
                P.wait("scalar", sPE, v_a)
                v_e = op("scalar", sACT, lambda e, bt=bt, cb=cb, sb=sb, c0=c0, w=w:
                         e.copy(sStg[sb][:, c0:c0 + w], bt[:, cb:cb + w]))
                mstc["pf_free"][bi] = ("scalar", sACT, v_e)
                evl.append(v_e)
            return evl

        def compose_gen():
            _chain_pre(upA_d, 5, 0)
            yield from _chain_run(upA_d, 5, 0)
            finalA = cmp["final"]

            def pre_upA(u):
                sb = u % 2
                P.wait("gpsimd", sPE, mstc["we_free"].get(sb, 0))
                return dma("gpsimd", chWEc[sb], sIn[sb][:],
                           wexpTb_d.ap().rearrange("(u p) c -> u p c", p=128)[u])

            v_rhs = {0: pre_upA(0)}
            for u in range(32):
                sb = u % 2
                if u + 1 < 32:
                    v_rhs[u + 1] = pre_upA(u + 1)
                evl = _apply_one(finalA, u, sb, ((0, 512), (512, 512), (1024, 128)),
                                 v_rhs[u], chWEc[sb])
                P.wait("gpsimd", sACT, evl[-1])
                v_st = dma("gpsimd", chStg[sb],
                           w1_dram.ap().rearrange("(u p) c -> u p c", p=128)[u],
                           sStg[sb][:])
                mstc["stg_free"][sb] = [(chStg[sb], v_st)]
                mstc["we_free"][sb] = sPE.n
                if u == 4:
                    _chain_pre(upB_d, 7, 1)
                yield
            mstc["mw_gate"][0] = sPE.n
            yield from _chain_run(upB_d, 7, 1)
            finalB = cmp["final"]
            for chs in chStg:
                P.wait("gpsimd", chs, chs.n)

            def pre_upB(g):
                sb = g % 2
                P.wait("gpsimd", sPE, mstc["in_free"].get(sb, 0))
                return dma("gpsimd", chIn[sb], sIn[sb][:],
                           bass.AP(w1_dram, g * DP, [[32 * DP, 128], [1, DP]]))

            v_rhs = {0: pre_upB(0)}
            for g in range(32):
                sb = g % 2
                if g + 1 < 32:
                    v_rhs[g + 1] = pre_upB(g + 1)
                evl = _apply_one(finalB, g, sb, ((0, 512), (512, 512), (1024, 128)),
                                 v_rhs[g], chIn[sb])
                mstc["in_free"][sb] = sPE.n
                yield
                P.wait("vector", sACT, evl[2])
                v_bg = op("vector", sDVE, lambda e, sb=sb, g=g:
                          e.tensor_copy(sBg2[:, g:g + 1], sStg[sb][:, 1024:1025]))
                for bi, (dcs, ve) in enumerate((((0, 1, 2, 3), evl[0]),
                                                ((4, 5, 6, 7), evl[1]),
                                                ((8,), evl[2]))):
                    bank = pB3[bi % 2]
                    engt, semt, valt = mstc["t3p_free"].get(bi % 2, ("vector", sDVE, 0))
                    P.wait("tensor", semt, valt)
                    P.wait("tensor", sACT, ve)
                    for kk, dc in enumerate(dcs):
                        bld = (lambda e, bank=bank, kk=kk, dc=dc, sb=sb:
                               e.transpose(bank[:, kk * 128:(kk + 1) * 128],
                                           sStg[sb][:, dc * 128:(dc + 1) * 128],
                                           sIdB[:]))
                        if kk < len(dcs) - 1:
                            opn("tensor", bld)
                        else:
                            v_t = op("tensor", sPE, bld)
                    P.wait("vector", sPE, v_t)
                    if bi == 0:
                        P.wait("vector", chT3, mstc["t3stg"].get(g % 2, 0))
                    nb = len(dcs)
                    dc0 = dcs[0]
                    base = (g % 2) * 9 * 128
                    v_t3 = op("vector", sDVE,
                              lambda e, bank=bank, nb=nb, dc0=dc0, base=base:
                              e.tensor_copy(sT3[:, base + dc0 * 128: base + (dc0 + nb) * 128],
                                            bank[:, 0:nb * 128]))
                    mstc["t3p_free"][bi % 2] = ("vector", sDVE, v_t3)
                P.wait("gpsimd", sDVE, v_t3)
                mstc["t3stg"][g % 2] = dma(
                    "gpsimd", chT3,
                    bass.AP(w1f_dram, g * 128, [[HID, 128], [128 * HID, 9], [1, 128]]),
                    bass.AP(sT3, (g % 2) * 9 * 128, [[2 * 9 * 128, 128], [128, 9], [1, 128]]))
                mstc["stg_free"][sb] = [(sPE, sPE.n), (sDVE, v_bg)]
                if g == 4:
                    _chain_pre(dnB_d, 7, 0)
                yield
            mstc["mw_gate"][1] = sPE.n
            cmp["w1f_done"] = [(chT3, chT3.n)]
            P.wait("vector", sDVE, sDVE.n)
            cmp["v_bh"] = op("vector", sDVE, lambda e:
                             e.tensor_tensor(sBiasH[:], sBg2[:], sUpb[:], ALU.add))
            yield
            yield from _chain_run(dnB_d, 7, 0)
            finalDB = cmp["final"]

            def pre_dnB(g):
                sb = g % 2
                P.wait("gpsimd", sPE, mstc["we_free"].get(sb, 0))
                return dma("gpsimd", chWEc[sb], sIn[sb][:, 0:D],
                           wconG_d.ap().rearrange("(g p) c -> g p c", p=128)[g])

            v_rhs = {0: pre_dnB(0)}
            for g in range(32):
                sb = g % 2
                if g + 1 < 32:
                    v_rhs[g + 1] = pre_dnB(g + 1)
                evl = _apply_one(finalDB, g, sb, ((0, 512), (512, 512)),
                                 v_rhs[g], chWEc[sb])
                P.wait("gpsimd", sACT, evl[-1])
                v_st = dma("gpsimd", chStg[sb],
                           r_dram.ap().rearrange("(g p) c -> g p c", p=128)[g],
                           sStg[sb][:, 0:D])
                mstc["stg_free"][sb] = [(chStg[sb], v_st)]
                mstc["we_free"][sb] = sPE.n
                if g == 4:
                    _chain_pre(dnA_d, 5, 1)
                yield
            mstc["mw_gate"][0] = sPE.n
            yield from _chain_run(dnA_d, 5, 1)
            finalDA = cmp["final"]
            for chs in chStg:
                P.wait("gpsimd", chs, chs.n)

            def pre_dnA(u):
                sb = u % 2
                P.wait("gpsimd", sPE, mstc["in_free"].get(sb, 0))
                for a4 in range(4):
                    v = dma("gpsimd", chIn[sb],
                            sIn[sb][a4 * 32:(a4 + 1) * 32, 0:D],
                            bass.AP(r_dram, (u * 4 + a4) * D, [[128 * D, 32], [1, D]]))
                return v

            v_rhs = {0: pre_dnA(0)}
            for u in range(32):
                sb = u % 2
                if u + 1 < 32:
                    v_rhs[u + 1] = pre_dnA(u + 1)
                evl = _apply_one(finalDA, u, sb, ((0, 512), (512, 512)),
                                 v_rhs[u], chIn[sb])
                P.wait("gpsimd", sACT, evl[-1])
                v_st = dma("gpsimd", chStg[sb],
                           r2_dram.ap().rearrange("(u p) c -> u p c", p=128)[u],
                           sStg[sb][:, 0:D])
                mstc["stg_free"][sb] = [(chStg[sb], v_st)]
                mstc["in_free"][sb] = sPE.n
                if cmp.get("sr_load"):
                    cmp["sr_load"](u, v_st, chStg[sb])
                yield
            mstc["mw_gate"][1] = sPE.n

        cmp["gen"] = compose_gen()
        cmp["seed_pf"] = lambda d: mstc["pf_free"].update(
            {k: v for k, v in d.items()})

        def drive(n):
            g = cmp.get("gen")
            if g is None:
                return
            for _ in range(n):
                try:
                    next(g)
                except StopIteration:
                    cmp["gen"] = None
                    return
        cmp["drive"] = drive
    else:
        cmp["drive"] = lambda n: None

    stO.close()
    if phase_limit >= 5:
        sW1F = st.enter_context(nc.sbuf_tensor("sW1F", [128, 8 * HID], BF16, side="right"))

        def emit_w1f_load(pe_val):
            P.wait("sync", sPE, pe_val)
            for sem_, val_ in cmp["w1f_done"]:
                P.wait("sync", sem_, val_)
            chans = []
            for q in range(4):
                v = dma("sync", chWF[q],
                        sW1F.ap().rearrange("p (dc c) -> p dc c", dc=8)[:, q * 2:(q + 1) * 2, :],
                        w1f_dram.ap().rearrange("(dc p) c -> p dc c", p=128)[:, q * 2:(q + 1) * 2, :])
                chans.append((chWF[q], v))
            cmp["w1f_chans"] = chans

    if phase_limit >= 3:
        stC = ExitStack()
        C_ = lambda n, s, d: stC.enter_context(nc.sbuf_tensor(n, s, d, side="left"))
        sWo = C_("sWo", [128, 8 * 1024], BF16)
        sYT = C_("sYT", [128, 8 * 512], F32)
        sY = C_("sY", [128, 4 * D], F32)
        sX1 = C_("sX1", [128, 4 * D], F32)
        sLn2 = C_("sLn2", [128, 4 * D], F32)
        sL2Tg = C_("sL2Tg", [128, 8 * 512], BF16)
        scrC = tuple(C_(f"scrC{i}", [128, 8], F32) for i in range(4))
        stPC = ExitStack()
        pY = [stPC.enter_context(nc.psum_tensor(f"pYc{i}", [128, 512], F32)) for i in range(2)]
        pT = [stPC.enter_context(nc.psum_tensor(f"pTc{i}", [128, 512], F32)) for i in range(2)]

        v_wo = dma("gpsimd", chWo, sWo.ap().rearrange("p (c n) -> p c n", c=8),
                   woT_d.ap().rearrange("(c p) n -> p c n", p=128))
        x_r = x_d.ap().rearrange("(g a p) d -> g p a d", a=4, p=128)
        cst = dict(ymm_free={}, yt_used=0, pt_free=0, sy_ev=0,
                   x1add={}, ln2=None, ln2_last=0, ln3_last=0, tr2_pe=0,
                   x1sp_free=0, l2t_free=0, v_x1={})
        drv = cmp["drive"]

        def emit_ymm(tg):
            t0 = tg * 512
            for oc in range(8):
                bank = pY[oc % 2]
                P.wait("tensor", sACT, cst["ymm_free"].get(oc % 2, 0))
                P.wait("tensor", chWo, v_wo)
                for pc in range(8):
                    bld = (lambda e, b=bank, oc=oc, pc=pc, t0=t0:
                           e.matmul(b[:], sWo[:, pc * 1024 + oc * 128: pc * 1024 + (oc + 1) * 128],
                                    sO[:, pc * S + t0: pc * S + t0 + 512],
                                    start=(pc == 0), stop=(pc == 7)))
                    if pc < 7:
                        opn("tensor", bld)
                    else:
                        v_mm = op("tensor", sPE, bld)
                P.wait("scalar", sPE, v_mm)
                P.wait("scalar", sPE, cst["yt_used"])
                cst["ymm_free"][oc % 2] = op("scalar", sACT,
                                             lambda e, b=bank, oc=oc:
                                             e.copy(sYT[:, oc * 512:(oc + 1) * 512], b[:]))
            cst["yt_ev"] = sACT.n

        def emit_ytr(tg):
            v_yt_ev = cst["yt_ev"]
            for a in range(4):
                P.wait("tensor", sACT, v_yt_ev)
                P.wait("tensor", sACT, cst["pt_free"])
                for oc in range(8):
                    bb = pT[0] if oc < 4 else pT[1]
                    bld = (lambda e, bb=bb, oc=oc, a=a:
                           e.transpose(bb[:, (oc % 4) * 128:(oc % 4 + 1) * 128],
                                       sYT[:, oc * 512 + a * 128: oc * 512 + (a + 1) * 128],
                                       sIdF[:]))
                    if oc < 7:
                        opn("tensor", bld)
                    else:
                        v_tr = op("tensor", sPE, bld)
                P.wait("scalar", sPE, v_tr)
                P.wait("scalar", sDVE, cst["ln2_last"])
                opn("scalar", lambda e, a=a: e.copy(sY[:, a * D: a * D + 512], pT[0][:]))
                cst["pt_free"] = op("scalar", sACT,
                                    lambda e, a=a:
                                    e.copy(sY[:, a * D + 512: (a + 1) * D], pT[1][:]))
            cst["yt_used"] = sPE.n
            cst["sy_ev"] = sACT.n

        def emit_lnsec(tg):
            P.wait("vector", sACT, cst["sy_ev"])
            P.wait("vector", sPE, cst["tr2_pe"])
            ln3 = emit_ln_wide(sY[:], [sY[:, a * D:(a + 1) * D] for a in range(4)],
                               sLn2[:], scrC, 4)
            cst["ln3_last"] = ln3[-1]
            P.wait("vector", chX, cst["v_x1"][tg])
            P.wait("vector", sDVE, ln3[-1])
            v_add = 0
            for a in range(4):
                v_add = op("vector", sDVE, lambda e, a=a:
                           e.tensor_tensor(sX1[:, a * D:(a + 1) * D],
                                           sX1[:, a * D:(a + 1) * D],
                                           sY[:, a * D:(a + 1) * D], ALU.add))
            cst["x1add"][tg] = v_add
            P.wait("sync", sDVE, v_add)
            for a in range(4):
                v_sp = dma("sync", chX1,
                           x1_dram.ap().rearrange("(n p) d -> n p d", p=128)[tg * 4 + a],
                           sX1[:, a * D:(a + 1) * D])
            cst["x1sp_free"] = v_sp
            cst["ln2"] = emit_ln_wide(sX1[:], [sLn2[:, a * D:(a + 1) * D] for a in range(4)],
                                      sY[:], scrC, 4, prev=v_add)
            cst["ln2_last"] = cst["ln2"][-1]

        def emit_tr2(tg):
            t0 = tg * 512
            for a in range(4):
                P.wait("tensor", sDVE, cst["ln2"][a])
                P.wait("tensor", sACT, cst["pt_free"])
                for dc in range(8):
                    bb = pT[0] if dc < 4 else pT[1]
                    bld = (lambda e, bb=bb, dc=dc, a=a:
                           e.transpose(bb[:, (dc % 4) * 128:(dc % 4 + 1) * 128],
                                       sLn2[:, a * D + dc * 128: a * D + (dc + 1) * 128],
                                       sIdF[:]))
                    if dc < 7:
                        opn("tensor", bld)
                    else:
                        v_tr2 = op("tensor", sPE, bld)
                P.wait("scalar", sPE, v_tr2)
                P.wait("scalar", chL2, cst["l2t_free"])
                opn("scalar", lambda e, a=a:
                    e.copy(bass.AP(sL2Tg, a * 128, [[4096, 128], [512, 4], [1, 128]]),
                           bass.AP(pT[0], 0, [[512, 128], [128, 4], [1, 128]])))
                cst["pt_free"] = op("scalar", sACT, lambda e, a=a:
                                    e.copy(bass.AP(sL2Tg, 2048 + a * 128,
                                                   [[4096, 128], [512, 4], [1, 128]]),
                                           bass.AP(pT[1], 0, [[512, 128], [128, 4], [1, 128]])))
            cst["tr2_pe"] = sPE.n
            v_l2ev = sACT.n
            P.wait("sync", sACT, v_l2ev)
            l2r = ln2T_dram.ap().rearrange("(dc p) t -> dc p t", p=128)
            for dc in range(8):
                v = dma("sync", chL2, l2r[dc][:, t0:t0 + 512],
                        sL2Tg[:, dc * 512:(dc + 1) * 512])
            cst["l2t_free"] = v

        def emit_xload(tg):
            P.wait("sync", chX1, cst["x1sp_free"])
            P.wait("sync", sDVE, cst["ln2_last"])
            cst["v_x1"][tg] = dma("sync", chX,
                                  sX1.ap().rearrange("p (a d) -> p a d", a=4), x_r[tg])

        cmp["act_only"] = True
        for tg in range(8):
            emit_ymm(tg)
            drv(3)
            if tg > 0:
                emit_lnsec(tg - 1)
            emit_xload(tg)
            emit_ytr(tg)
            drv(2)
            if tg > 0:
                emit_tr2(tg - 1)
            drv(2)
        cmp["act_only"] = False
        emit_lnsec(7)
        drv(16)
        emit_tr2(7)
        if phase_limit >= 4:
            cmp["xbanks"] = [(pY[0], 0), (pY[1], 0)]
            gate = ("scalar", sACT, sACT.n)
            cmp["seed_pf"]({2: gate, 3: gate})
        stC.close()
        if phase_limit >= 5:
            sR_ = st.enter_context(nc.sbuf_tensor("sR", [128, 32 * D], BF16, side="right"))
            cmp["sR"] = sR_
            rl_done = {}

            def _sr_load(u, v_st, chs):
                if (u + 1) % 8 != 0:
                    return
                q = u // 8
                for chs_ in chStg:
                    P.wait("sync", chs_, chs_.n)
                rl_done[q] = dma("sync", chRL[q],
                                 sR_[32 * q:32 * (q + 1), :].rearrange(
                                     "p (g c) -> p g c", c=D),
                                 bass.AP(r2_dram, q * 32 * 32 * D,
                                         [[32 * D, 32], [D, 32], [1, D]]))
            cmp["sr_load"] = _sr_load
            cmp["rl_done"] = rl_done
        while cmp.get("gen") is not None:
            drv(1)
            if phase_limit >= 5 and cmp.get("w1f_done") and not cmp.get("w1f_chans"):
                emit_w1f_load(sPE.n)

        if "dbg_x1" in dbg_d:
            P.wait("sync", chX1, chX1.n)
            dma("sync", chDbg, dbg_d["dbg_x1"].ap(), x1_dram.ap())
        if "dbg_l2t" in dbg_d:
            P.wait("sync", chL2, chL2.n)
            dma("sync", chDbg, dbg_d["dbg_l2t"].ap(), ln2T_dram.ap())
        if "dbg_bh" in dbg_d and phase_limit >= 4:
            P.wait("sync", sDVE, sDVE.n)
            dma("sync", chDbg, dbg_d["dbg_bh"].ap(), sBiasH[:])
        P.barrier()
        stPC.close()
    elif phase_limit >= 4:
        while cmp.get("gen") is not None:
            cmp["drive"](50)
        P.barrier()
    if phase_limit >= 4:
        stPZ.close()
        stCm.close()

    if phase_limit >= 5:
        sR = cmp["sR"]
        if cmp.get("w1f_chans") is None:
            emit_w1f_load(sPE.n)
        rl_chans = [(chRL[q], cmp["rl_done"][q]) for q in range(4)]
        for ch_, v_ in cmp["w1f_chans"]:
            P.wait("tensor", ch_, v_)
        stD = ExitStack()
        D_ = lambda n, s, d: stD.enter_context(nc.sbuf_tensor(n, s, d, side="left"))
        sL2T = [D_(f"sL2T{i}", [128, 8 * 256], BF16) for i in range(2)]
        sX1g = [D_(f"sX1g{i}", [128, 2 * D], F32) for i in range(2)]
        sG = D_("sG", [128, 32 * 256], BF16)
        sBY = D_("sBY", [128, D], F32)
        sYF = D_("sYF", [128, D], F32)
        sLn4 = D_("sLn4", [128, D], F32)
        sOut = [D_(f"sOut{i}", [128, D], F32) for i in range(2)]
        sSqD = D_("sSqD", [128, D], F32)
        scrD = tuple(D_(f"scrD{i}", [128, 8], F32) for i in range(4))
        stPD = ExitStack()
        pH = [stPD.enter_context(nc.psum_tensor(f"pH{i}", [128, 256], F32)) for i in range(2)]
        pYd = [stPD.enter_context(nc.psum_tensor(f"pYd{i}", [128, 512], F32)) for i in range(2)]

        v_b1 = dma("sync", chBY, sBY[:], bcon_d.ap())
        v_b2 = dma("sync", chBY, sYF[:], dnb_d.ap())
        P.wait("vector", chBY, v_b2)
        v_by = op("vector", sDVE, lambda e: e.tensor_tensor(sBY[:], sBY[:], sYF[:], ALU.add))

        dst = dict(h_free={}, g_free=0, y_free={}, l2t_free={}, x1g_free={},
                   out_free={}, yf_free=0, ln4_free=0)
        v_l2 = {}
        v_x1g = {}

        def dma_group(g):
            gb = g % 2
            P.wait("sync", sPE, dst["l2t_free"].get(gb, 0))
            v_l2[g] = dma("sync", chL2D[gb],
                          sL2T[gb].ap().rearrange("p (dc c) -> p dc c", dc=8),
                          bass.AP(ln2T_dram, g * 256, [[S, 128], [128 * S, 8], [1, 256]]))
            P.wait("sync", sDVE, dst["x1g_free"].get(gb, 0))
            v_x1g[g] = dma("sync", chX1D[gb],
                           sX1g[gb].ap().rearrange("p (a d) -> p a d", a=2),
                           x1_dram.ap().rearrange("(n a p) d -> n p a d", a=2, p=128)[g])

        dma_group(0)
        for g in range(16):
            gb = g % 2
            if g + 1 < 16:
                dma_group(g + 1)
            P.wait("tensor", chL2D[gb], v_l2[g])
            for u in range(32):
                ub = u % 2
                P.wait("tensor", sACT, dst["h_free"].get(ub, 0))
                for dc in range(8):
                    bld = (lambda e, ub=ub, u=u, dc=dc, gb=gb:
                           e.matmul(pH[ub][:], sW1F[:, dc * HID + u * 128: dc * HID + (u + 1) * 128],
                                    sL2T[gb][:, dc * 256:(dc + 1) * 256],
                                    start=(dc == 0), stop=(dc == 7)))
                    if dc < 7:
                        opn("tensor", bld)
                    else:
                        v_h = op("tensor", sPE, bld)
                P.wait("scalar", sPE, v_h)
                P.wait("scalar", sPE, dst["g_free"])
                P.wait("scalar", sDVE, cmp.get("v_bh", 0))
                dst["h_free"][ub] = op("scalar", sACT,
                                       lambda e, ub=ub, u=u:
                                       e.activation(sG[:, u * 256:(u + 1) * 256], pH[ub][:],
                                                    AF.Gelu, bias=sBiasH[:, u:u + 1]))
            v_gelu = sACT.n
            for a in range(2):
                for ch_, v_ in rl_chans:
                    P.wait("tensor", ch_, v_)
                P.wait("tensor", sACT, v_gelu)
                P.wait("tensor", sDVE, dst["y_free"].get(0, 0))
                P.wait("tensor", sDVE, dst["y_free"].get(1, 0))
                for u in range(32):
                    for oh in range(2):
                        bld = (lambda e, a=a, u=u, oh=oh:
                               e.matmul(pYd[oh][:], sG[:, u * 256 + a * 128: u * 256 + (a + 1) * 128],
                                        sR[:, u * D + oh * 512: u * D + (oh + 1) * 512],
                                        start=(u == 0), stop=(u == 31)))
                        if u == 31 and oh == 1:
                            v_y = op("tensor", sPE, bld)
                        else:
                            opn("tensor", bld)
                P.wait("vector", sPE, v_y)
                P.wait("vector", sDVE, dst["yf_free"])
                P.wait("vector", sDVE, v_by)
                opn("vector", lambda e: e.tensor_tensor(sYF[:, 0:512], pYd[0][:],
                                                        sBY[:, 0:512], ALU.add))
                v_yf = op("vector", sDVE,
                          lambda e: e.tensor_tensor(sYF[:, 512:1024], pYd[1][:],
                                                    sBY[:, 512:1024], ALU.add))
                dst["y_free"][0] = v_yf
                dst["y_free"][1] = v_yf
                v_ln4 = emit_ln_wide(sYF[:], [sLn4[:]], sSqD[:], scrD, 1,
                                     prev=max(v_yf, dst["ln4_free"]))[-1]
                dst["yf_free"] = v_ln4
                ob = a % 2
                P.wait("vector", chX1D[gb], v_x1g[g])
                P.wait("vector", chOut[ob], dst["out_free"].get(ob, 0))
                P.wait("vector", sDVE, v_ln4)
                v_o = op("vector", sDVE, lambda e, gb=gb, a=a, ob=ob:
                         e.tensor_tensor(sOut[ob][:], sX1g[gb][:, a * D:(a + 1) * D],
                                         sLn4[:], ALU.add))
                dst["ln4_free"] = v_o
                if a == 1:
                    dst["x1g_free"][gb] = v_o
                P.wait("sync", sDVE, v_o)
                dst["out_free"][ob] = dma("sync", chOut[ob],
                                          out_d.ap().rearrange("(n p) d -> n p d", p=128)[g * 2 + a],
                                          sOut[ob][:])
            dst["g_free"] = sPE.n
            dst["l2t_free"][gb] = v_h
        for chf in (chOut[0], chOut[1], chDbg):
            P.wait("gpsimd", chf, chf.n)
            P.wait("sync", chf, chf.n)
        stPD.close()
        stD.close()
    else:
        for chf in (chOut[0], chOut[1], chDbg):
            P.wait("gpsimd", chf, chf.n)
            P.wait("sync", chf, chf.n)

    with st, nc.Block() as block:
        @block.sync
        def _(e):
            for f in P.ops["sync"]:
                f(e)

        @block.tensor
        def _(e):
            for f in P.ops["tensor"]:
                f(e)

        @block.vector
        def _(e):
            for f in P.ops["vector"]:
                f(e)

        @block.scalar
        def _(e):
            for f in P.ops["scalar"]:
                f(e)

        @block.gpsimd
        def _(e):
            for f in P.ops["gpsimd"]:
                f(e)

    return nc


def kernel(**inputs):
    x, shared = _host_prepare(inputs)
    nc = build_nc()
    in_maps = [dict(shared, x=np.ascontiguousarray(x[b])) for b in range(8)]
    res = run_bass_kernel_spmd(nc, in_maps, core_ids=list(range(8)))
    return np.stack([np.asarray(r["out"], np.float32) for r in res.results])
